# version 26
# baseline (speedup 1.0000x reference)
import numpy as np
from contextlib import ExitStack
import concourse.bass as bass
import concourse.mybir as mybir
from concourse.bass_utils import run_bass_kernel_spmd

F32 = mybir.dt.float32
BF16 = mybir.dt.bfloat16
I32 = mybir.dt.int32
U32 = mybir.dt.uint32
AF = mybir.ActivationFunctionType
ALU = mybir.AluOpType
SL = np.s_

ENGINES = ("tensor", "vector", "scalar", "gpsimd", "sync")


def make_cfg(S=2048, nseq=2, cap=384, n_sub=4):
    return dict(S=S, nseq=nseq, D=1024, H=16, HD=64, E=8, F=4096,
                NDC=8, NTC=S // 128, NSC=S // 512, CAP=cap, n_sub=n_sub)


class Em:
    def __init__(self, nc, es):
        self.nc = nc
        self.thunks = {e: [] for e in ENGINES}
        self.count = {e: 0 for e in ENGINES}
        self.selfsem = {}
        self.sems = {}
        for e in ENGINES:
            name = "sq_" + e
            self.sems[name] = es.enter_context(nc.semaphore(name))
            self.selfsem[e] = name
        self.dmaq = {"qg": "gpsimd", "qs": "sync", "qa": "scalar"}
        self._es = es
        self.dmastream = {}
        self.dmacount = {}
        self.dmasems = set()
        self.waited = {e: {} for e in ENGINES}
        self.maxwait = {}
        self.last_writer = {}
        self.readers = {}
        self._pending = {}
        self.clock = {e: {} for e in ENGINES}
        self.snap = {}

    def _emit_wait(self, eng, sem, val):
        if self.maxwait.get(sem, 0) < val:
            self.maxwait[sem] = val
        ck = self.clock[eng]
        if ck.get(sem, 0) >= val:
            return
        w = self.waited[eng]
        if w.get(sem, 0) < val:
            w[sem] = val
            self.thunks[eng].append(lambda e, s=self.sems[sem], v=val: e.wait_ge(s, v))
        sn = self.snap.get((sem, val))
        if sn is not None:
            for s2, v2 in sn.items():
                if ck.get(s2, 0) < v2:
                    ck[s2] = v2
        if ck.get(sem, 0) < val:
            ck[sem] = val

    def _dep_waits(self, eng, reads, writes):
        best = {}

        def add(t):
            if t is None:
                return
            sem, val, peng = t
            if peng == "tensor" and eng == "tensor":
                return
            if sem in self.dmasems:
                val = 16 * self.dmacount[sem]
            if best.get(sem, 0) < val:
                best[sem] = val

        for k in reads:
            add(self.last_writer.get(k))
        for k in writes:
            add(self.last_writer.get(k))
            for t in self.readers.get(k, ()):
                add(t)
        for sem, val in best.items():
            self._emit_wait(eng, sem, val)

    def _record(self, tok, reads, writes):
        for k in writes:
            self.last_writer[k] = tok
            self.readers[k] = set()
        for k in reads:
            self.readers.setdefault(k, set()).add(tok)

    def op_multi(self, eng, fns, reads=(), writes=()):
        self._dep_waits(eng, reads, writes)
        self.count[eng] += 1
        sem = self.selfsem[eng]
        h = self.sems[sem]

        def thunk(e, fns=fns, h=h):
            for fn in fns[:-1]:
                fn(e)
            fns[-1](e).then_inc(h, 1)

        self.thunks[eng].append(thunk)
        tok = (sem, self.count[eng], eng)
        sn = dict(self.clock[eng])
        sn[sem] = self.count[eng]
        self.snap[(sem, self.count[eng])] = sn
        self._record(tok, reads, writes)
        return tok

    def op(self, eng, fn, reads=(), writes=(), light=False):
        self._dep_waits(eng, reads, writes)
        sem = self.selfsem[eng]
        if light:
            raise NotImplementedError
        self.count[eng] += 1
        h = self.sems[sem]

        def thunk(e, fn=fn, h=h):
            fn(e).then_inc(h, 1)

        self.thunks[eng].append(thunk)
        tok = (sem, self.count[eng], eng)
        sn = dict(self.clock[eng])
        sn[sem] = self.count[eng]
        self.snap[(sem, self.count[eng])] = sn
        self._record(tok, reads, writes)
        return tok

    def dma(self, q, fn, reads=(), writes=(), stream=None):
        eng = self.dmaq[q]
        sname = stream if stream is not None else q
        sem = self.dmastream.get(sname)
        if sem is None:
            sem = "dma_" + sname
            self.sems[sem] = self._es.enter_context(self.nc.semaphore(sem))
            self.dmastream[sname] = sem
            self.dmacount[sem] = 0
            self.dmasems.add(sem)
        mw = self.maxwait.get(sem, 0)
        if mw > 0:
            self._emit_wait(eng, sem, mw)
        self._dep_waits(eng, reads, writes)
        self.dmacount[sem] += 1
        h = self.sems[sem]

        def thunk(e, fn=fn, h=h):
            fn(e).then_inc(h, 16)

        self.thunks[eng].append(thunk)
        tok = (sem, 16 * self.dmacount[sem], eng)
        self.snap[(sem, 16 * self.dmacount[sem])] = dict(self.clock[eng])
        self._record(tok, reads, writes)
        return tok

    def barrier(self):
        targets = [(self.selfsem[e], self.count[e]) for e in ENGINES]
        targets += [(sem, 16 * cnt) for sem, cnt in self.dmacount.items()]
        for e in ENGINES:
            for sem, val in targets:
                if val > 0:
                    self._emit_wait(e, sem, val)


class T:
    __slots__ = ("h", "key")

    def __init__(self, h, key):
        self.h = h
        self.key = key


_uid = [0]


def build_program(cfg):
    S = cfg["S"]; NDC = cfg["NDC"]; NTC = cfg["NTC"]; NSC = cfg["NSC"]
    H = cfg["H"]; E = cfg["E"]; F = cfg["F"]; D = cfg["D"]
    CAP = cfg["CAP"]; NSEQ = cfg["nseq"]; NSUB = cfg["n_sub"]
    NFI = F // 128
    CAPT = CAP // 128

    nc = bass.Bass("TRN2", target_bir_lowering=False)
    root = ExitStack()
    em = Em(nc, root)

    def din(name, shape):
        return T(nc.dram_tensor(name, shape, F32, kind="ExternalInput"), "d_" + name)

    x_in = din("x", [NSEQ, S, D])
    aa_wq = din("aa_wq", [D, D]); aa_wk = din("aa_wk", [D, D])
    aa_wv = din("aa_wv", [D, D]); aa_wo = din("aa_wo", [D, D])
    aa_gam = din("aa_gam", [H])
    ab_ln1s = din("ab_ln1s", [D]); ab_ln1b = din("ab_ln1b", [D])
    ab_wq = din("ab_wq", [D, D]); ab_wk = din("ab_wk", [D, D]); ab_wv = din("ab_wv", [D, D])
    ab_bq = din("ab_bq", [D]); ab_bk = din("ab_bk", [D]); ab_bv = din("ab_bv", [D])
    ab_wo = din("ab_wo", [D, D]); ab_bo = din("ab_bo", [D])
    ab_ln2s = din("ab_ln2s", [D]); ab_ln2b = din("ab_ln2b", [D])
    ab_w1 = din("ab_w1", [D, F]); ab_b1 = din("ab_b1", [F])
    ab_w2 = din("ab_w2", [F, D]); ab_b2 = din("ab_b2", [D])
    f_wr = din("f_wr", [D, E]); f_br = din("f_br", [E])
    f_w1 = din("f_w1", [E, D, F]); f_b1 = din("f_b1", [E, F])
    f_w2 = din("f_w2", [E, F, D]); f_b2 = din("f_b2", [E, D])

    y_out = T(nc.dram_tensor("y", [NSEQ, S, D], F32, kind="ExternalOutput"), "d_y")

    xn_dram = T(nc.dram_tensor("xn_scr", [S + 128, D], BF16, kind="Internal"), "d_xnscr")
    yf_dram = T(nc.dram_tensor("yf_scr", [S + 128, D], F32, kind="Internal"), "d_yfscr")
    eid_dram = T(nc.dram_tensor("eid_scr", [NTC, 128], F32, kind="Internal"), "d_eidscr")
    idx_dram = T(nc.dram_tensor("idx_scr", [16384, 1], I32, kind="Internal"), "d_idxscr")
    wt_dram = T(nc.dram_tensor("wt_scr", [4, D, D], BF16, kind="Internal"), "d_wtscr")
    w1b_dram = T(nc.dram_tensor("w1b_scr", [E, D, F], BF16, kind="Internal"), "d_w1bscr")
    w2b_dram = T(nc.dram_tensor("w2b_scr", [E, F, D], BF16, kind="Internal"), "d_w2bscr")
    abw_dram = T(nc.dram_tensor("abw_scr", [4, D, D], BF16, kind="Internal"), "d_abwscr")
    abw1b = T(nc.dram_tensor("abw1b_scr", [D, F], BF16, kind="Internal"), "d_abw1b")
    b1b_dram = T(nc.dram_tensor("b1b_scr", [E, F], BF16, kind="Internal"), "d_b1b")
    abb1b_dram = T(nc.dram_tensor("abb1b_scr", [1, F], BF16, kind="Internal"), "d_abb1b")
    abw2b = T(nc.dram_tensor("abw2b_scr", [F, D], BF16, kind="Internal"), "d_abw2b")
    xsp_dram = T(nc.dram_tensor("xsp_scr", [NDC, 128, S], F32, kind="Internal"), "d_xspscr")

    arena = {"lo": None, "hi": None}

    def mk_sb(stack, name, shape, dt):
        _uid[0] += 1
        nm = f"{name}_{_uid[0]}"
        return T(stack.enter_context(nc.sbuf_tensor(nm, shape, dt)), nm)

    def _psize(shape, dt):
        n = 1
        for d in shape[1:]:
            n *= d
        sz = n * mybir.dt.size(dt)
        return (sz + 31) & ~31

    class Scope:
        def __init__(self, side="right"):
            self.side = side
            self.saved = None

        def _ensure(self):
            if self.saved is None:
                self.saved = arena[{"right": "hi", "left": "lo"}[self.side]]

        def sb(self, name, shape, dt):
            self._ensure()
            _uid[0] += 1
            nm = f"{name}_{_uid[0]}"
            sz = _psize(shape, dt)
            if self.side == "right":
                arena["hi"] -= sz
                off = arena["hi"]
            else:
                off = arena["lo"]
                arena["lo"] += sz
            assert arena["lo"] <= arena["hi"], \
                f"arena overflow at {nm}: lo={arena['lo']} hi={arena['hi']}"
            h = nc.alloc_sbuf_tensor_at(nm, shape, dt, offset=off)
            return T(h, nm)

        def close(self):
            em.barrier()
            self._ensure()
            if self.side == "right":
                arena["hi"] = self.saved
            else:
                arena["lo"] = self.saved

    def sbL(name, shape, dt):
        return mk_sb(root, name, shape, dt)

    P = [T(root.enter_context(nc.psum_tensor(f"P{i}", [128, 512], F32)), f"P{i}")
         for i in range(7)]
    PTB = T(root.enter_context(nc.psum_tensor("PTB", [128, 1024], BF16)), "PTB")

    ident_b = sbL("ident_b", [128, 128], BF16)
    ident_f = sbL("ident_f", [128, 128], F32)
    ones_row_f = sbL("ones_row_f", [1, 512], F32)
    ones_row_b = sbL("ones_row_b", [1, 512], BF16)
    ones_col_f = sbL("ones_col_f", [128, 1], F32)
    ones_col_b = sbL("ones_col_b", [128, 1], BF16)
    zrow_b = sbL("zrow_b", [1, 512], BF16)
    iota_tok = sbL("iota_tok", [128, NTC], I32)
    ecol = sbL("ecol", [E, 1], F32)
    brcol = sbL("brcol", [E, 1], F32)
    wrb = [sbL(f"wrb{j}", [128, E], F32) for j in range(NDC)]

    dmask = [sbL(f"dmask{h}", [128, 128], BF16) for h in range(H)]
    gqrow = [sbL(f"gqrow{h}", [128, 128], BF16) for h in range(H)]
    gkcol = sbL("gkcol", [128, H], F32)
    gcbc = sbL("gcbc", [64, H], F32)
    lngam = sbL("lngam", [1, H], F32)
    gamrow = sbL("gamrow", [1, H], F32)
    dtile = sbL("dtile", [128, 128], F32)
    riota_row = sbL("riota_row", [1, 128], F32)
    riota_col = sbL("riota_col", [128, 1], F32)
    lgcol = sbL("lgcol", [128, 1], F32)
    ln8col = sbL("ln8col", [128, 1], F32)
    eps5col = sbL("eps5col", [1, 1], F32)
    eps6col = sbL("eps6col", [1, 1], F32)
    gamsb = sbL("gamsb", [1, 16], F32)

    XN = [sbL(f"XN{c}", [128, 512], BF16) for c in range(NDC)]
    Sst = [sbL(f"Sst{h}", [64, 64], F32) for h in range(H)]
    Sbf = [sbL(f"Sbf{h}", [64, 64], BF16) for h in range(H)]
    sqt = [sbL(f"sqt{i}", [128, 512], F32) for i in range(2)]
    arow = sbL("arow", [1, 512], F32)
    brow = sbL("brow", [1, 512], F32)
    r512a = sbL("r512a", [1, 512], F32)
    r512b = sbL("r512b", [1, 512], F32)
    m2row = sbL("m2row", [1, 512], F32)
    rows2 = sbL("rows2", [2, 512], F32)
    lnps = sbL("lnps", [1, D], F32)
    lnpb = sbL("lnpb", [1, D], F32)
    lnsc = sbL("lnsc", [128, NDC], F32)
    lnsb = sbL("lnsb", [128, NDC], F32)
    xtmp = sbL("xtmp", [128, 512], F32)
    xtmp2 = sbL("xtmp2", [128, 512], F32)
    io_tm = sbL("io_tm", [128, D], F32)

    _persist_end = 0
    for _t in ([ident_b, ident_f, ones_row_f, ones_row_b, ones_col_f, ones_col_b,
                iota_tok, ecol, brcol, gkcol, gcbc, lngam, gamrow, dtile,
                riota_row, riota_col, lgcol, ln8col, eps5col, eps6col, gamsb,
                arow, brow, r512a, r512b, m2row, rows2, lnps, lnpb, lnsc, lnsb, xtmp, xtmp2, io_tm]
               + wrb + dmask + gqrow + XN + sqt):
        ml = nc.lookup_mloc(_t.h)
        _persist_end = max(_persist_end, ml.addr + _psize(list(_t.h.shape), _t.h.dtype))
    arena["lo"] = _persist_end
    arena["hi"] = 229344

    grpbuf = {"fns": [], "reads": set()}
    FUSE_GROUPS = False

    def mm(pt, prange, lhsT, lr, rhs, rr, reads, start, stop, fuse=False):
        fuse = fuse and FUSE_GROUPS
        fn = (lambda e, pt=pt, prange=prange, lhsT=lhsT, lr=lr, rhs=rhs, rr=rr,
              start=start, stop=stop: e.matmul(pt.h[prange], lhsT[lr], rhs[rr],
                                               start=start, stop=stop,
                                               skip_group_check=True))
        if fuse and not stop:
            grpbuf["fns"].append(fn)
            grpbuf["reads"].update(reads)
            return None
        if fuse:
            fns = grpbuf["fns"] + [fn]
            rds = tuple(grpbuf["reads"] | set(reads))
            grpbuf["fns"] = []
            grpbuf["reads"] = set()
            return em.op_multi("tensor", fns, reads=rds, writes=(pt.key,))
        return em.op("tensor", fn, reads=reads, writes=(pt.key,))

    def cp(eng, dst, dsl, src, ssl, reads, writes):
        if eng == "scalar":
            return em.op(eng, lambda e: e.copy(dst[dsl], src[ssl]), reads=reads, writes=writes)
        return em.op(eng, lambda e: e.tensor_copy(dst[dsl], src[ssl]), reads=reads, writes=writes)

    rot = [0]

    def transpose_pe(dstT, dsl, srcT, ssl, bf16=True, n=128):
        if bf16:
            pt = PTB
        else:
            pt = P[rot[0] % 2]
            rot[0] += 1
        idn = ident_b if bf16 else ident_f
        em.op("tensor", lambda e: e.transpose(pt.h[0:128, 0:n], srcT.h[ssl], idn.h[0:n, 0:n]),
              reads=(srcT.key, idn.key), writes=(pt.key,))
        cp("vector", dstT.h, dsl, pt.h, SL[0:128, 0:n], (pt.key,), (dstT.key,))

    def load_row_f32(dst, n, src, row=0):
        em.dma("qs", lambda e: e.dma_start(dst.h[row:row + 1, 0:n],
                                           src.h[:].rearrange("(a b) -> a b", a=1)),
               reads=(src.key,), writes=(dst.key,), stream="rows")

    def load_row_bf16(dst, n, src):
        em.dma("qg", lambda e: e.dma_start(dst.h[0:1, 0:n],
                                           src.h[:].rearrange("(a b) -> a b", a=1)),
               reads=(src.key,), writes=(dst.key,), stream="rows_b")

    em.op("gpsimd", lambda e: e.memset(ident_b.h[:, :], 1.0), writes=(ident_b.key,))
    em.op("gpsimd", lambda e: e.affine_select(ident_b.h[:, :], ident_b.h[:, :], [[-1, 128]],
                                              ALU.is_equal, 0.0, base=0, channel_multiplier=1),
          reads=(ident_b.key,), writes=(ident_b.key,))
    em.op("gpsimd", lambda e: e.memset(ident_f.h[:, :], 1.0), writes=(ident_f.key,))
    em.op("gpsimd", lambda e: e.affine_select(ident_f.h[:, :], ident_f.h[:, :], [[-1, 128]],
                                              ALU.is_equal, 0.0, base=0, channel_multiplier=1),
          reads=(ident_f.key,), writes=(ident_f.key,))
    em.op("gpsimd", lambda e: e.memset(ones_row_f.h[:, :], 1.0), writes=(ones_row_f.key,))
    em.op("gpsimd", lambda e: e.memset(ones_row_b.h[:, :], 1.0), writes=(ones_row_b.key,))
    em.op("gpsimd", lambda e: e.memset(ones_col_f.h[:, :], 1.0), writes=(ones_col_f.key,))
    em.op("gpsimd", lambda e: e.memset(ones_col_b.h[:, :], 1.0), writes=(ones_col_b.key,))
    em.op("gpsimd", lambda e: e.memset(zrow_b.h[:, :], 0.0), writes=(zrow_b.key,))
    em.op("gpsimd", lambda e: e.memset(rows2.h[0:2, :], 1.0), writes=(rows2.key,))
    em.op("gpsimd", lambda e: e.memset(ln8col.h[:, :], float(np.log(0.125))),
          writes=(ln8col.key,))
    em.op("gpsimd", lambda e: e.memset(eps5col.h[:, :], 1e-5), writes=(eps5col.key,))
    em.op("gpsimd", lambda e: e.memset(eps6col.h[:, :], 1e-6), writes=(eps6col.key,))
    em.op("gpsimd", lambda e: e.iota(iota_tok.h[:, :], [[128, NTC]], base=0,
                                     channel_multiplier=1), writes=(iota_tok.key,))
    em.op("gpsimd", lambda e: e.iota(ecol.h[:, :], [[0, 1]], base=0, channel_multiplier=1,
                                     allow_small_or_imprecise_dtypes=True), writes=(ecol.key,))
    em.op("gpsimd", lambda e: e.iota(dtile.h[:, :], [[1, 128]], base=0, channel_multiplier=-1,
                                     allow_small_or_imprecise_dtypes=True), writes=(dtile.key,))
    em.op("gpsimd", lambda e: e.iota(riota_row.h[:, :], [[1, 128]], base=1, channel_multiplier=0,
                                     allow_small_or_imprecise_dtypes=True),
          writes=(riota_row.key,))
    em.op("gpsimd", lambda e: e.iota(riota_col.h[:, :], [[0, 1]], base=127, channel_multiplier=-1,
                                     allow_small_or_imprecise_dtypes=True),
          writes=(riota_col.key,))

    em.dma("qs", lambda e: e.dma_start(gamsb.h[0:1, 0:16],
                                       aa_gam.h[:].rearrange("(a b) -> a b", a=1)),
           reads=(aa_gam.key,), writes=(gamsb.key,))
    em.dma("qs", lambda e: e.dma_start(brcol.h[0:E, 0:1],
                                       f_br.h[:].rearrange("(a b) -> a b", b=1)),
           reads=(f_br.key,), writes=(brcol.key,))
    for j in range(NDC):
        em.dma("qs", lambda e, j=j: e.dma_start(wrb[j].h[:, :], f_wr.h[j * 128:(j + 1) * 128, :]),
               reads=(f_wr.key,), writes=(wrb[j].key,), stream="rows")

    em.op("scalar", lambda e: e.activation(lngam.h[:, :], gamsb.h[0:1, 0:16], AF.Ln),
          reads=(gamsb.key,), writes=(lngam.key,))
    em.op("scalar", lambda e: e.activation(gamrow.h[:, :], lngam.h[:, :], AF.Exp, scale=128.0),
          reads=(lngam.key,), writes=(gamrow.key,))
    mm(P[5], SL[0:64, 0:16], ones_row_f.h, SL[0:1, 0:64], gamrow.h, SL[0:1, 0:16],
       (ones_row_f.key, gamrow.key), True, True)
    cp("vector", gcbc.h, SL[:, :], P[5].h, SL[0:64, 0:16], (P[5].key,), (gcbc.key,))
    LN8 = float(np.log(0.125))
    for h in range(H):
        mm(P[5], SL[0:128, 0:1], ones_row_f.h, SL[0:1, 0:128], lngam.h, SL[0:1, h:h + 1],
           (ones_row_f.key, lngam.key), True, True)
        cp("vector", lgcol.h, SL[:, :], P[5].h, SL[0:128, 0:1], (P[5].key,), (lgcol.key,))
        em.op("scalar", lambda e, h=h: e.activation(dmask[h].h[:, :], dtile.h[:, :], AF.Exp,
                                                    bias=ln8col.h[:, 0:1],
                                                    scale=lgcol.h[:, 0:1]),
              reads=(dtile.key, lgcol.key, ln8col.key), writes=(dmask[h].key,))
        em.op("gpsimd", lambda e, h=h: e.affine_select(dmask[h].h[:, :], dmask[h].h[:, :],
                                                       [[1, 128]], ALU.is_ge, 0.0, base=0,
                                                       channel_multiplier=-1),
              reads=(dmask[h].key,), writes=(dmask[h].key,))
        em.op("scalar", lambda e, h=h: e.activation(r512a.h[0:1, 0:128], riota_row.h[:, :],
                                                    AF.Exp, bias=ln8col.h[0:1, 0:1],
                                                    scale=lngam.h[0:1, h:h + 1]),
              reads=(riota_row.key, lngam.key, ln8col.key), writes=(r512a.key,))
        mm(P[5], SL[0:128, 0:128], ones_row_f.h, SL[0:1, 0:128], r512a.h, SL[0:1, 0:128],
           (ones_row_f.key, r512a.key), True, True)
        cp("vector", gqrow[h].h, SL[:, :], P[5].h, SL[0:128, 0:128], (P[5].key,),
           (gqrow[h].key,))
        em.op("scalar", lambda e, h=h: e.activation(gkcol.h[:, h:h + 1], riota_col.h[:, :],
                                                    AF.Exp, scale=lgcol.h[:, 0:1]),
              reads=(riota_col.key, lgcol.key), writes=(gkcol.key,))

    tsc = Scope("right")
    wtmp = [tsc.sb(f"wtmp{j}", [128, D], BF16) for j in range(NDC)]
    wTt = [tsc.sb(f"wTt{j}", [128, D], BF16) for j in range(NDC)]
    for wi, wsrc in enumerate((aa_wq, aa_wk, aa_wv, aa_wo)):
        for j in range(NDC):
            em.dma("qg", lambda e, j=j, wsrc=wsrc: e.dma_start(
                wtmp[j].h[:, :], wsrc.h[j * 128:(j + 1) * 128, :]),
                   reads=(wsrc.key,), writes=(wtmp[j].key,), stream=f"wt{j % 2}")
        for i in range(NDC):
            for j in range(NDC):
                transpose_pe(wTt[j], SL[:, i * 128:(i + 1) * 128],
                             wtmp[i], SL[:, j * 128:(j + 1) * 128])
        for j in range(NDC):
            em.dma("qs", lambda e, wi=wi, j=j: e.dma_start(
                wt_dram.h[wi, j * 128:(j + 1) * 128, :], wTt[j].h[:, :]),
                   reads=(wTt[j].key,), writes=(wt_dram.key,), stream=f"wtd{j % 2}")
    tsc.close()

    def issue_w1_conversions():
        for e_i in range(E):
            for r0 in range(0, D, 256):
                em.dma("qg", lambda e, e_i=e_i, r0=r0: e.dma_start(
                    w1b_dram.h[e_i, r0:r0 + 256, :], f_w1.h[e_i, r0:r0 + 256, :]),
                       reads=(f_w1.key,), writes=(w1b_dram.key,), stream="cvd")
        em.dma("qg", lambda e: e.dma_start(b1b_dram.h[:, :], f_b1.h[:, :]),
               reads=(f_b1.key,), writes=(b1b_dram.key,), stream="cvd")
        em.dma("qg", lambda e: e.dma_start(
            abb1b_dram.h[:, :], ab_b1.h[0:F].rearrange("(a b) -> a b", a=1)),
               reads=(ab_b1.key,), writes=(abb1b_dram.key,), stream="cvd")

    def gen_w2():
        for e_i in range(E):
            for r0 in range(0, F, 1024):
                em.dma("qg", lambda e, e_i=e_i, r0=r0: e.dma_start(
                    w2b_dram.h[e_i, r0:r0 + 1024, :], f_w2.h[e_i, r0:r0 + 1024, :]),
                       reads=(f_w2.key,), writes=(w2b_dram.key,), stream="cvd")
                yield

    def gen_ab():
        for wi, wsrc in enumerate((ab_wq, ab_wk, ab_wv, ab_wo)):
            for r0 in range(0, D, 512):
                em.dma("qg", lambda e, wi=wi, r0=r0, wsrc=wsrc: e.dma_start(
                    abw_dram.h[wi, r0:r0 + 512, :], wsrc.h[r0:r0 + 512, :]),
                       reads=(wsrc.key,), writes=(abw_dram.key,), stream="cvd")
                yield
        for r0 in range(0, D, 256):
            em.dma("qg", lambda e, r0=r0: e.dma_start(
                abw1b.h[r0:r0 + 256, :], ab_w1.h[r0:r0 + 256, :]),
                   reads=(ab_w1.key,), writes=(abw1b.key,), stream="cvd")
            yield
        for r0 in range(0, F, 1024):
            em.dma("qg", lambda e, r0=r0: e.dma_start(
                abw2b.h[r0:r0 + 1024, :], ab_w2.h[r0:r0 + 1024, :]),
                   reads=(ab_w2.key,), writes=(abw2b.key,), stream="cvd")
            yield

    def layernorm(src_list, sc, eps, dst_list=None, scaled=False, stats_banks=None):
        pa, pb = stats_banks if stats_banks is not None else (P[3], P[4])
        t0 = sc * 512
        csl = SL[:, t0:t0 + 512]
        for c in range(NDC):
            sq = sqt[c % 2]
            em.op("scalar", lambda e, c=c, sq=sq, csl=csl: e.activation(
                sq.h[:, :], src_list[c].h[csl], AF.Square),
                  reads=(src_list[c].key,), writes=(sq.key,))
            mm(pa, SL[0:1, 0:512], ones_col_f.h, SL[0:128, 0:1], src_list[c].h, csl,
               (ones_col_f.key, src_list[c].key), c == 0, c == NDC - 1)
            mm(pb, SL[0:1, 0:512], ones_col_f.h, SL[0:128, 0:1], sq.h, SL[:, :],
               (ones_col_f.key, sq.key), c == 0, c == NDC - 1)
        em.op("vector", lambda e, pa=pa: e.tensor_scalar_mul(
            r512a.h[:, :], pa.h[0:1, 0:512], 1.0 / D),
              reads=(pa.key,), writes=(r512a.key,))
        em.op("vector", lambda e, pb=pb: e.tensor_scalar_mul(
            r512b.h[:, :], pb.h[0:1, 0:512], 1.0 / D),
              reads=(pb.key,), writes=(r512b.key,))
        em.op("scalar", lambda e: e.activation(m2row.h[:, :], r512a.h[:, :], AF.Square),
              reads=(r512a.key,), writes=(m2row.key,))
        em.op("vector", lambda e: e.tensor_sub(r512b.h[:, :], r512b.h[:, :], m2row.h[:, :]),
              reads=(r512b.key, m2row.key), writes=(r512b.key,))
        ecl = eps5col if eps == 1e-5 else eps6col
        em.op("scalar", lambda e, ecl=ecl: e.activation(r512b.h[:, :], r512b.h[:, :], AF.Sqrt,
                                                        bias=ecl.h[0:1, 0:1]),
              reads=(r512b.key, ecl.key), writes=(r512b.key,))
        em.op("vector", lambda e: e.reciprocal(arow.h[:, :], r512b.h[:, :]),
              reads=(r512b.key,), writes=(arow.key,))
        em.op("vector", lambda e: e.tensor_scalar_mul(r512a.h[:, :], r512a.h[:, :], -1.0),
              reads=(r512a.key,), writes=(r512a.key,))
        em.op("vector", lambda e: e.tensor_mul(brow.h[:, :], r512a.h[:, :], arow.h[:, :]),
              reads=(r512a.key, arow.key), writes=(brow.key,))
        cp("vector", rows2.h, SL[0:1, :], brow.h, SL[0:1, :], (brow.key,), (rows2.key,))
        if dst_list is None:
            return
        mm(P[5], SL[0:128, 0:512], ones_row_f.h, SL[0:1, 0:128],
           arow.h, SL[0:1, :], (ones_row_f.key, arow.key), True, True)
        mm(P[6], SL[0:128, 0:512], ones_row_f.h, SL[0:1, 0:128],
           rows2.h, SL[0:1, 0:512], (ones_row_f.key, rows2.key), True, True)
        for c in range(NDC):
            xt_ = xtmp if c % 2 == 0 else xtmp2
            em.op("vector", lambda e, c=c, csl=csl, xt_=xt_: e.tensor_mul(
                xt_.h[:, :], src_list[c].h[csl], P[5].h[0:128, 0:512]),
                  reads=(src_list[c].key, P[5].key), writes=(xt_.key,))
            if scaled:
                em.op("vector", lambda e, c=c, xt_=xt_: e.tensor_add(
                    xt_.h[:, :], xt_.h[:, :], P[6].h[0:128, 0:512]),
                      reads=(xt_.key, P[6].key), writes=(xt_.key,))
                em.op("vector", lambda e, c=c, xt_=xt_: e.tensor_scalar(
                    dst_list[c].h[:, 0:512], xt_.h[:, :],
                    lnsc.h[:, c:c + 1], lnsb.h[:, c:c + 1], ALU.mult, ALU.add),
                      reads=(xt_.key, lnsc.key, lnsb.key), writes=(dst_list[c].key,))
            else:
                em.op("vector", lambda e, c=c, xt_=xt_: e.tensor_add(
                    dst_list[c].h[:, 0:512], xt_.h[:, :], P[6].h[0:128, 0:512]),
                      reads=(xt_.key, P[6].key), writes=(dst_list[c].key,))

    def attn_a(X, pump=None):
        a = Scope("right")
        wsA = [a.sb(f"wsA{j}", [128, D], BF16) for j in range(NDC)]
        wsB = [a.sb(f"wsB{j}", [128, D], BF16) for j in range(NDC)]
        qfm = [a.sb(f"qfm{c}", [128, 512], BF16) for c in range(NDC)]
        kfm = [a.sb(f"kfm{c}", [128, 512], BF16) for c in range(NDC)]
        ktm = [a.sb(f"ktm{t}", [128, D], BF16) for t in range(4)]
        vtm = [a.sb(f"vtm{t}", [128, D], BF16) for t in range(4)]
        retf = [a.sb(f"retf{c}", [128, 512], BF16) for c in range(NDC)]
        wintra = [a.sb(f"wintra{i}", [128, 128], BF16) for i in range(2)]
        qhat = [a.sb(f"qhat{i}", [64, 128], BF16) for i in range(2)]
        khat = [a.sb(f"khat{i}", [128, 64], BF16) for i in range(2)]


        def loadw(slot, wi, sname):
            for j in range(NDC):
                em.dma("qs", lambda e, j=j, wi=wi, slot=slot: e.dma_start(
                    slot[j].h[:, :], wt_dram.h[wi, j * 128:(j + 1) * 128, :]),
                       reads=(wt_dram.key,), writes=(slot[j].key,),
                       stream=sname + str(j % 2))

        def pp():
            if pump is not None:
                pump()

        for sc in range(NSC):
            t0 = sc * 512
            layernorm(X, sc, 1e-5, dst_list=XN)
            pp()
            loadw(wsA, 0, "wsa")
            loadw(wsB, 1, "wsb")
            for (ws, dstf) in ((wsA, qfm), (wsB, kfm)):
                for i in range(NDC):
                    pb = P[i % 4]
                    for j in range(NDC):
                        mm(pb, SL[0:128, 0:512], ws[j].h, SL[:, i * 128:(i + 1) * 128],
                           XN[j].h, SL[:, 0:512], (ws[j].key, XN[j].key),
                           j == 0, j == NDC - 1, fuse=True)
                    cp("vector", dstf[i].h, SL[:, :], pb.h, SL[0:128, 0:512],
                       (pb.key,), (dstf[i].key,))
                    pp()
            loadw(wsA, 2, "wsa")
            for (ws, dstt) in ((wsB, ktm), (wsA, vtm)):
                for tt in range(4):
                    for nh in range(2):
                        pb = P[(tt * 2 + nh) % 4]
                        for j in range(NDC):
                            mm(pb, SL[0:128, 0:512],
                               XN[j].h, SL[:, tt * 128:(tt + 1) * 128],
                               ws[j].h, SL[:, nh * 512:(nh + 1) * 512],
                               (XN[j].key, ws[j].key), j == 0, j == NDC - 1, fuse=True)
                        cp("vector", dstt[tt].h, SL[:, nh * 512:(nh + 1) * 512],
                           pb.h, SL[0:128, 0:512], (pb.key,), (dstt[tt].key,))
                        pp()
            loadw(wsB, 3, "wsb")
            for tt in range(4):
                c = sc * 4 + tt
                for h in range(H):
                    hc, ho = h // 2, (h % 2) * 64
                    par = h % 2
                    wi_t, qh_t, kh_t = wintra[par], qhat[par], khat[par]
                    sbk = P[[2, 5, 0][h % 3]]
                    mm(sbk, SL[0:128, 0:128], kfm[hc].h, SL[ho:ho + 64, tt * 128:(tt + 1) * 128],
                       qfm[hc].h, SL[ho:ho + 64, tt * 128:(tt + 1) * 128],
                       (kfm[hc].key, qfm[hc].key), True, True)
                    em.op("vector", lambda e, sbk=sbk, h=h, wi_t=wi_t: e.tensor_mul(
                        wi_t.h[:, :], sbk.h[0:128, 0:128], dmask[h].h[:, :]),
                          reads=(sbk.key, dmask[h].key), writes=(wi_t.key,))
                    rbk = P[[3, 6, 1][h % 3]]
                    mm(rbk, SL[0:64, 0:128], vtm[tt].h, SL[:, h * 64:(h + 1) * 64],
                       wi_t.h, SL[:, :], (vtm[tt].key, wi_t.key), True, c == 0, fuse=True)
                    if c > 0:
                        em.op("vector", lambda e, h=h, hc=hc, ho=ho, tt=tt, qh_t=qh_t: e.tensor_mul(
                            qh_t.h[:, :], qfm[hc].h[ho:ho + 64, tt * 128:(tt + 1) * 128],
                            gqrow[h].h[ho:ho + 64, :]),
                              reads=(qfm[hc].key, gqrow[h].key), writes=(qh_t.key,))
                        mm(rbk, SL[0:64, 0:128], Sbf[h].h, SL[:, :], qh_t.h, SL[:, :],
                           (Sbf[h].key, qh_t.key), False, True, fuse=True)
                    cp("scalar", retf[hc].h, SL[ho:ho + 64, tt * 128:(tt + 1) * 128],
                       rbk.h, SL[0:64, 0:128], (rbk.key,), (retf[hc].key,))
                    em.op("scalar", lambda e, h=h, tt=tt, kh_t=kh_t: e.activation(
                        kh_t.h[:, :], ktm[tt].h[:, h * 64:(h + 1) * 64],
                        AF.Copy, scale=gkcol.h[:, h:h + 1]),
                          reads=(ktm[tt].key, gkcol.key), writes=(kh_t.key,))
                    ubk = P[4]
                    mm(ubk, SL[0:64, 0:64], kh_t.h, SL[:, :], vtm[tt].h,
                       SL[:, h * 64:(h + 1) * 64], (kh_t.key, vtm[tt].key), True, True)
                    if c == 0:
                        cp("vector", Sst[h].h, SL[:, :], ubk.h, SL[0:64, 0:64],
                           (ubk.key,), (Sst[h].key,))
                    else:
                        em.op("vector", lambda e, h=h, ubk=ubk: e.scalar_tensor_tensor(
                            Sst[h].h[:, :], Sst[h].h[:, :], gcbc.h[:, h:h + 1],
                            ubk.h[0:64, 0:64], ALU.mult, ALU.add),
                              reads=(Sst[h].key, gcbc.key, ubk.key), writes=(Sst[h].key,))
                    if c < NTC - 1:
                        cp("scalar", Sbf[h].h, SL[:, :], Sst[h].h, SL[:, :],
                           (Sst[h].key,), (Sbf[h].key,))
            for dc in range(NDC):
                pb = P[dc % 4]
                for j in range(NDC):
                    mm(pb, SL[0:128, 0:512], wsB[j].h, SL[:, dc * 128:(dc + 1) * 128],
                       retf[j].h, SL[:, :], (wsB[j].key, retf[j].key), j == 0, j == NDC - 1,
                       fuse=True)
                em.op("vector", lambda e, dc=dc, pb=pb, t0=t0, X=X: e.tensor_add(
                    X[dc].h[:, t0:t0 + 512], X[dc].h[:, t0:t0 + 512], pb.h[0:128, 0:512]),
                      reads=(X[dc].key, pb.key), writes=(X[dc].key,))
                pp()
            for _ in range(8):
                pp()
        a.close()

    def ffn(X, pump=None, final=False):
        r = Scope("right")
        sc_s1 = r.sb("sc_s1", [E, S], F32)
        mask8 = r.sb("mask8", [E, S], F32)
        rank8 = r.sb("rank8", [E, S], F32)
        zero8 = r.sb("zero8", [E, S], F32)
        tsc8 = r.sb("tsc8", [128, 8], F32)
        tmax = r.sb("tmax", [128, 8], F32)
        tidx = r.sb("tidx", [128, 8], U32)
        eidc = r.sb("eidc", [128, NTC], F32)
        eidrow = r.sb("eidrow", [1, S], F32)
        wsum_c = r.sb("wsum_c", [E, 1], F32)
        rsel = r.sb("rsel", [128, 1], F32)
        slotf = r.sb("slotf", [128, 1], F32)
        sloti = r.sb("sloti", [128, 1], I32)
        selc = r.sb("selc", [128, 1], F32)
        senti = r.sb("senti", [128, 32], I32)
        eid16 = r.sb("eid16", [16, 128], F32)
        stage = [r.sb(f"stage{i}", [128, D], BF16) for i in range(2)]

        em.op("gpsimd", lambda e: e.memset(zero8.h[:, :], 0.0), writes=(zero8.key,))
        em.op("gpsimd", lambda e: e.memset(stage[0].h[:, :], 0.0), writes=(stage[0].key,))
        em.dma("qs", lambda e: e.dma_start(xn_dram.h[S:S + 128, :], stage[0].h[:, :]),
               reads=(stage[0].key,), writes=(xn_dram.key,), stream="xnw0")
        em.op("gpsimd", lambda e: e.memset(senti.h[:, :], S), writes=(senti.key,))
        for sc in range(NSC):
            t0 = sc * 512
            layernorm(X, sc, 1e-5, dst_list=XN)
            for tt in range(4):
                tc = sc * 4 + tt
                stg = stage[tt % 2]
                for dc in range(NDC):
                    em.op("tensor", lambda e, dc=dc, tt=tt: e.transpose(
                        PTB.h[0:128, dc * 128:(dc + 1) * 128],
                        XN[dc].h[:, tt * 128:(tt + 1) * 128], ident_b.h[:, :]),
                          reads=(XN[dc].key, ident_b.key), writes=(PTB.key,))
                cp("vector", stg.h, SL[:, :], PTB.h, SL[0:128, 0:1024],
                   (PTB.key,), (stg.key,))
                em.dma("qs", lambda e, tc=tc, stg=stg: e.dma_start(
                    xn_dram.h[tc * 128:(tc + 1) * 128, :], stg.h[:, :]),
                       reads=(stg.key,), writes=(xn_dram.key,),
                       stream=f"xnw{tt % 2}")
            for j in range(NDC):
                mm(P[2], SL[0:E, 0:512], wrb[j].h, SL[:, :], X[j].h, SL[:, t0:t0 + 512],
                   (wrb[j].key, X[j].key), j == 0, j == NDC - 1, fuse=True)
            cp("vector", sc_s1.h, SL[0:E, t0:t0 + 512], P[2].h, SL[0:E, 0:512],
               (P[2].key,), (sc_s1.key,))
            if sc == 0:
                for j in range(NDC):
                    mm(P[2], SL[0:E, 0:1], wrb[j].h, SL[:, :], ones_col_f.h, SL[:, 0:1],
                       (wrb[j].key, ones_col_f.key), j == 0, j == NDC - 1)
                cp("vector", wsum_c.h, SL[:, :], P[2].h, SL[0:E, 0:1],
                   (P[2].key,), (wsum_c.key,))
            mm(P[5], SL[0:E, 0:512], ones_row_f.h, SL[0:1, 0:E], arow.h, SL[0:1, :],
               (ones_row_f.key, arow.key), True, True)
            mm(P[6], SL[0:E, 0:512], ones_row_f.h, SL[0:1, 0:E], brow.h, SL[0:1, :],
               (ones_row_f.key, brow.key), True, True)
            em.op("vector", lambda e, t0=t0: e.tensor_mul(
                sc_s1.h[0:E, t0:t0 + 512], sc_s1.h[0:E, t0:t0 + 512], P[5].h[0:E, 0:512]),
                  reads=(sc_s1.key, P[5].key), writes=(sc_s1.key,))
            em.op("vector", lambda e: e.tensor_scalar(
                xtmp.h[0:E, 0:512], P[6].h[0:E, 0:512], wsum_c.h[:, 0:1],
                brcol.h[:, 0:1], ALU.mult, ALU.add),
                  reads=(P[6].key, wsum_c.key, brcol.key), writes=(xtmp.key,))
            em.op("vector", lambda e, t0=t0: e.tensor_add(
                sc_s1.h[0:E, t0:t0 + 512], sc_s1.h[0:E, t0:t0 + 512], xtmp.h[0:E, 0:512]),
                  reads=(sc_s1.key, xtmp.key), writes=(sc_s1.key,))
        for tc in range(NTC):
            transpose_pe(tsc8, SL[0:128, 0:8], sc_s1, SL[0:E, tc * 128:(tc + 1) * 128],
                         bf16=False, n=8)
            em.op("vector", lambda e: e.max(tmax.h[:, :], tsc8.h[:, 0:8]),
                  reads=(tsc8.key,), writes=(tmax.key,))
            em.op("vector", lambda e: e.max_index(tidx.h[:, :], tmax.h[:, :], tsc8.h[:, 0:8]),
                  reads=(tmax.key, tsc8.key), writes=(tidx.key,))
            cp("vector", eidc.h, SL[:, tc:tc + 1], tidx.h, SL[:, 0:1], (tidx.key,), (eidc.key,))
        em.op("tensor", lambda e: e.transpose(P[2].h[0:NTC, 0:128], eidc.h[:, :],
                                              ident_f.h[0:128, 0:128]),
              reads=(eidc.key, ident_f.key), writes=(P[2].key,))
        cp("vector", eid16.h, SL[0:NTC, :], P[2].h, SL[0:NTC, 0:128], (P[2].key,), (eid16.key,))
        em.dma("qs", lambda e: e.dma_start(eid_dram.h[:, :], eid16.h[0:NTC, :]),
               reads=(eid16.key,), writes=(eid_dram.key,))
        em.dma("qs", lambda e: e.dma_start(
            eidrow.h[0:1, 0:S],
            eid_dram.h[:, :].rearrange("a b -> (a b)").rearrange("(a b) -> a b", a=1)),
               reads=(eid_dram.key,), writes=(eidrow.key,))
        for sc in range(NSC):
            bank = P[5] if sc % 2 == 0 else P[6]
            mm(bank, SL[0:E, 0:512], ones_row_f.h, SL[0:1, 0:E],
               eidrow.h, SL[0:1, sc * 512:(sc + 1) * 512], (ones_row_f.key, eidrow.key),
               True, True)
            em.op("vector", lambda e, sc=sc, bank=bank: e.tensor_scalar(
                mask8.h[0:E, sc * 512:(sc + 1) * 512], bank.h[0:E, 0:512],
                ecol.h[:, 0:1], None, ALU.is_equal),
                  reads=(bank.key, ecol.key), writes=(mask8.key,))
        em.op("vector", lambda e: e.tensor_tensor_scan(
            rank8.h[0:E, 0:S], mask8.h[0:E, 0:S], zero8.h[0:E, 0:S], 0.0, ALU.add, ALU.add),
              reads=(mask8.key, zero8.key), writes=(rank8.key,))
        nfull = (E * CAP) // 128
        em.dma("qs", lambda e, nfull=nfull: e.dma_start(
            idx_dram.h[0:E * CAP, :].rearrange("(p a) b -> p (a b)", p=128),
            senti.h[:, 0:nfull]),
               reads=(senti.key,), writes=(idx_dram.key,))
        em.op("vector", lambda e: e.tensor_mul(rank8.h[0:E, 0:S], rank8.h[0:E, 0:S],
                                               mask8.h[0:E, 0:S]),
              reads=(rank8.key, mask8.key), writes=(rank8.key,))
        for tc in range(NTC):
            mm(P[5], SL[0:128, 0:1], rank8.h, SL[0:E, tc * 128:(tc + 1) * 128],
               ones_col_f.h, SL[0:E, 0:1], (rank8.key, ones_col_f.key), True, True)
            cp("vector", rsel.h, SL[:, :], P[5].h, SL[0:128, 0:1], (P[5].key,), (rsel.key,))
            em.op("vector", lambda e, tc=tc: e.tensor_scalar(
                slotf.h[:, :], eidc.h[:, tc:tc + 1], float(CAP), -1.0, ALU.mult, ALU.add),
                  reads=(eidc.key,), writes=(slotf.key,))
            em.op("vector", lambda e: e.tensor_add(slotf.h[:, :], slotf.h[:, :], rsel.h[:, :]),
                  reads=(slotf.key, rsel.key), writes=(slotf.key,))
            em.op("vector", lambda e: e.tensor_scalar(
                selc.h[:, :], rsel.h[:, :], float(CAP), None, ALU.is_gt),
                  reads=(rsel.key,), writes=(selc.key,))
            em.op("vector", lambda e: e.scalar_tensor_tensor(
                slotf.h[:, :], selc.h[:, :], 8192.0, slotf.h[:, :], ALU.mult, ALU.add),
                  reads=(selc.key, slotf.key), writes=(slotf.key,))
            cp("vector", sloti.h, SL[:, :], slotf.h, SL[:, :], (slotf.key,), (sloti.key,))
            em.dma("qg", lambda e, tc=tc: e.indirect_dma_start(
                idx_dram.h[:, :], bass.IndirectOffsetOnAxis(sloti.h[:, :], 0),
                iota_tok.h[:, tc:tc + 1], None),
                   reads=(sloti.key, iota_tok.key), writes=(idx_dram.key,))
        r.close()

        xp = Scope("right")
        xg_tm = [xp.sb(f"xg_tm{i}", [128, D], BF16) for i in range(2)]
        idxc2 = [xp.sb(f"idxc2_{i}", [128, 1], I32) for i in range(2)]
        xg_fm = [xp.sb(f"xg_fm{c}", [128, CAP], BF16) for c in range(NDC)]
        hsb = [xp.sb(f"hsb{i}", [128, CAP], BF16) for i in range(NFI)]
        W1Q = [[xp.sb(f"W1Q{p}_{j}", [128, 512], BF16) for j in range(NDC)]
               for p in range(3)]
        w2pr = [xp.sb(f"w2pr{i}", [128, 2048], BF16) for i in range(2)]
        b1fmb = xp.sb("b1fmb", [128, 32], BF16)
        b1fm = xp.sb("b1fm", [128, 32], F32)
        b2row = xp.sb("b2row", [1, D], BF16)
        ysc = xp.sb("ysc", [128, D], F32)
        ysc2 = [ysc, ysc]
        idxcol = xp.sb("idxcol", [128, 1], I32)
        for c in range(NDC):
            em.op("gpsimd", lambda e, c=c: e.memset(xg_fm[c].h[:, :], 0.0),
                  writes=(xg_fm[c].key,))
        em.op("gpsimd", lambda e: e.memset(ysc.h[:, :], 0.0), writes=(ysc.key,))
        for tc in range(NTC):
            em.dma("qs", lambda e, tc=tc: e.dma_start(
                yf_dram.h[tc * 128:(tc + 1) * 128, :], ysc.h[:, :]),
                   reads=(ysc.key,), writes=(yf_dram.key,), stream="ypre")

        def load_w1e(e_i, e8):
            buf = W1Q[e8 % 3]
            for j in range(NDC):
                em.dma("qa" if j % 2 == 0 else "qs",
                       lambda e, e_i=e_i, e8=e8, j=j, buf=buf: e.dma_start(
                           buf[j].h[:, :],
                           w1b_dram.h[e_i, j * 128:(j + 1) * 128,
                                      e8 * 512:(e8 + 1) * 512]),
                       reads=(w1b_dram.key,), writes=(buf[j].key,),
                       stream=f"w1_{j % 4}")

        def load_w2pair(e_i, fp):
            wb = w2pr[fp % 2]
            em.dma("qa" if fp % 2 == 0 else "qs",
                   lambda e, e_i=e_i, fp=fp, wb=wb: e.dma_start(
                       wb.h[:, :].rearrange("p (a d) -> p a d", a=2),
                       w2b_dram.h[e_i, 2 * fp * 128:(2 * fp + 2) * 128, :].rearrange(
                           "(a p) d -> p a d", p=128)),
                   reads=(w2b_dram.key,), writes=(wb.key,), stream=f"w2_{fp % 2}")
            return wb

        load_w1e(0, 0)
        load_w1e(0, 1)
        def gather_tile(e_i, k):
            par = (e_i * CAPT + k) % 2
            gt, ic = xg_tm[par], idxc2[par]
            em.dma("qs", lambda e, e_i=e_i, k=k, ic=ic: e.dma_start(
                ic.h[:, :],
                idx_dram.h[e_i * CAP + k * 128:e_i * CAP + (k + 1) * 128, :]),
                   reads=(idx_dram.key,), writes=(ic.key,), stream=f"idx{par}")
            em.dma("qg", lambda e, gt=gt, ic=ic: e.indirect_dma_start(
                gt.h[:, :], None, xn_dram.h[:, :],
                bass.IndirectOffsetOnAxis(ic.h[:, :], 0)),
                   reads=(xn_dram.key, ic.key), writes=(gt.key,),
                   stream=f"xg{par}")

        def place_tile(e_i, k):
            par = (e_i * CAPT + k) % 2
            gt = xg_tm[par]
            for dc in range(NDC):
                em.op("tensor", lambda e, dc=dc, gt=gt: e.transpose(
                    PTB.h[0:128, dc * 128:(dc + 1) * 128],
                    gt.h[:, dc * 128:(dc + 1) * 128], ident_b.h[:, :]),
                      reads=(gt.key, ident_b.key), writes=(PTB.key,))
            for dc in range(NDC):
                cp("vector",
                   xg_fm[dc].h, SL[:, k * 128:(k + 1) * 128],
                   PTB.h, SL[0:128, dc * 128:(dc + 1) * 128],
                   (PTB.key,), (xg_fm[dc].key,))

        gather_tile(0, 0)
        for e_i in range(E):
            for k in range(CAPT):
                if k + 1 < CAPT:
                    gather_tile(e_i, k + 1)
                elif e_i + 1 < E:
                    gather_tile(e_i + 1, 0)
                place_tile(e_i, k)
            em.dma("qg", lambda e, e_i=e_i: e.dma_start(b2row.h[0:1, :],
                                                        f_b2.h[e_i:e_i + 1, :]),
                   reads=(f_b2.key,), writes=(b2row.key,), stream="brow")
            em.dma("qs", lambda e, e_i=e_i: e.dma_start(
                b1fmb.h[:, :],
                b1b_dram.h[e_i].rearrange("(a b) -> a b", a=32), transpose=True),
                   reads=(b1b_dram.key,), writes=(b1fmb.key,), stream="b1t")
            cp("vector", b1fm.h, SL[:, :], b1fmb.h, SL[:, :], (b1fmb.key,), (b1fm.key,))
            for e8 in range(8):
                buf = W1Q[e8 % 3]
                for fj in range(4):
                    fi = e8 * 4 + fj
                    pb = P[fi % 7]
                    for j in range(NDC):
                        mm(pb, SL[0:128, 0:CAP], buf[j].h, SL[:, fj * 128:(fj + 1) * 128],
                           xg_fm[j].h, SL[:, 0:CAP], (buf[j].key, xg_fm[j].key),
                           j == 0, j == NDC - 1, fuse=True)
                    em.op("scalar", lambda e, fi=fi, pb=pb: e.activation(
                        hsb[fi].h[:, 0:CAP], pb.h[0:128, 0:CAP], AF.Gelu,
                        bias=b1fm.h[:, fi:fi + 1]),
                          reads=(pb.key, b1fm.key), writes=(hsb[fi].key,))
                nxt = e8 + 2
                if nxt < 8:
                    load_w1e(e_i, nxt)
                elif e_i < E - 1:
                    load_w1e(e_i + 1, nxt - 8)
            if pump is not None:
                pump(2)
            load_w2pair(e_i, 0)
            for fi in range(NFI):
                fp = fi // 2
                if fi % 2 == 0:
                    if fp + 1 < NFI // 2:
                        load_w2pair(e_i, fp + 1)
                wb = w2pr[fp % 2]
                co = (fi % 2) * 1024
                for k in range(CAPT):
                    pb, pb2 = P[1 + 2 * k], P[2 + 2 * k]
                    mm(pb, SL[0:128, 0:512], hsb[fi].h, SL[:, k * 128:(k + 1) * 128],
                       wb.h, SL[:, co:co + 512], (hsb[fi].key, wb.key),
                       fi == 0, False)
                    mm(pb2, SL[0:128, 0:512], hsb[fi].h, SL[:, k * 128:(k + 1) * 128],
                       wb.h, SL[:, co + 512:co + 1024], (hsb[fi].key, wb.key),
                       fi == 0, False)
            for k in range(CAPT):
                ybuf = ysc2[k % 2]
                pb, pb2 = P[1 + 2 * k], P[2 + 2 * k]
                mm(pb, SL[0:128, 0:512], ones_row_b.h, SL[0:1, 0:128],
                   b2row.h, SL[0:1, 0:512], (ones_row_b.key, b2row.key), False, True)
                mm(pb2, SL[0:128, 0:512], ones_row_b.h, SL[0:1, 0:128],
                   b2row.h, SL[0:1, 512:1024], (ones_row_b.key, b2row.key), False, True)
                cp("vector", ybuf.h, SL[:, 0:512], pb.h, SL[0:128, 0:512],
                   (pb.key,), (ybuf.key,))
                cp("vector", ybuf.h, SL[:, 512:1024], pb2.h, SL[0:128, 0:512],
                   (pb2.key,), (ybuf.key,))
                em.dma("qs", lambda e, e_i=e_i, k=k: e.dma_start(
                    idxcol.h[:, :],
                    idx_dram.h[e_i * CAP + k * 128:e_i * CAP + (k + 1) * 128, :]),
                       reads=(idx_dram.key,), writes=(idxcol.key,), stream="idx")
                em.dma("qg", lambda e, k=k, ybuf=ybuf: e.indirect_dma_start(
                    yf_dram.h[:, :], bass.IndirectOffsetOnAxis(idxcol.h[:, :], 0),
                    ybuf.h[:, :], None),
                       reads=(ybuf.key, idxcol.key), writes=(yf_dram.key,),
                       stream="ysct")
        if not final:
            for tc in range(NTC):
                em.dma("qs", lambda e, tc=tc: e.dma_start(
                    ysc.h[:, :], yf_dram.h[tc * 128:(tc + 1) * 128, :]),
                       reads=(yf_dram.key,), writes=(ysc.key,), stream="yrd")
                for dc in range(NDC):
                    pb = P[dc % 7]
                    em.op("tensor", lambda e, dc=dc, pb=pb: e.transpose(
                        pb.h[0:128, 0:128], ysc.h[:, dc * 128:(dc + 1) * 128],
                        ident_f.h[:, :]),
                          reads=(ysc.key, ident_f.key), writes=(pb.key,))
                    em.op("vector", lambda e, dc=dc, tc=tc, pb=pb, X=X: e.tensor_add(
                        X[dc].h[:, tc * 128:(tc + 1) * 128],
                        X[dc].h[:, tc * 128:(tc + 1) * 128], pb.h[0:128, 0:128]),
                          reads=(X[dc].key, pb.key), writes=(X[dc].key,))
        xp.close()

    def attn_b_partA(X, R, x_scope):
        for sc in range(NSC):
            t0 = sc * 512
            layernorm(X, sc, 1e-5, dst_list=None)
            for c in range(NDC):
                mm(P[5], SL[0:128, 0:512], ones_row_f.h, SL[0:1, 0:128],
                   arow.h, SL[0:1, :], (ones_row_f.key, arow.key), True, True)
                mm(P[6], SL[0:128, 0:512], ones_row_f.h, SL[0:1, 0:128],
                   rows2.h, SL[0:1, 0:512], (ones_row_f.key, rows2.key), True, True)
                em.op("vector", lambda e, c=c, t0=t0, X=X: e.tensor_mul(
                    xtmp.h[:, :], X[c].h[:, t0:t0 + 512], P[5].h[0:128, 0:512]),
                      reads=(X[c].key, P[5].key), writes=(xtmp.key,))
                em.op("vector", lambda e, c=c, t0=t0, R=R: e.tensor_add(
                    R[c].h[:, t0:t0 + 512], xtmp.h[:, :], P[6].h[0:128, 0:512]),
                      reads=(xtmp.key, P[6].key), writes=(R[c].key,))
        for dc in range(NDC):
            em.dma("qs", lambda e, dc=dc, X=X: e.dma_start(
                xsp_dram.h[dc, :, :], X[dc].h[:, :]),
                   reads=(X[dc].key,), writes=(xsp_dram.key,), stream="spill")
        x_scope.close()

    def attn_b_partB(R):
        b = Scope("right")
        abW = [b.sb(f"abW{j}", [128, D], BF16) for j in range(NDC)]
        abW2 = [b.sb(f"abW2{j}", [128, D], BF16) for j in range(NDC)]
        phiqh = [b.sb(f"phiqh{h}", [64, 512], BF16) for h in range(H)]
        attn_f = [b.sb(f"attn_f{c}", [128, 512], BF16) for c in range(NDC)]
        kv_ext = b.sb("kv_ext", [64, H * 65], BF16)
        phik = b.sb("phik", [128, D], BF16)
        vtile = b.sb("vtile", [128, D], BF16)
        recr2 = [b.sb(f"recr{i}", [128, 1], F32) for i in range(8)]
        atm = [b.sb(f"atm{i}", [128, 64], BF16) for i in range(8)]
        bqrow = b.sb("bqrow", [1, D], BF16)
        bkrow = b.sb("bkrow", [1, D], BF16)
        bvrow = b.sb("bvrow", [1, D], BF16)
        borow = b.sb("borow", [1, D], BF16)

        def ln1(sc):
            layernorm(R, sc, 1e-6, dst_list=XN, scaled=True, stats_banks=(P[0], P[1]))

        def load_lnfm():
            for c in range(NDC):
                mm(P[2], SL[0:128, c:c + 1], lnps.h, SL[0:1, c * 128:(c + 1) * 128],
                   ones_row_f.h, SL[0:1, 0:1], (lnps.key, ones_row_f.key),
                   c == 0, c == NDC - 1)
                mm(P[3], SL[0:128, c:c + 1], lnpb.h, SL[0:1, c * 128:(c + 1) * 128],
                   ones_row_f.h, SL[0:1, 0:1], (lnpb.key, ones_row_f.key),
                   c == 0, c == NDC - 1)
            cp("vector", lnsc.h, SL[:, :], P[2].h, SL[0:128, 0:NDC], (P[2].key,),
               (lnsc.key,))
            cp("vector", lnsb.h, SL[:, :], P[3].h, SL[0:128, 0:NDC], (P[3].key,),
               (lnsb.key,))

        load_row_f32(lnps, D, ab_ln1s)
        load_row_f32(lnpb, D, ab_ln1b)
        load_lnfm()
        load_row_bf16(bqrow, D, ab_bq)
        load_row_bf16(bkrow, D, ab_bk)
        load_row_bf16(bvrow, D, ab_bv)
        load_row_bf16(borow, D, ab_bo)
        for j in range(NDC):
            em.dma("qa", lambda e, j=j: e.dma_start(abW[j].h[:, :],
                                                    abw_dram.h[1, j * 128:(j + 1) * 128, :]),
                   reads=(abw_dram.key,), writes=(abW[j].key,), stream=f"abw{j % 2}")
            em.dma("qa", lambda e, j=j: e.dma_start(abW2[j].h[:, :],
                                                    abw_dram.h[2, j * 128:(j + 1) * 128, :]),
                   reads=(abw_dram.key,), writes=(abW2[j].key,), stream=f"abv{j % 2}")
        for bank in (P[2], P[3], P[4]):
            mm(bank, SL[0:128, 0:512], ones_row_b.h, SL[0:1, 0:128],
               zrow_b.h, SL[0:1, 0:512], (ones_row_b.key, zrow_b.key), True, True)
        for sc in range(NSC):
            ln1(sc)
            for tt in range(4):
                tc = sc * 4 + tt
                for wi, (wsb, bias) in enumerate(((abW, bkrow), (abW2, bvrow))):
                    for nh in range(2):
                        pb = P[[0, 1, 5, 6][(wi * 2 + nh) % 4]]
                        for j in range(NDC):
                            mm(pb, SL[0:128, 0:512], XN[j].h,
                               SL[:, tt * 128:(tt + 1) * 128],
                               wsb[j].h, SL[:, nh * 512:(nh + 1) * 512],
                               (XN[j].key, wsb[j].key), j == 0, False, fuse=True)
                        mm(pb, SL[0:128, 0:512], ones_row_b.h, SL[0:1, 0:128],
                           bias.h, SL[0:1, nh * 512:(nh + 1) * 512],
                           (ones_row_b.key, bias.key), False, True, fuse=True)
                        if wi == 0:
                            em.op("vector", lambda e, nh=nh, pb=pb: e.tensor_scalar(
                                phik.h[:, nh * 512:(nh + 1) * 512], pb.h[0:128, 0:512],
                                0.0, 1e-3, ALU.max, ALU.add),
                                  reads=(pb.key,), writes=(phik.key,))
                        else:
                            cp("vector", vtile.h, SL[:, nh * 512:(nh + 1) * 512], pb.h,
                               SL[0:128, 0:512], (pb.key,), (vtile.key,))
                last = (tc == NTC - 1)
                for h in range(H):
                    bank = P[2] if h < 8 else P[3]
                    off = (h % 8) * 64
                    mm(bank, SL[0:64, off:off + 64], phik.h, SL[:, h * 64:(h + 1) * 64],
                       vtile.h, SL[:, h * 64:(h + 1) * 64], (phik.key, vtile.key),
                       False, last)
                for dc in range(NDC):
                    mm(P[4], SL[0:128, dc:dc + 1], phik.h, SL[:, dc * 128:(dc + 1) * 128],
                       ones_col_b.h, SL[0:128, 0:1], (phik.key, ones_col_b.key),
                       False, last)
        for h in range(H):
            bank = P[2] if h < 8 else P[3]
            off = (h % 8) * 64
            cp("vector", kv_ext.h, SL[:, h * 65:h * 65 + 64], bank.h,
               SL[0:64, off:off + 64], (bank.key,), (kv_ext.key,))
        for dc in range(NDC):
            cp("vector", kv_ext.h, SL[:, (2 * dc) * 65 + 64:(2 * dc) * 65 + 65], P[4].h,
               SL[0:64, dc:dc + 1], (P[4].key,), (kv_ext.key,))
            cp("vector", kv_ext.h, SL[:, (2 * dc + 1) * 65 + 64:(2 * dc + 1) * 65 + 65],
               P[4].h, SL[64:128, dc:dc + 1], (P[4].key,), (kv_ext.key,))
        for j in range(NDC):
            em.dma("qa", lambda e, j=j: e.dma_start(abW[j].h[:, :],
                                                    abw_dram.h[0, j * 128:(j + 1) * 128, :]),
                   reads=(abw_dram.key,), writes=(abW[j].key,), stream=f"abw{j % 2}")
            em.dma("qa", lambda e, j=j: e.dma_start(abW2[j].h[:, :],
                                                    abw_dram.h[3, j * 128:(j + 1) * 128, :]),
                   reads=(abw_dram.key,), writes=(abW2[j].key,), stream=f"abv{j % 2}")
        for sc in range(NSC):
            t0 = sc * 512
            ln1(sc)
            for i in range(NDC):
                pb = P[[0, 1, 5, 6][i % 4]]
                for j in range(NDC):
                    mm(pb, SL[0:128, 0:512], abW[j].h, SL[:, i * 128:(i + 1) * 128],
                       XN[j].h, SL[:, 0:512], (abW[j].key, XN[j].key), j == 0, False,
                       fuse=True)
                mm(pb, SL[0:128, 0:512], bqrow.h, SL[0:1, i * 128:(i + 1) * 128],
                   ones_row_b.h, SL[0:1, 0:512], (bqrow.key, ones_row_b.key), False, True,
                   fuse=True)
                for sub in range(2):
                    hh = 2 * i + sub
                    em.op("vector", lambda e, hh=hh, sub=sub, pb=pb: e.tensor_scalar(
                        phiqh[hh].h[:, :], pb.h[sub * 64:(sub + 1) * 64, 0:512],
                        0.0, 1e-3, ALU.max, ALU.add),
                          reads=(pb.key,), writes=(phiqh[hh].key,))
            for tp in range(2):
                for hg in range(2):
                    for hh in range(8):
                        for t2 in range(2):
                            tt = tp * 2 + t2
                            h = hg * 8 + hh
                            s8 = t2 * 4 + h % 4
                            nb = P[[2, 3, 5, 6][h % 4]] if t2 == 0 else \
                                P[[5, 6, 2, 3][h % 4]]
                            mm(nb, SL[0:128, 0:65], phiqh[h].h,
                               SL[0:64, tt * 128:(tt + 1) * 128],
                               kv_ext.h, SL[:, h * 65:(h + 1) * 65],
                               (phiqh[h].key, kv_ext.key), True, True)
                            em.op("vector", lambda e, nb=nb, s8=s8: e.reciprocal(
                                recr2[s8].h[:, :], nb.h[0:128, 64:65]),
                                  reads=(nb.key,), writes=(recr2[s8].key,))
                            em.op("vector", lambda e, nb=nb, s8=s8: e.tensor_scalar(
                                atm[s8].h[:, :], nb.h[0:128, 0:64],
                                recr2[s8].h[:, 0:1], None, ALU.mult),
                                  reads=(nb.key, recr2[s8].key), writes=(atm[s8].key,))
                            em.op("tensor", lambda e, s8=s8, hh=hh, t2=t2: e.transpose(
                                PTB.h[t2 * 64:t2 * 64 + 64, hh * 128:(hh + 1) * 128],
                                atm[s8].h[:, :], ident_b.h[:, :]),
                                  reads=(atm[s8].key, ident_b.key), writes=(PTB.key,))
                    for hh in range(8):
                        for t2 in range(2):
                            tt = tp * 2 + t2
                            h = hg * 8 + hh
                            hc, ho = h // 2, (h % 2) * 64
                            cp("vector", attn_f[hc].h,
                               SL[ho:ho + 64, tt * 128:(tt + 1) * 128],
                               PTB.h, SL[t2 * 64:t2 * 64 + 64, hh * 128:(hh + 1) * 128],
                               (PTB.key,), (attn_f[hc].key,))
            for dc in range(NDC):
                pb = P[[0, 1, 5, 6][dc % 4]]
                for j in range(NDC):
                    mm(pb, SL[0:128, 0:512], abW2[j].h, SL[:, dc * 128:(dc + 1) * 128],
                       attn_f[j].h, SL[:, :], (abW2[j].key, attn_f[j].key), j == 0, False,
                       fuse=True)
                mm(pb, SL[0:128, 0:512], borow.h, SL[0:1, dc * 128:(dc + 1) * 128],
                   ones_row_b.h, SL[0:1, 0:512], (borow.key, ones_row_b.key), False, True,
                   fuse=True)
                em.op("vector", lambda e, dc=dc, t0=t0, pb=pb, R=R: e.tensor_add(
                    R[dc].h[:, t0:t0 + 512], R[dc].h[:, t0:t0 + 512], pb.h[0:128, 0:512]),
                      reads=(R[dc].key, pb.key), writes=(R[dc].key,))
        b.close()

        m = Scope("right")
        W1q2 = [[m.sb(f"W1q{p}_{j}", [128, D], BF16) for j in range(NDC)]
                for p in range(2)]
        w2bl8 = [m.sb(f"w2bl8_{i}", [128, D], BF16) for i in range(NDC)]
        hsb8 = [m.sb(f"hsb8_{i}", [128, 512], BF16) for i in range(NDC)]
        xn2 = [m.sb(f"xn2_{c}", [128, 512], BF16) for c in range(NDC)]
        b1fmb = m.sb("b1fmbm", [128, 32], BF16)
        b1fm = m.sb("b1fmm", [128, 32], F32)
        b2row_f = m.sb("b2rowm", [1, D], F32)
        em.dma("qs", lambda e: e.dma_start(
            b1fmb.h[:, :], abb1b_dram.h[0].rearrange("(a b) -> a b", a=32),
            transpose=True),
               reads=(abb1b_dram.key,), writes=(b1fmb.key,), stream="b1t")
        cp("vector", b1fm.h, SL[:, :], b1fmb.h, SL[:, :], (b1fmb.key,), (b1fm.key,))
        load_row_f32(b2row_f, D, ab_b2)
        load_row_f32(lnps, D, ab_ln2s)
        load_row_f32(lnpb, D, ab_ln2b)
        load_lnfm()

        def load_w1q_m(fq):
            buf = W1q2[fq % 2]
            for j in range(NDC):
                em.dma("qa" if j % 2 == 0 else "qs",
                       lambda e, fq=fq, j=j, buf=buf: e.dma_start(
                           buf[j].h[:, :],
                           abw1b.h[j * 128:(j + 1) * 128, fq * 1024:(fq + 1) * 1024]),
                       reads=(abw1b.key,), writes=(buf[j].key,),
                       stream=f"abw1_{j % 4}")

        def load_w2_m(fq):
            for fj in range(NDC):
                fi = fq * 8 + fj
                em.dma("qa" if fj % 2 == 0 else "qs",
                       lambda e, fi=fi, fj=fj: e.dma_start(
                           w2bl8[fj].h[:, :], abw2b.h[fi * 128:(fi + 1) * 128, :]),
                       reads=(abw2b.key,), writes=(w2bl8[fj].key,),
                       stream=f"abw2_{fj % 4}")

        load_w1q_m(0)
        for sc in range(NSC):
            t0 = sc * 512
            layernorm(R, sc, 1e-6, dst_list=xn2, scaled=True)
            for fq in range(4):
                if fq < 3:
                    load_w1q_m(fq + 1)
                elif sc < NSC - 1:
                    load_w1q_m(0)
                W1q = W1q2[fq % 2]
                load_w2_m(fq)
                for fj in range(NDC):
                    fi = fq * 8 + fj
                    pb = P[fj % 2]
                    for j in range(NDC):
                        mm(pb, SL[0:128, 0:512], W1q[j].h, SL[:, fj * 128:(fj + 1) * 128],
                           xn2[j].h, SL[:, 0:512], (W1q[j].key, xn2[j].key),
                           j == 0, j == NDC - 1, fuse=True)
                    em.op("scalar", lambda e, fj=fj, fi=fi, pb=pb: e.activation(
                        hsb8[fj].h[:, :], pb.h[0:128, 0:512], AF.Gelu_apprx_tanh,
                        bias=b1fm.h[:, fi:fi + 1]),
                          reads=(pb.key, b1fm.key), writes=(hsb8[fj].key,))
                for fj in range(NDC):
                    fi = fq * 8 + fj
                    wb = w2bl8[fj]
                    for dc in range(7):
                        mm(P[dc], SL[0:128, 0:512], wb.h, SL[:, dc * 128:(dc + 1) * 128],
                           hsb8[fj].h, SL[:, 0:512], (wb.key, hsb8[fj].key),
                           fj == 0, fj == NDC - 1)
                for dc in range(7):
                    em.op("vector", lambda e, dc=dc, t0=t0, R=R: e.tensor_add(
                        R[dc].h[:, t0:t0 + 512], R[dc].h[:, t0:t0 + 512],
                        P[dc].h[0:128, 0:512]),
                          reads=(R[dc].key, P[dc].key), writes=(R[dc].key,))
                for fj in range(NDC):
                    fi = fq * 8 + fj
                    wb = w2bl8[fj]
                    mm(P[0], SL[0:128, 0:512], wb.h, SL[:, 7 * 128:8 * 128],
                       hsb8[fj].h, SL[:, 0:512], (wb.key, hsb8[fj].key),
                       fj == 0, fj == NDC - 1)
                em.op("vector", lambda e, t0=t0, R=R: e.tensor_add(
                    R[7].h[:, t0:t0 + 512], R[7].h[:, t0:t0 + 512],
                    P[0].h[0:128, 0:512]),
                      reads=(R[7].key, P[0].key), writes=(R[7].key,))
            xld = sqt[0]
            for dc in range(NDC):
                em.dma("qs", lambda e, dc=dc, t0=t0, xld=xld: e.dma_start(
                    xld.h[:, :], xsp_dram.h[dc, :, t0:t0 + 512]),
                       reads=(xsp_dram.key,), writes=(xld.key,), stream="spill")
                mm(P[5], SL[0:128, 0:512], b2row_f.h, SL[0:1, dc * 128:(dc + 1) * 128],
                   ones_row_f.h, SL[0:1, 0:512], (b2row_f.key, ones_row_f.key), True, True)
                em.op("vector", lambda e, xld=xld: e.tensor_add(
                    xtmp.h[:, :], xld.h[:, :], P[5].h[0:128, 0:512]),
                      reads=(xld.key, P[5].key), writes=(xtmp.key,))
                em.op("vector", lambda e, dc=dc, t0=t0, R=R: e.tensor_add(
                    R[dc].h[:, t0:t0 + 512], R[dc].h[:, t0:t0 + 512], xtmp.h[:, :]),
                      reads=(R[dc].key, xtmp.key), writes=(R[dc].key,))
        m.close()

    for seq in range(NSEQ):
        xsc = Scope("right")
        X = [xsc.sb(f"X{c}", [128, S], F32) for c in range(NDC)]
        iosc = Scope("right")
        io_in = [iosc.sb(f"ioin{i}", [128, D], F32) for i in range(2)]
        for tc in range(NTC):
            par = tc % 2
            em.dma("qs", lambda e, seq=seq, tc=tc, par=par: e.dma_start(
                io_in[par].h[:, :], x_in.h[seq, tc * 128:(tc + 1) * 128, :]),
                   reads=(x_in.key,), writes=(io_in[par].key,), stream=f"ioi{par}")
            for dc in range(NDC):
                pb = P[dc % 7]
                em.op("tensor", lambda e, dc=dc, pb=pb, par=par: e.transpose(
                    pb.h[0:128, 0:128], io_in[par].h[:, dc * 128:(dc + 1) * 128],
                    ident_f.h[:, :]),
                      reads=(io_in[par].key, ident_f.key), writes=(pb.key,))
                cp("vector", X[dc].h, SL[:, tc * 128:(tc + 1) * 128], pb.h,
                   SL[0:128, 0:128], (pb.key,), (X[dc].key,))
        iosc.close()
        if seq == 0:
            issue_w1_conversions()
            cgen = gen_w2()

            def pump(n=1):
                for _ in range(n):
                    if next(cgen, "done") == "done":
                        break
        else:
            pump = None
        if NSUB >= 1:
            attn_a(X, pump=pump)
        if seq == 0:
            for _ in range(100):
                if next(cgen, "done") == "done":
                    break
            fgen = gen_ab()

            def fpump(n=2):
                for _ in range(n):
                    if next(fgen, "done") == "done":
                        break
        else:
            fpump = None
        if NSUB >= 2:
            ffn(X, pump=fpump)
        if seq == 0 and NSUB >= 2:
            for _ in range(100):
                if next(fgen, "done") == "done":
                    break
        elif seq == 0:
            fgen2 = gen_ab()
            for _ in range(100):
                if next(fgen2, "done") == "done":
                    break
        if NSUB >= 3:
            rsc = Scope("left")
            R = [rsc.sb(f"R{c}", [128, S], F32) for c in range(NDC)]
            attn_b_partA(X, R, xsc)
            attn_b_partB(R)
            stream = R
        else:
            stream = X
        if NSUB >= 4:
            ffn(stream, final=True)
        osc = Scope("right")
        iot2 = [osc.sb(f"iot2_{i}", [128, D], F32) for i in range(2)]
        yr2 = [osc.sb(f"yr2_{i}", [128, D], F32) for i in range(2)]
        for tc in range(NTC):
            par = tc % 2
            if NSUB >= 4:
                em.dma("qa", lambda e, tc=tc, par=par: e.dma_start(
                    yr2[par].h[:, :], yf_dram.h[tc * 128:(tc + 1) * 128, :]),
                       reads=(yf_dram.key,), writes=(yr2[par].key,), stream=f"yrd{par}")
            for dc in range(NDC):
                pb = P[dc % 7]
                em.op("tensor", lambda e, dc=dc, tc=tc, pb=pb, stream=stream: e.transpose(
                    pb.h[0:128, 0:128], stream[dc].h[:, tc * 128:(tc + 1) * 128],
                    ident_f.h[:, :]),
                      reads=(stream[dc].key, ident_f.key), writes=(pb.key,))
                if NSUB >= 4:
                    em.op("vector", lambda e, dc=dc, pb=pb, par=par: e.tensor_add(
                        iot2[par].h[:, dc * 128:(dc + 1) * 128],
                        yr2[par].h[:, dc * 128:(dc + 1) * 128], pb.h[0:128, 0:128]),
                          reads=(yr2[par].key, pb.key), writes=(iot2[par].key,))
                else:
                    cp("vector", iot2[par].h, SL[:, dc * 128:(dc + 1) * 128], pb.h,
                       SL[0:128, 0:128], (pb.key,), (iot2[par].key,))
            em.dma("qs", lambda e, seq=seq, tc=tc, par=par: e.dma_start(
                y_out.h[seq, tc * 128:(tc + 1) * 128, :], iot2[par].h[:, :]),
                   reads=(iot2[par].key,), writes=(y_out.key,), stream=f"ioo{par}")
        osc.close()
        if NSUB >= 3:
            rsc.close()
        else:
            xsc.close()
    em.barrier()

    with nc.Block() as block:
        @block.tensor
        def _(e):
            for fn in em.thunks["tensor"]:
                fn(e)

        @block.vector
        def _(e):
            for fn in em.thunks["vector"]:
                fn(e)

        @block.scalar
        def _(e):
            for fn in em.thunks["scalar"]:
                fn(e)

        @block.gpsimd
        def _(e):
            for fn in em.thunks["gpsimd"]:
                fn(e)

        @block.sync
        def _(e):
            for fn in em.thunks["sync"]:
                fn(e)

    root.close()
    return nc


def kernel(**inputs):
    cfg = make_cfg()
    nc = build_program(cfg)
    x = np.ascontiguousarray(np.asarray(inputs["L0_attn_a_x"], np.float32))
    common = {
        "aa_wq": inputs["L0_attn_a_wq"], "aa_wk": inputs["L0_attn_a_wk"],
        "aa_wv": inputs["L0_attn_a_wv"], "aa_wo": inputs["L0_attn_a_wo"],
        "aa_gam": inputs["L0_attn_a_gammas"],
        "ab_ln1s": inputs["L0_attn_b_ln1_scale"], "ab_ln1b": inputs["L0_attn_b_ln1_bias"],
        "ab_wq": np.asarray(inputs["L0_attn_b_Wq"]).reshape(1024, 1024),
        "ab_wk": np.asarray(inputs["L0_attn_b_Wk"]).reshape(1024, 1024),
        "ab_wv": np.asarray(inputs["L0_attn_b_Wv"]).reshape(1024, 1024),
        "ab_bq": np.asarray(inputs["L0_attn_b_bq"]).reshape(1024),
        "ab_bk": np.asarray(inputs["L0_attn_b_bk"]).reshape(1024),
        "ab_bv": np.asarray(inputs["L0_attn_b_bv"]).reshape(1024),
        "ab_wo": np.asarray(inputs["L0_attn_b_Wo"]).reshape(1024, 1024),
        "ab_bo": inputs["L0_attn_b_bo"],
        "ab_ln2s": inputs["L0_attn_b_ln2_scale"], "ab_ln2b": inputs["L0_attn_b_ln2_bias"],
        "ab_w1": inputs["L0_attn_b_W1"], "ab_b1": inputs["L0_attn_b_b1"],
        "ab_w2": inputs["L0_attn_b_W2"], "ab_b2": inputs["L0_attn_b_b2"],
        "f_wr": inputs["L0_ffn_wr"], "f_br": inputs["L0_ffn_br"],
        "f_w1": inputs["L0_ffn_w1"], "f_b1": inputs["L0_ffn_b1"],
        "f_w2": inputs["L0_ffn_w2"], "f_b2": inputs["L0_ffn_b2"],
    }
    common = {k: np.ascontiguousarray(np.asarray(v, np.float32)) for k, v in common.items()}
    in_maps = []
    for c in range(8):
        m = dict(common)
        m["x"] = np.ascontiguousarray(x[2 * c:2 * c + 2])
        in_maps.append(m)
    res = run_bass_kernel_spmd(nc, in_maps, core_ids=list(range(8)))
    return np.concatenate([np.asarray(r["y"]) for r in res.results], axis=0)


# version 27
# speedup vs baseline: 1.0498x; 1.0498x over previous
import numpy as np
from contextlib import ExitStack
import concourse.bass as bass
import concourse.mybir as mybir
from concourse.bass_utils import run_bass_kernel_spmd

F32 = mybir.dt.float32
BF16 = mybir.dt.bfloat16
I32 = mybir.dt.int32
U32 = mybir.dt.uint32
AF = mybir.ActivationFunctionType
ALU = mybir.AluOpType
SL = np.s_

ENGINES = ("tensor", "vector", "scalar", "gpsimd", "sync")


def make_cfg(S=2048, nseq=2, cap=384, n_sub=4):
    return dict(S=S, nseq=nseq, D=1024, H=16, HD=64, E=8, F=4096,
                NDC=8, NTC=S // 128, NSC=S // 512, CAP=cap, n_sub=n_sub)


class Em:
    def __init__(self, nc, es):
        self.nc = nc
        self.thunks = {e: [] for e in ENGINES}
        self.count = {e: 0 for e in ENGINES}
        self.selfsem = {}
        self.sems = {}
        for e in ENGINES:
            name = "sq_" + e
            self.sems[name] = es.enter_context(nc.semaphore(name))
            self.selfsem[e] = name
        self.dmaq = {"qg": "gpsimd", "qs": "sync", "qa": "scalar"}
        self._es = es
        self.dmastream = {}
        self.dmacount = {}
        self.dmasems = set()
        self.waited = {e: {} for e in ENGINES}
        self.maxwait = {}
        self.last_writer = {}
        self.readers = {}
        self._pending = {}
        self.clock = {e: {} for e in ENGINES}
        self.snap = {}

    def _emit_wait(self, eng, sem, val):
        if self.maxwait.get(sem, 0) < val:
            self.maxwait[sem] = val
        ck = self.clock[eng]
        if ck.get(sem, 0) >= val:
            return
        w = self.waited[eng]
        if w.get(sem, 0) < val:
            w[sem] = val
            self.thunks[eng].append(lambda e, s=self.sems[sem], v=val: e.wait_ge(s, v))
        sn = self.snap.get((sem, val))
        if sn is not None:
            for s2, v2 in sn.items():
                if ck.get(s2, 0) < v2:
                    ck[s2] = v2
        if ck.get(sem, 0) < val:
            ck[sem] = val

    def _dep_waits(self, eng, reads, writes):
        best = {}

        def add(t):
            if t is None:
                return
            sem, val, peng = t
            if peng == "tensor" and eng == "tensor":
                return
            if sem in self.dmasems:
                val = 16 * self.dmacount[sem]
            if best.get(sem, 0) < val:
                best[sem] = val

        for k in reads:
            add(self.last_writer.get(k))
        for k in writes:
            add(self.last_writer.get(k))
            for t in self.readers.get(k, ()):
                add(t)
        for sem, val in best.items():
            self._emit_wait(eng, sem, val)

    def _record(self, tok, reads, writes):
        for k in writes:
            self.last_writer[k] = tok
            self.readers[k] = set()
        for k in reads:
            self.readers.setdefault(k, set()).add(tok)

    def op_multi(self, eng, fns, reads=(), writes=()):
        self._dep_waits(eng, reads, writes)
        self.count[eng] += 1
        sem = self.selfsem[eng]
        h = self.sems[sem]

        def thunk(e, fns=fns, h=h):
            for fn in fns[:-1]:
                fn(e)
            fns[-1](e).then_inc(h, 1)

        self.thunks[eng].append(thunk)
        tok = (sem, self.count[eng], eng)
        sn = dict(self.clock[eng])
        sn[sem] = self.count[eng]
        self.snap[(sem, self.count[eng])] = sn
        self._record(tok, reads, writes)
        return tok

    def op(self, eng, fn, reads=(), writes=(), light=False):
        self._dep_waits(eng, reads, writes)
        sem = self.selfsem[eng]
        if light:
            raise NotImplementedError
        self.count[eng] += 1
        h = self.sems[sem]

        def thunk(e, fn=fn, h=h):
            fn(e).then_inc(h, 1)

        self.thunks[eng].append(thunk)
        tok = (sem, self.count[eng], eng)
        sn = dict(self.clock[eng])
        sn[sem] = self.count[eng]
        self.snap[(sem, self.count[eng])] = sn
        self._record(tok, reads, writes)
        return tok

    def dma(self, q, fn, reads=(), writes=(), stream=None):
        eng = self.dmaq[q]
        sname = stream if stream is not None else q
        sem = self.dmastream.get(sname)
        if sem is None:
            sem = "dma_" + sname
            self.sems[sem] = self._es.enter_context(self.nc.semaphore(sem))
            self.dmastream[sname] = sem
            self.dmacount[sem] = 0
            self.dmasems.add(sem)
        mw = self.maxwait.get(sem, 0)
        if mw > 0:
            self._emit_wait(eng, sem, mw)
        self._dep_waits(eng, reads, writes)
        self.dmacount[sem] += 1
        h = self.sems[sem]

        def thunk(e, fn=fn, h=h):
            fn(e).then_inc(h, 16)

        self.thunks[eng].append(thunk)
        tok = (sem, 16 * self.dmacount[sem], eng)
        self.snap[(sem, 16 * self.dmacount[sem])] = dict(self.clock[eng])
        self._record(tok, reads, writes)
        return tok

    def barrier(self):
        targets = [(self.selfsem[e], self.count[e]) for e in ENGINES]
        targets += [(sem, 16 * cnt) for sem, cnt in self.dmacount.items()]
        for e in ENGINES:
            for sem, val in targets:
                if val > 0:
                    self._emit_wait(e, sem, val)


class T:
    __slots__ = ("h", "key")

    def __init__(self, h, key):
        self.h = h
        self.key = key


_uid = [0]


def build_program(cfg):
    S = cfg["S"]; NDC = cfg["NDC"]; NTC = cfg["NTC"]; NSC = cfg["NSC"]
    H = cfg["H"]; E = cfg["E"]; F = cfg["F"]; D = cfg["D"]
    CAP = cfg["CAP"]; NSEQ = cfg["nseq"]; NSUB = cfg["n_sub"]
    NFI = F // 128
    CAPT = CAP // 128

    nc = bass.Bass("TRN2", target_bir_lowering=False)
    root = ExitStack()
    em = Em(nc, root)

    def din(name, shape):
        return T(nc.dram_tensor(name, shape, F32, kind="ExternalInput"), "d_" + name)

    x_in = din("x", [NSEQ, S, D])
    aa_wq = din("aa_wq", [D, D]); aa_wk = din("aa_wk", [D, D])
    aa_wv = din("aa_wv", [D, D]); aa_wo = din("aa_wo", [D, D])
    aa_gam = din("aa_gam", [H])
    ab_ln1s = din("ab_ln1s", [D]); ab_ln1b = din("ab_ln1b", [D])
    ab_wq = din("ab_wq", [D, D]); ab_wk = din("ab_wk", [D, D]); ab_wv = din("ab_wv", [D, D])
    ab_bq = din("ab_bq", [D]); ab_bk = din("ab_bk", [D]); ab_bv = din("ab_bv", [D])
    ab_wo = din("ab_wo", [D, D]); ab_bo = din("ab_bo", [D])
    ab_ln2s = din("ab_ln2s", [D]); ab_ln2b = din("ab_ln2b", [D])
    ab_w1 = din("ab_w1", [D, F]); ab_b1 = din("ab_b1", [F])
    ab_w2 = din("ab_w2", [F, D]); ab_b2 = din("ab_b2", [D])
    f_wr = din("f_wr", [D, E]); f_br = din("f_br", [E])
    f_w1 = din("f_w1", [E, D, F]); f_b1 = din("f_b1", [E, F])
    f_w2 = din("f_w2", [E, F, D]); f_b2 = din("f_b2", [E, D])

    y_out = T(nc.dram_tensor("y", [NSEQ, S, D], F32, kind="ExternalOutput"), "d_y")

    xn_dram = T(nc.dram_tensor("xn_scr", [S + 128, D], BF16, kind="Internal"), "d_xnscr")
    yf_dram = T(nc.dram_tensor("yf_scr", [S + 128, D], F32, kind="Internal"), "d_yfscr")
    eid_dram = T(nc.dram_tensor("eid_scr", [NTC, 128], F32, kind="Internal"), "d_eidscr")
    idx_dram = T(nc.dram_tensor("idx_scr", [16384, 1], I32, kind="Internal"), "d_idxscr")
    wt_dram = T(nc.dram_tensor("wt_scr", [4, D, D], BF16, kind="Internal"), "d_wtscr")
    w1b_dram = T(nc.dram_tensor("w1b_scr", [E, D, F], BF16, kind="Internal"), "d_w1bscr")
    w2b_dram = T(nc.dram_tensor("w2b_scr", [E, F, D], BF16, kind="Internal"), "d_w2bscr")
    abw_dram = T(nc.dram_tensor("abw_scr", [4, D, D], BF16, kind="Internal"), "d_abwscr")
    abw1b = T(nc.dram_tensor("abw1b_scr", [D, F], BF16, kind="Internal"), "d_abw1b")
    b1b_dram = T(nc.dram_tensor("b1b_scr", [E, F], BF16, kind="Internal"), "d_b1b")
    abb1b_dram = T(nc.dram_tensor("abb1b_scr", [1, F], BF16, kind="Internal"), "d_abb1b")
    abw2b = T(nc.dram_tensor("abw2b_scr", [F, D], BF16, kind="Internal"), "d_abw2b")
    xsp_dram = T(nc.dram_tensor("xsp_scr", [NDC, 128, S], F32, kind="Internal"), "d_xspscr")

    arena = {"lo": None, "hi": None}

    def mk_sb(stack, name, shape, dt):
        _uid[0] += 1
        nm = f"{name}_{_uid[0]}"
        return T(stack.enter_context(nc.sbuf_tensor(nm, shape, dt)), nm)

    def _psize(shape, dt):
        n = 1
        for d in shape[1:]:
            n *= d
        sz = n * mybir.dt.size(dt)
        return (sz + 31) & ~31

    class Scope:
        def __init__(self, side="right"):
            self.side = side
            self.saved = None

        def _ensure(self):
            if self.saved is None:
                self.saved = arena[{"right": "hi", "left": "lo"}[self.side]]

        def sb(self, name, shape, dt):
            self._ensure()
            _uid[0] += 1
            nm = f"{name}_{_uid[0]}"
            sz = _psize(shape, dt)
            if self.side == "right":
                arena["hi"] -= sz
                off = arena["hi"]
            else:
                off = arena["lo"]
                arena["lo"] += sz
            assert arena["lo"] <= arena["hi"], \
                f"arena overflow at {nm}: lo={arena['lo']} hi={arena['hi']}"
            h = nc.alloc_sbuf_tensor_at(nm, shape, dt, offset=off)
            return T(h, nm)

        def close(self):
            em.barrier()
            self._ensure()
            if self.side == "right":
                arena["hi"] = self.saved
            else:
                arena["lo"] = self.saved

    def sbL(name, shape, dt):
        return mk_sb(root, name, shape, dt)

    P = [T(root.enter_context(nc.psum_tensor(f"P{i}", [128, 512], F32)), f"P{i}")
         for i in range(7)]
    PTB = T(root.enter_context(nc.psum_tensor("PTB", [128, 1024], BF16)), "PTB")

    ident_b = sbL("ident_b", [128, 128], BF16)
    ident_f = sbL("ident_f", [128, 128], F32)
    ones_row_f = sbL("ones_row_f", [1, 512], F32)
    ones_row_b = sbL("ones_row_b", [1, 512], BF16)
    ones_col_f = sbL("ones_col_f", [128, 1], F32)
    ones_col_b = sbL("ones_col_b", [128, 1], BF16)
    zrow_b = sbL("zrow_b", [1, 512], BF16)
    iota_tok = sbL("iota_tok", [128, NTC], I32)
    ecol = sbL("ecol", [E, 1], F32)
    brcol = sbL("brcol", [E, 1], F32)
    wrb = [sbL(f"wrb{j}", [128, E], F32) for j in range(NDC)]

    dmask = [sbL(f"dmask{h}", [128, 128], BF16) for h in range(H)]
    gqrow = [sbL(f"gqrow{h}", [128, 128], BF16) for h in range(H)]
    gkcol = sbL("gkcol", [128, H], F32)
    gcbc = sbL("gcbc", [64, H], F32)
    lngam = sbL("lngam", [1, H], F32)
    gamrow = sbL("gamrow", [1, H], F32)
    dtile = sbL("dtile", [128, 128], F32)
    riota_row = sbL("riota_row", [1, 128], F32)
    riota_col = sbL("riota_col", [128, 1], F32)
    lgcol = sbL("lgcol", [128, 1], F32)
    ln8col = sbL("ln8col", [128, 1], F32)
    eps5col = sbL("eps5col", [1, 1], F32)
    eps6col = sbL("eps6col", [1, 1], F32)
    gamsb = sbL("gamsb", [1, 16], F32)

    XN = [sbL(f"XN{c}", [128, 512], BF16) for c in range(NDC)]
    Sst = [sbL(f"Sst{h}", [64, 64], F32) for h in range(H)]
    Sbf = [sbL(f"Sbf{h}", [64, 64], BF16) for h in range(H)]
    sqt = [sbL(f"sqt{i}", [128, 512], F32) for i in range(2)]
    arow = sbL("arow", [1, 512], F32)
    brow = sbL("brow", [1, 512], F32)
    r512a = sbL("r512a", [1, 512], F32)
    r512b = sbL("r512b", [1, 512], F32)
    m2row = sbL("m2row", [1, 512], F32)
    rows2 = sbL("rows2", [2, 512], F32)
    lnps = sbL("lnps", [1, D], F32)
    lnpb = sbL("lnpb", [1, D], F32)
    lnsc = sbL("lnsc", [128, NDC], F32)
    lnsb = sbL("lnsb", [128, NDC], F32)
    xtmp = sbL("xtmp", [128, 512], F32)
    xtmp2 = sbL("xtmp2", [128, 512], F32)
    io_tm = sbL("io_tm", [128, D], F32)

    _persist_end = 0
    for _t in ([ident_b, ident_f, ones_row_f, ones_row_b, ones_col_f, ones_col_b,
                iota_tok, ecol, brcol, gkcol, gcbc, lngam, gamrow, dtile,
                riota_row, riota_col, lgcol, ln8col, eps5col, eps6col, gamsb,
                arow, brow, r512a, r512b, m2row, rows2, lnps, lnpb, lnsc, lnsb, xtmp, xtmp2, io_tm]
               + wrb + dmask + gqrow + XN + sqt):
        ml = nc.lookup_mloc(_t.h)
        _persist_end = max(_persist_end, ml.addr + _psize(list(_t.h.shape), _t.h.dtype))
    arena["lo"] = _persist_end
    arena["hi"] = 229344

    grpbuf = {"fns": [], "reads": set()}
    FUSE_GROUPS = False

    def mm(pt, prange, lhsT, lr, rhs, rr, reads, start, stop, fuse=False):
        fuse = fuse and FUSE_GROUPS
        fn = (lambda e, pt=pt, prange=prange, lhsT=lhsT, lr=lr, rhs=rhs, rr=rr,
              start=start, stop=stop: e.matmul(pt.h[prange], lhsT[lr], rhs[rr],
                                               start=start, stop=stop,
                                               skip_group_check=True))
        if fuse and not stop:
            grpbuf["fns"].append(fn)
            grpbuf["reads"].update(reads)
            return None
        if fuse:
            fns = grpbuf["fns"] + [fn]
            rds = tuple(grpbuf["reads"] | set(reads))
            grpbuf["fns"] = []
            grpbuf["reads"] = set()
            return em.op_multi("tensor", fns, reads=rds, writes=(pt.key,))
        return em.op("tensor", fn, reads=reads, writes=(pt.key,))

    def cp(eng, dst, dsl, src, ssl, reads, writes):
        if eng == "scalar":
            return em.op(eng, lambda e: e.copy(dst[dsl], src[ssl]), reads=reads, writes=writes)
        return em.op(eng, lambda e: e.tensor_copy(dst[dsl], src[ssl]), reads=reads, writes=writes)

    rot = [0]

    def transpose_pe(dstT, dsl, srcT, ssl, bf16=True, n=128):
        if bf16:
            pt = PTB
        else:
            pt = P[rot[0] % 2]
            rot[0] += 1
        idn = ident_b if bf16 else ident_f
        em.op("tensor", lambda e: e.transpose(pt.h[0:128, 0:n], srcT.h[ssl], idn.h[0:n, 0:n]),
              reads=(srcT.key, idn.key), writes=(pt.key,))
        cp("vector", dstT.h, dsl, pt.h, SL[0:128, 0:n], (pt.key,), (dstT.key,))

    def load_row_f32(dst, n, src, row=0):
        em.dma("qs", lambda e: e.dma_start(dst.h[row:row + 1, 0:n],
                                           src.h[:].rearrange("(a b) -> a b", a=1)),
               reads=(src.key,), writes=(dst.key,), stream="rows")

    def load_row_bf16(dst, n, src):
        em.dma("qg", lambda e: e.dma_start(dst.h[0:1, 0:n],
                                           src.h[:].rearrange("(a b) -> a b", a=1)),
               reads=(src.key,), writes=(dst.key,), stream="rows_b")

    em.op("gpsimd", lambda e: e.memset(ident_b.h[:, :], 1.0), writes=(ident_b.key,))
    em.op("gpsimd", lambda e: e.affine_select(ident_b.h[:, :], ident_b.h[:, :], [[-1, 128]],
                                              ALU.is_equal, 0.0, base=0, channel_multiplier=1),
          reads=(ident_b.key,), writes=(ident_b.key,))
    em.op("gpsimd", lambda e: e.memset(ident_f.h[:, :], 1.0), writes=(ident_f.key,))
    em.op("gpsimd", lambda e: e.affine_select(ident_f.h[:, :], ident_f.h[:, :], [[-1, 128]],
                                              ALU.is_equal, 0.0, base=0, channel_multiplier=1),
          reads=(ident_f.key,), writes=(ident_f.key,))
    em.op("gpsimd", lambda e: e.memset(ones_row_f.h[:, :], 1.0), writes=(ones_row_f.key,))
    em.op("gpsimd", lambda e: e.memset(ones_row_b.h[:, :], 1.0), writes=(ones_row_b.key,))
    em.op("gpsimd", lambda e: e.memset(ones_col_f.h[:, :], 1.0), writes=(ones_col_f.key,))
    em.op("gpsimd", lambda e: e.memset(ones_col_b.h[:, :], 1.0), writes=(ones_col_b.key,))
    em.op("gpsimd", lambda e: e.memset(zrow_b.h[:, :], 0.0), writes=(zrow_b.key,))
    em.op("gpsimd", lambda e: e.memset(rows2.h[0:2, :], 1.0), writes=(rows2.key,))
    em.op("gpsimd", lambda e: e.memset(ln8col.h[:, :], float(np.log(0.125))),
          writes=(ln8col.key,))
    em.op("gpsimd", lambda e: e.memset(eps5col.h[:, :], 1e-5), writes=(eps5col.key,))
    em.op("gpsimd", lambda e: e.memset(eps6col.h[:, :], 1e-6), writes=(eps6col.key,))
    em.op("gpsimd", lambda e: e.iota(iota_tok.h[:, :], [[128, NTC]], base=0,
                                     channel_multiplier=1), writes=(iota_tok.key,))
    em.op("gpsimd", lambda e: e.iota(ecol.h[:, :], [[0, 1]], base=0, channel_multiplier=1,
                                     allow_small_or_imprecise_dtypes=True), writes=(ecol.key,))
    em.op("gpsimd", lambda e: e.iota(dtile.h[:, :], [[1, 128]], base=0, channel_multiplier=-1,
                                     allow_small_or_imprecise_dtypes=True), writes=(dtile.key,))
    em.op("gpsimd", lambda e: e.iota(riota_row.h[:, :], [[1, 128]], base=1, channel_multiplier=0,
                                     allow_small_or_imprecise_dtypes=True),
          writes=(riota_row.key,))
    em.op("gpsimd", lambda e: e.iota(riota_col.h[:, :], [[0, 1]], base=127, channel_multiplier=-1,
                                     allow_small_or_imprecise_dtypes=True),
          writes=(riota_col.key,))

    em.dma("qs", lambda e: e.dma_start(gamsb.h[0:1, 0:16],
                                       aa_gam.h[:].rearrange("(a b) -> a b", a=1)),
           reads=(aa_gam.key,), writes=(gamsb.key,))
    em.dma("qs", lambda e: e.dma_start(brcol.h[0:E, 0:1],
                                       f_br.h[:].rearrange("(a b) -> a b", b=1)),
           reads=(f_br.key,), writes=(brcol.key,))
    for j in range(NDC):
        em.dma("qs", lambda e, j=j: e.dma_start(wrb[j].h[:, :], f_wr.h[j * 128:(j + 1) * 128, :]),
               reads=(f_wr.key,), writes=(wrb[j].key,), stream="rows")

    em.op("scalar", lambda e: e.activation(lngam.h[:, :], gamsb.h[0:1, 0:16], AF.Ln),
          reads=(gamsb.key,), writes=(lngam.key,))
    em.op("scalar", lambda e: e.activation(gamrow.h[:, :], lngam.h[:, :], AF.Exp, scale=128.0),
          reads=(lngam.key,), writes=(gamrow.key,))
    mm(P[5], SL[0:64, 0:16], ones_row_f.h, SL[0:1, 0:64], gamrow.h, SL[0:1, 0:16],
       (ones_row_f.key, gamrow.key), True, True)
    cp("vector", gcbc.h, SL[:, :], P[5].h, SL[0:64, 0:16], (P[5].key,), (gcbc.key,))
    LN8 = float(np.log(0.125))
    for h in range(H):
        mm(P[5], SL[0:128, 0:1], ones_row_f.h, SL[0:1, 0:128], lngam.h, SL[0:1, h:h + 1],
           (ones_row_f.key, lngam.key), True, True)
        cp("vector", lgcol.h, SL[:, :], P[5].h, SL[0:128, 0:1], (P[5].key,), (lgcol.key,))
        em.op("scalar", lambda e, h=h: e.activation(dmask[h].h[:, :], dtile.h[:, :], AF.Exp,
                                                    bias=ln8col.h[:, 0:1],
                                                    scale=lgcol.h[:, 0:1]),
              reads=(dtile.key, lgcol.key, ln8col.key), writes=(dmask[h].key,))
        em.op("gpsimd", lambda e, h=h: e.affine_select(dmask[h].h[:, :], dmask[h].h[:, :],
                                                       [[1, 128]], ALU.is_ge, 0.0, base=0,
                                                       channel_multiplier=-1),
              reads=(dmask[h].key,), writes=(dmask[h].key,))
        em.op("scalar", lambda e, h=h: e.activation(r512a.h[0:1, 0:128], riota_row.h[:, :],
                                                    AF.Exp, bias=ln8col.h[0:1, 0:1],
                                                    scale=lngam.h[0:1, h:h + 1]),
              reads=(riota_row.key, lngam.key, ln8col.key), writes=(r512a.key,))
        mm(P[5], SL[0:128, 0:128], ones_row_f.h, SL[0:1, 0:128], r512a.h, SL[0:1, 0:128],
           (ones_row_f.key, r512a.key), True, True)
        cp("vector", gqrow[h].h, SL[:, :], P[5].h, SL[0:128, 0:128], (P[5].key,),
           (gqrow[h].key,))
        em.op("scalar", lambda e, h=h: e.activation(gkcol.h[:, h:h + 1], riota_col.h[:, :],
                                                    AF.Exp, scale=lgcol.h[:, 0:1]),
              reads=(riota_col.key, lgcol.key), writes=(gkcol.key,))

    tsc = Scope("right")
    wtmp = [tsc.sb(f"wtmp{j}", [128, D], BF16) for j in range(NDC)]
    wTt = [tsc.sb(f"wTt{j}", [128, D], BF16) for j in range(NDC)]
    for wi, wsrc in enumerate((aa_wq, aa_wk, aa_wv, aa_wo)):
        for j in range(NDC):
            em.dma("qg", lambda e, j=j, wsrc=wsrc: e.dma_start(
                wtmp[j].h[:, :], wsrc.h[j * 128:(j + 1) * 128, :]),
                   reads=(wsrc.key,), writes=(wtmp[j].key,), stream=f"wt{j % 2}")
        for i in range(NDC):
            for j in range(NDC):
                transpose_pe(wTt[j], SL[:, i * 128:(i + 1) * 128],
                             wtmp[i], SL[:, j * 128:(j + 1) * 128])
        for j in range(NDC):
            em.dma("qs", lambda e, wi=wi, j=j: e.dma_start(
                wt_dram.h[wi, j * 128:(j + 1) * 128, :], wTt[j].h[:, :]),
                   reads=(wTt[j].key,), writes=(wt_dram.key,), stream=f"wtd{j % 2}")
    tsc.close()

    def issue_w1_conversions():
        for e_i in range(E):
            for r0 in range(0, D, 256):
                em.dma("qg", lambda e, e_i=e_i, r0=r0: e.dma_start(
                    w1b_dram.h[e_i, r0:r0 + 256, :], f_w1.h[e_i, r0:r0 + 256, :]),
                       reads=(f_w1.key,), writes=(w1b_dram.key,), stream="cvd")
        em.dma("qg", lambda e: e.dma_start(b1b_dram.h[:, :], f_b1.h[:, :]),
               reads=(f_b1.key,), writes=(b1b_dram.key,), stream="cvd")
        em.dma("qg", lambda e: e.dma_start(
            abb1b_dram.h[:, :], ab_b1.h[0:F].rearrange("(a b) -> a b", a=1)),
               reads=(ab_b1.key,), writes=(abb1b_dram.key,), stream="cvd")

    def gen_w2():
        for e_i in range(E):
            for r0 in range(0, F, 1024):
                em.dma("qg", lambda e, e_i=e_i, r0=r0: e.dma_start(
                    w2b_dram.h[e_i, r0:r0 + 1024, :], f_w2.h[e_i, r0:r0 + 1024, :]),
                       reads=(f_w2.key,), writes=(w2b_dram.key,), stream="cvd")
                yield

    def gen_ab():
        for wi, wsrc in enumerate((ab_wq, ab_wk, ab_wv, ab_wo)):
            for r0 in range(0, D, 512):
                em.dma("qg", lambda e, wi=wi, r0=r0, wsrc=wsrc: e.dma_start(
                    abw_dram.h[wi, r0:r0 + 512, :], wsrc.h[r0:r0 + 512, :]),
                       reads=(wsrc.key,), writes=(abw_dram.key,), stream="cvd")
                yield
        for r0 in range(0, D, 256):
            em.dma("qg", lambda e, r0=r0: e.dma_start(
                abw1b.h[r0:r0 + 256, :], ab_w1.h[r0:r0 + 256, :]),
                   reads=(ab_w1.key,), writes=(abw1b.key,), stream="cvd")
            yield
        for r0 in range(0, F, 1024):
            em.dma("qg", lambda e, r0=r0: e.dma_start(
                abw2b.h[r0:r0 + 1024, :], ab_w2.h[r0:r0 + 1024, :]),
                   reads=(ab_w2.key,), writes=(abw2b.key,), stream="cvd")
            yield

    def layernorm(src_list, sc, eps, dst_list=None, scaled=False, stats_banks=None):
        pa, pb = stats_banks if stats_banks is not None else (P[3], P[4])
        t0 = sc * 512
        csl = SL[:, t0:t0 + 512]
        for c in range(NDC):
            sq = sqt[c % 2]
            em.op("scalar", lambda e, c=c, sq=sq, csl=csl: e.activation(
                sq.h[:, :], src_list[c].h[csl], AF.Square),
                  reads=(src_list[c].key,), writes=(sq.key,))
            mm(pa, SL[0:1, 0:512], ones_col_f.h, SL[0:128, 0:1], src_list[c].h, csl,
               (ones_col_f.key, src_list[c].key), c == 0, c == NDC - 1)
            mm(pb, SL[0:1, 0:512], ones_col_f.h, SL[0:128, 0:1], sq.h, SL[:, :],
               (ones_col_f.key, sq.key), c == 0, c == NDC - 1)
        em.op("vector", lambda e, pa=pa: e.tensor_scalar_mul(
            r512a.h[:, :], pa.h[0:1, 0:512], 1.0 / D),
              reads=(pa.key,), writes=(r512a.key,))
        em.op("vector", lambda e, pb=pb: e.tensor_scalar_mul(
            r512b.h[:, :], pb.h[0:1, 0:512], 1.0 / D),
              reads=(pb.key,), writes=(r512b.key,))
        em.op("scalar", lambda e: e.activation(m2row.h[:, :], r512a.h[:, :], AF.Square),
              reads=(r512a.key,), writes=(m2row.key,))
        em.op("vector", lambda e: e.tensor_sub(r512b.h[:, :], r512b.h[:, :], m2row.h[:, :]),
              reads=(r512b.key, m2row.key), writes=(r512b.key,))
        ecl = eps5col if eps == 1e-5 else eps6col
        em.op("scalar", lambda e, ecl=ecl: e.activation(r512b.h[:, :], r512b.h[:, :], AF.Sqrt,
                                                        bias=ecl.h[0:1, 0:1]),
              reads=(r512b.key, ecl.key), writes=(r512b.key,))
        em.op("vector", lambda e: e.reciprocal(arow.h[:, :], r512b.h[:, :]),
              reads=(r512b.key,), writes=(arow.key,))
        em.op("vector", lambda e: e.tensor_scalar_mul(r512a.h[:, :], r512a.h[:, :], -1.0),
              reads=(r512a.key,), writes=(r512a.key,))
        em.op("vector", lambda e: e.tensor_mul(brow.h[:, :], r512a.h[:, :], arow.h[:, :]),
              reads=(r512a.key, arow.key), writes=(brow.key,))
        cp("vector", rows2.h, SL[0:1, :], brow.h, SL[0:1, :], (brow.key,), (rows2.key,))
        if dst_list is None:
            return
        mm(P[5], SL[0:128, 0:512], ones_row_f.h, SL[0:1, 0:128],
           arow.h, SL[0:1, :], (ones_row_f.key, arow.key), True, True)
        mm(P[6], SL[0:128, 0:512], ones_row_f.h, SL[0:1, 0:128],
           rows2.h, SL[0:1, 0:512], (ones_row_f.key, rows2.key), True, True)
        for c in range(NDC):
            xt_ = xtmp if c % 2 == 0 else xtmp2
            em.op("vector", lambda e, c=c, csl=csl, xt_=xt_: e.tensor_mul(
                xt_.h[:, :], src_list[c].h[csl], P[5].h[0:128, 0:512]),
                  reads=(src_list[c].key, P[5].key), writes=(xt_.key,))
            if scaled:
                em.op("vector", lambda e, c=c, xt_=xt_: e.tensor_add(
                    xt_.h[:, :], xt_.h[:, :], P[6].h[0:128, 0:512]),
                      reads=(xt_.key, P[6].key), writes=(xt_.key,))
                em.op("vector", lambda e, c=c, xt_=xt_: e.tensor_scalar(
                    dst_list[c].h[:, 0:512], xt_.h[:, :],
                    lnsc.h[:, c:c + 1], lnsb.h[:, c:c + 1], ALU.mult, ALU.add),
                      reads=(xt_.key, lnsc.key, lnsb.key), writes=(dst_list[c].key,))
            else:
                em.op("vector", lambda e, c=c, xt_=xt_: e.tensor_add(
                    dst_list[c].h[:, 0:512], xt_.h[:, :], P[6].h[0:128, 0:512]),
                      reads=(xt_.key, P[6].key), writes=(dst_list[c].key,))

    def attn_a(X, pump=None):
        a = Scope("right")
        wsA = [a.sb(f"wsA{j}", [128, D], BF16) for j in range(NDC)]
        wsB = [a.sb(f"wsB{j}", [128, D], BF16) for j in range(NDC)]
        qfm = [a.sb(f"qfm{c}", [128, 512], BF16) for c in range(NDC)]
        kfm = [a.sb(f"kfm{c}", [128, 512], BF16) for c in range(NDC)]
        ktm = [a.sb(f"ktm{t}", [128, D], BF16) for t in range(4)]
        vtm = [a.sb(f"vtm{t}", [128, D], BF16) for t in range(4)]
        retf = [a.sb(f"retf{c}", [128, 512], BF16) for c in range(NDC)]
        wintra = [a.sb(f"wintra{i}", [128, 128], BF16) for i in range(2)]
        qhat = [a.sb(f"qhat{i}", [64, 128], BF16) for i in range(2)]
        khat = [a.sb(f"khat{i}", [128, 64], BF16) for i in range(2)]


        def loadw(slot, wi, sname):
            for j in range(NDC):
                em.dma("qs", lambda e, j=j, wi=wi, slot=slot: e.dma_start(
                    slot[j].h[:, :], wt_dram.h[wi, j * 128:(j + 1) * 128, :]),
                       reads=(wt_dram.key,), writes=(slot[j].key,),
                       stream=sname + str(j % 2))

        def pp():
            if pump is not None:
                pump()

        for sc in range(NSC):
            t0 = sc * 512
            layernorm(X, sc, 1e-5, dst_list=XN)
            pp()
            loadw(wsA, 0, "wsa")
            loadw(wsB, 1, "wsb")
            for (ws, dstf) in ((wsA, qfm), (wsB, kfm)):
                for i in range(NDC):
                    pb = P[i % 4]
                    for j in range(NDC):
                        mm(pb, SL[0:128, 0:512], ws[j].h, SL[:, i * 128:(i + 1) * 128],
                           XN[j].h, SL[:, 0:512], (ws[j].key, XN[j].key),
                           j == 0, j == NDC - 1, fuse=True)
                    cp("vector", dstf[i].h, SL[:, :], pb.h, SL[0:128, 0:512],
                       (pb.key,), (dstf[i].key,))
                    pp()
            loadw(wsA, 2, "wsa")
            for (ws, dstt) in ((wsB, ktm), (wsA, vtm)):
                for tt in range(4):
                    for nh in range(2):
                        pb = P[(tt * 2 + nh) % 4]
                        for j in range(NDC):
                            mm(pb, SL[0:128, 0:512],
                               XN[j].h, SL[:, tt * 128:(tt + 1) * 128],
                               ws[j].h, SL[:, nh * 512:(nh + 1) * 512],
                               (XN[j].key, ws[j].key), j == 0, j == NDC - 1, fuse=True)
                        cp("vector", dstt[tt].h, SL[:, nh * 512:(nh + 1) * 512],
                           pb.h, SL[0:128, 0:512], (pb.key,), (dstt[tt].key,))
                        pp()
            loadw(wsB, 3, "wsb")
            for tt in range(4):
                c = sc * 4 + tt
                for h in range(H):
                    hc, ho = h // 2, (h % 2) * 64
                    par = h % 2
                    wi_t, qh_t, kh_t = wintra[par], qhat[par], khat[par]
                    sbk = P[[2, 5, 0][h % 3]]
                    mm(sbk, SL[0:128, 0:128], kfm[hc].h, SL[ho:ho + 64, tt * 128:(tt + 1) * 128],
                       qfm[hc].h, SL[ho:ho + 64, tt * 128:(tt + 1) * 128],
                       (kfm[hc].key, qfm[hc].key), True, True)
                    em.op("vector", lambda e, sbk=sbk, h=h, wi_t=wi_t: e.tensor_mul(
                        wi_t.h[:, :], sbk.h[0:128, 0:128], dmask[h].h[:, :]),
                          reads=(sbk.key, dmask[h].key), writes=(wi_t.key,))
                    rbk = P[[3, 6, 1][h % 3]]
                    mm(rbk, SL[0:64, 0:128], vtm[tt].h, SL[:, h * 64:(h + 1) * 64],
                       wi_t.h, SL[:, :], (vtm[tt].key, wi_t.key), True, c == 0, fuse=True)
                    if c > 0:
                        em.op("vector", lambda e, h=h, hc=hc, ho=ho, tt=tt, qh_t=qh_t: e.tensor_mul(
                            qh_t.h[:, :], qfm[hc].h[ho:ho + 64, tt * 128:(tt + 1) * 128],
                            gqrow[h].h[ho:ho + 64, :]),
                              reads=(qfm[hc].key, gqrow[h].key), writes=(qh_t.key,))
                        mm(rbk, SL[0:64, 0:128], Sbf[h].h, SL[:, :], qh_t.h, SL[:, :],
                           (Sbf[h].key, qh_t.key), False, True, fuse=True)
                    cp("scalar", retf[hc].h, SL[ho:ho + 64, tt * 128:(tt + 1) * 128],
                       rbk.h, SL[0:64, 0:128], (rbk.key,), (retf[hc].key,))
                    em.op("scalar", lambda e, h=h, tt=tt, kh_t=kh_t: e.activation(
                        kh_t.h[:, :], ktm[tt].h[:, h * 64:(h + 1) * 64],
                        AF.Copy, scale=gkcol.h[:, h:h + 1]),
                          reads=(ktm[tt].key, gkcol.key), writes=(kh_t.key,))
                    ubk = P[4]
                    mm(ubk, SL[0:64, 0:64], kh_t.h, SL[:, :], vtm[tt].h,
                       SL[:, h * 64:(h + 1) * 64], (kh_t.key, vtm[tt].key), True, True)
                    if c == 0:
                        cp("vector", Sst[h].h, SL[:, :], ubk.h, SL[0:64, 0:64],
                           (ubk.key,), (Sst[h].key,))
                    else:
                        em.op("vector", lambda e, h=h, ubk=ubk: e.scalar_tensor_tensor(
                            Sst[h].h[:, :], Sst[h].h[:, :], gcbc.h[:, h:h + 1],
                            ubk.h[0:64, 0:64], ALU.mult, ALU.add),
                              reads=(Sst[h].key, gcbc.key, ubk.key), writes=(Sst[h].key,))
                    if c < NTC - 1:
                        cp("scalar", Sbf[h].h, SL[:, :], Sst[h].h, SL[:, :],
                           (Sst[h].key,), (Sbf[h].key,))
            for dc in range(NDC):
                pb = P[dc % 4]
                for j in range(NDC):
                    mm(pb, SL[0:128, 0:512], wsB[j].h, SL[:, dc * 128:(dc + 1) * 128],
                       retf[j].h, SL[:, :], (wsB[j].key, retf[j].key), j == 0, j == NDC - 1,
                       fuse=True)
                em.op("vector", lambda e, dc=dc, pb=pb, t0=t0, X=X: e.tensor_add(
                    X[dc].h[:, t0:t0 + 512], X[dc].h[:, t0:t0 + 512], pb.h[0:128, 0:512]),
                      reads=(X[dc].key, pb.key), writes=(X[dc].key,))
                pp()
            for _ in range(8):
                pp()
        a.close()

    def ffn(X, pump=None, final=False):
        r = Scope("right")
        sc_s1 = r.sb("sc_s1", [E, S], F32)
        mask8 = r.sb("mask8", [E, S], F32)
        rank8 = r.sb("rank8", [E, S], F32)
        zero8 = r.sb("zero8", [E, S], F32)
        tsc8 = r.sb("tsc8", [128, 8], F32)
        tmax = r.sb("tmax", [128, 8], F32)
        tidx = r.sb("tidx", [128, 8], U32)
        eidc = r.sb("eidc", [128, NTC], F32)
        eidrow = r.sb("eidrow", [1, S], F32)
        wsum_c = r.sb("wsum_c", [E, 1], F32)
        rsel = r.sb("rsel", [128, 1], F32)
        slotf = r.sb("slotf", [128, 1], F32)
        sloti = r.sb("sloti", [128, 1], I32)
        selc = r.sb("selc", [128, 1], F32)
        senti = r.sb("senti", [128, 32], I32)
        eid16 = r.sb("eid16", [16, 128], F32)
        stage = [r.sb(f"stage{i}", [128, D], BF16) for i in range(2)]

        em.op("gpsimd", lambda e: e.memset(zero8.h[:, :], 0.0), writes=(zero8.key,))
        em.op("gpsimd", lambda e: e.memset(stage[0].h[:, :], 0.0), writes=(stage[0].key,))
        em.dma("qs", lambda e: e.dma_start(xn_dram.h[S:S + 128, :], stage[0].h[:, :]),
               reads=(stage[0].key,), writes=(xn_dram.key,), stream="xnw0")
        em.op("gpsimd", lambda e: e.memset(senti.h[:, :], S), writes=(senti.key,))
        for sc in range(NSC):
            t0 = sc * 512
            layernorm(X, sc, 1e-5, dst_list=XN)
            for tt in range(4):
                tc = sc * 4 + tt
                stg = stage[tt % 2]
                for dc in range(NDC):
                    em.op("tensor", lambda e, dc=dc, tt=tt: e.transpose(
                        PTB.h[0:128, dc * 128:(dc + 1) * 128],
                        XN[dc].h[:, tt * 128:(tt + 1) * 128], ident_b.h[:, :]),
                          reads=(XN[dc].key, ident_b.key), writes=(PTB.key,))
                cp("vector", stg.h, SL[:, :], PTB.h, SL[0:128, 0:1024],
                   (PTB.key,), (stg.key,))
                em.dma("qs", lambda e, tc=tc, stg=stg: e.dma_start(
                    xn_dram.h[tc * 128:(tc + 1) * 128, :], stg.h[:, :]),
                       reads=(stg.key,), writes=(xn_dram.key,),
                       stream=f"xnw{tt % 2}")
            for j in range(NDC):
                mm(P[2], SL[0:E, 0:512], wrb[j].h, SL[:, :], X[j].h, SL[:, t0:t0 + 512],
                   (wrb[j].key, X[j].key), j == 0, j == NDC - 1, fuse=True)
            cp("vector", sc_s1.h, SL[0:E, t0:t0 + 512], P[2].h, SL[0:E, 0:512],
               (P[2].key,), (sc_s1.key,))
            if sc == 0:
                for j in range(NDC):
                    mm(P[2], SL[0:E, 0:1], wrb[j].h, SL[:, :], ones_col_f.h, SL[:, 0:1],
                       (wrb[j].key, ones_col_f.key), j == 0, j == NDC - 1)
                cp("vector", wsum_c.h, SL[:, :], P[2].h, SL[0:E, 0:1],
                   (P[2].key,), (wsum_c.key,))
            mm(P[5], SL[0:E, 0:512], ones_row_f.h, SL[0:1, 0:E], arow.h, SL[0:1, :],
               (ones_row_f.key, arow.key), True, True)
            mm(P[6], SL[0:E, 0:512], ones_row_f.h, SL[0:1, 0:E], brow.h, SL[0:1, :],
               (ones_row_f.key, brow.key), True, True)
            em.op("vector", lambda e, t0=t0: e.tensor_mul(
                sc_s1.h[0:E, t0:t0 + 512], sc_s1.h[0:E, t0:t0 + 512], P[5].h[0:E, 0:512]),
                  reads=(sc_s1.key, P[5].key), writes=(sc_s1.key,))
            em.op("vector", lambda e: e.tensor_scalar(
                xtmp.h[0:E, 0:512], P[6].h[0:E, 0:512], wsum_c.h[:, 0:1],
                brcol.h[:, 0:1], ALU.mult, ALU.add),
                  reads=(P[6].key, wsum_c.key, brcol.key), writes=(xtmp.key,))
            em.op("vector", lambda e, t0=t0: e.tensor_add(
                sc_s1.h[0:E, t0:t0 + 512], sc_s1.h[0:E, t0:t0 + 512], xtmp.h[0:E, 0:512]),
                  reads=(sc_s1.key, xtmp.key), writes=(sc_s1.key,))
        for tc in range(NTC):
            transpose_pe(tsc8, SL[0:128, 0:8], sc_s1, SL[0:E, tc * 128:(tc + 1) * 128],
                         bf16=False, n=8)
            em.op("vector", lambda e: e.max(tmax.h[:, :], tsc8.h[:, 0:8]),
                  reads=(tsc8.key,), writes=(tmax.key,))
            em.op("vector", lambda e: e.max_index(tidx.h[:, :], tmax.h[:, :], tsc8.h[:, 0:8]),
                  reads=(tmax.key, tsc8.key), writes=(tidx.key,))
            cp("vector", eidc.h, SL[:, tc:tc + 1], tidx.h, SL[:, 0:1], (tidx.key,), (eidc.key,))
        em.op("tensor", lambda e: e.transpose(P[2].h[0:NTC, 0:128], eidc.h[:, :],
                                              ident_f.h[0:128, 0:128]),
              reads=(eidc.key, ident_f.key), writes=(P[2].key,))
        cp("vector", eid16.h, SL[0:NTC, :], P[2].h, SL[0:NTC, 0:128], (P[2].key,), (eid16.key,))
        em.dma("qs", lambda e: e.dma_start(eid_dram.h[:, :], eid16.h[0:NTC, :]),
               reads=(eid16.key,), writes=(eid_dram.key,))
        em.dma("qs", lambda e: e.dma_start(
            eidrow.h[0:1, 0:S],
            eid_dram.h[:, :].rearrange("a b -> (a b)").rearrange("(a b) -> a b", a=1)),
               reads=(eid_dram.key,), writes=(eidrow.key,))
        for sc in range(NSC):
            bank = P[5] if sc % 2 == 0 else P[6]
            mm(bank, SL[0:E, 0:512], ones_row_f.h, SL[0:1, 0:E],
               eidrow.h, SL[0:1, sc * 512:(sc + 1) * 512], (ones_row_f.key, eidrow.key),
               True, True)
            em.op("vector", lambda e, sc=sc, bank=bank: e.tensor_scalar(
                mask8.h[0:E, sc * 512:(sc + 1) * 512], bank.h[0:E, 0:512],
                ecol.h[:, 0:1], None, ALU.is_equal),
                  reads=(bank.key, ecol.key), writes=(mask8.key,))
        em.op("vector", lambda e: e.tensor_tensor_scan(
            rank8.h[0:E, 0:S], mask8.h[0:E, 0:S], zero8.h[0:E, 0:S], 0.0, ALU.add, ALU.add),
              reads=(mask8.key, zero8.key), writes=(rank8.key,))
        nfull = (E * CAP) // 128
        em.dma("qs", lambda e, nfull=nfull: e.dma_start(
            idx_dram.h[0:E * CAP, :].rearrange("(p a) b -> p (a b)", p=128),
            senti.h[:, 0:nfull]),
               reads=(senti.key,), writes=(idx_dram.key,))
        em.op("vector", lambda e: e.tensor_mul(rank8.h[0:E, 0:S], rank8.h[0:E, 0:S],
                                               mask8.h[0:E, 0:S]),
              reads=(rank8.key, mask8.key), writes=(rank8.key,))
        for tc in range(NTC):
            mm(P[5], SL[0:128, 0:1], rank8.h, SL[0:E, tc * 128:(tc + 1) * 128],
               ones_col_f.h, SL[0:E, 0:1], (rank8.key, ones_col_f.key), True, True)
            cp("vector", rsel.h, SL[:, :], P[5].h, SL[0:128, 0:1], (P[5].key,), (rsel.key,))
            em.op("vector", lambda e, tc=tc: e.tensor_scalar(
                slotf.h[:, :], eidc.h[:, tc:tc + 1], float(CAP), -1.0, ALU.mult, ALU.add),
                  reads=(eidc.key,), writes=(slotf.key,))
            em.op("vector", lambda e: e.tensor_add(slotf.h[:, :], slotf.h[:, :], rsel.h[:, :]),
                  reads=(slotf.key, rsel.key), writes=(slotf.key,))
            em.op("vector", lambda e: e.tensor_scalar(
                selc.h[:, :], rsel.h[:, :], float(CAP), None, ALU.is_gt),
                  reads=(rsel.key,), writes=(selc.key,))
            em.op("vector", lambda e: e.scalar_tensor_tensor(
                slotf.h[:, :], selc.h[:, :], 8192.0, slotf.h[:, :], ALU.mult, ALU.add),
                  reads=(selc.key, slotf.key), writes=(slotf.key,))
            cp("vector", sloti.h, SL[:, :], slotf.h, SL[:, :], (slotf.key,), (sloti.key,))
            em.dma("qg", lambda e, tc=tc: e.indirect_dma_start(
                idx_dram.h[:, :], bass.IndirectOffsetOnAxis(sloti.h[:, :], 0),
                iota_tok.h[:, tc:tc + 1], None),
                   reads=(sloti.key, iota_tok.key), writes=(idx_dram.key,))
        r.close()

        xp = Scope("right")
        xg_tm = [xp.sb(f"xg_tm{i}", [128, D], BF16) for i in range(2)]
        idxc2 = [xp.sb(f"idxc2_{i}", [128, 1], I32) for i in range(2)]
        xg_fm = [xp.sb(f"xg_fm{c}", [128, CAP], BF16) for c in range(NDC)]
        hsb = [xp.sb(f"hsb{i}", [128, CAP], BF16) for i in range(NFI)]
        W1Q = [[xp.sb(f"W1Q{p}_{j}", [128, 1024], BF16) for j in range(NDC)]
               for p in range(2)]
        w2pr = [xp.sb(f"w2pr{i}", [128, 2048], BF16) for i in range(2)]
        b1fmb = xp.sb("b1fmb", [128, 32], BF16)
        b1fm = xp.sb("b1fm", [128, 32], F32)
        b2row = xp.sb("b2row", [1, D], BF16)
        ysc = xp.sb("ysc", [128, D], F32)
        ysc2 = [ysc, ysc]
        idxcol = xp.sb("idxcol", [128, 1], I32)
        for c in range(NDC):
            em.op("gpsimd", lambda e, c=c: e.memset(xg_fm[c].h[:, :], 0.0),
                  writes=(xg_fm[c].key,))
        em.op("gpsimd", lambda e: e.memset(ysc.h[:, :], 0.0), writes=(ysc.key,))
        for tc in range(NTC):
            em.dma("qs", lambda e, tc=tc: e.dma_start(
                yf_dram.h[tc * 128:(tc + 1) * 128, :], ysc.h[:, :]),
                   reads=(ysc.key,), writes=(yf_dram.key,), stream="ypre")

        def load_w1q(e_i, q4):
            buf = W1Q[q4 % 2]
            for j in range(NDC):
                em.dma("qa" if j % 2 == 0 else "qs",
                       lambda e, e_i=e_i, q4=q4, j=j, buf=buf: e.dma_start(
                           buf[j].h[:, :],
                           w1b_dram.h[e_i, j * 128:(j + 1) * 128,
                                      q4 * 1024:(q4 + 1) * 1024]),
                       reads=(w1b_dram.key,), writes=(buf[j].key,),
                       stream=f"w1_{j % 4}")

        def load_w2pair(e_i, fp):
            wb = w2pr[fp % 2]
            em.dma("qa" if fp % 2 == 0 else "qs",
                   lambda e, e_i=e_i, fp=fp, wb=wb: e.dma_start(
                       wb.h[:, :].rearrange("p (a d) -> p a d", a=2),
                       w2b_dram.h[e_i, 2 * fp * 128:(2 * fp + 2) * 128, :].rearrange(
                           "(a p) d -> p a d", p=128)),
                   reads=(w2b_dram.key,), writes=(wb.key,), stream=f"w2_{fp % 2}")
            return wb

        load_w1q(0, 0)
        def gather_tile(e_i, k):
            par = (e_i * CAPT + k) % 2
            gt, ic = xg_tm[par], idxc2[par]
            em.dma("qs", lambda e, e_i=e_i, k=k, ic=ic: e.dma_start(
                ic.h[:, :],
                idx_dram.h[e_i * CAP + k * 128:e_i * CAP + (k + 1) * 128, :]),
                   reads=(idx_dram.key,), writes=(ic.key,), stream=f"idx{par}")
            em.dma("qg", lambda e, gt=gt, ic=ic: e.indirect_dma_start(
                gt.h[:, :], None, xn_dram.h[:, :],
                bass.IndirectOffsetOnAxis(ic.h[:, :], 0)),
                   reads=(xn_dram.key, ic.key), writes=(gt.key,),
                   stream=f"xg{par}")

        def place_tile(e_i, k):
            par = (e_i * CAPT + k) % 2
            gt = xg_tm[par]
            for dc in range(NDC):
                em.op("tensor", lambda e, dc=dc, gt=gt: e.transpose(
                    PTB.h[0:128, dc * 128:(dc + 1) * 128],
                    gt.h[:, dc * 128:(dc + 1) * 128], ident_b.h[:, :]),
                      reads=(gt.key, ident_b.key), writes=(PTB.key,))
            for dc in range(NDC):
                cp("vector",
                   xg_fm[dc].h, SL[:, k * 128:(k + 1) * 128],
                   PTB.h, SL[0:128, dc * 128:(dc + 1) * 128],
                   (PTB.key,), (xg_fm[dc].key,))

        gather_tile(0, 0)
        for e_i in range(E):
            for k in range(CAPT):
                if k + 1 < CAPT:
                    gather_tile(e_i, k + 1)
                elif e_i + 1 < E:
                    gather_tile(e_i + 1, 0)
                place_tile(e_i, k)
            em.dma("qg", lambda e, e_i=e_i: e.dma_start(b2row.h[0:1, :],
                                                        f_b2.h[e_i:e_i + 1, :]),
                   reads=(f_b2.key,), writes=(b2row.key,), stream="brow")
            em.dma("qs", lambda e, e_i=e_i: e.dma_start(
                b1fmb.h[:, :],
                b1b_dram.h[e_i].rearrange("(a b) -> a b", a=32), transpose=True),
                   reads=(b1b_dram.key,), writes=(b1fmb.key,), stream="b1t")
            cp("vector", b1fm.h, SL[:, :], b1fmb.h, SL[:, :], (b1fmb.key,), (b1fm.key,))
            for q4 in range(4):
                if q4 < 3:
                    load_w1q(e_i, q4 + 1)
                elif e_i < E - 1:
                    load_w1q(e_i + 1, 0)
                buf = W1Q[q4 % 2]
                for fj in range(NDC):
                    fi = q4 * 8 + fj
                    pb = P[fj % 7]
                    for j in range(NDC):
                        mm(pb, SL[0:128, 0:CAP], buf[j].h, SL[:, fj * 128:(fj + 1) * 128],
                           xg_fm[j].h, SL[:, 0:CAP], (buf[j].key, xg_fm[j].key),
                           j == 0, j == NDC - 1, fuse=True)
                    em.op("scalar", lambda e, fi=fi, pb=pb: e.activation(
                        hsb[fi].h[:, 0:CAP], pb.h[0:128, 0:CAP], AF.Gelu,
                        bias=b1fm.h[:, fi:fi + 1]),
                          reads=(pb.key, b1fm.key), writes=(hsb[fi].key,))
            if pump is not None:
                pump(2)
            load_w2pair(e_i, 0)
            for fi in range(NFI):
                fp = fi // 2
                if fi % 2 == 0:
                    if fp + 1 < NFI // 2:
                        load_w2pair(e_i, fp + 1)
                wb = w2pr[fp % 2]
                co = (fi % 2) * 1024
                for k in range(CAPT):
                    pb, pb2 = P[1 + 2 * k], P[2 + 2 * k]
                    mm(pb, SL[0:128, 0:512], hsb[fi].h, SL[:, k * 128:(k + 1) * 128],
                       wb.h, SL[:, co:co + 512], (hsb[fi].key, wb.key),
                       fi == 0, False)
                    mm(pb2, SL[0:128, 0:512], hsb[fi].h, SL[:, k * 128:(k + 1) * 128],
                       wb.h, SL[:, co + 512:co + 1024], (hsb[fi].key, wb.key),
                       fi == 0, False)
            for k in range(CAPT):
                ybuf = ysc2[k % 2]
                pb, pb2 = P[1 + 2 * k], P[2 + 2 * k]
                mm(pb, SL[0:128, 0:512], ones_row_b.h, SL[0:1, 0:128],
                   b2row.h, SL[0:1, 0:512], (ones_row_b.key, b2row.key), False, True)
                mm(pb2, SL[0:128, 0:512], ones_row_b.h, SL[0:1, 0:128],
                   b2row.h, SL[0:1, 512:1024], (ones_row_b.key, b2row.key), False, True)
                cp("vector", ybuf.h, SL[:, 0:512], pb.h, SL[0:128, 0:512],
                   (pb.key,), (ybuf.key,))
                cp("vector", ybuf.h, SL[:, 512:1024], pb2.h, SL[0:128, 0:512],
                   (pb2.key,), (ybuf.key,))
                em.dma("qs", lambda e, e_i=e_i, k=k: e.dma_start(
                    idxcol.h[:, :],
                    idx_dram.h[e_i * CAP + k * 128:e_i * CAP + (k + 1) * 128, :]),
                       reads=(idx_dram.key,), writes=(idxcol.key,), stream="idx")
                em.dma("qg", lambda e, k=k, ybuf=ybuf: e.indirect_dma_start(
                    yf_dram.h[:, :], bass.IndirectOffsetOnAxis(idxcol.h[:, :], 0),
                    ybuf.h[:, :], None),
                       reads=(ybuf.key, idxcol.key), writes=(yf_dram.key,),
                       stream="ysct")
        if not final:
            for tc in range(NTC):
                em.dma("qs", lambda e, tc=tc: e.dma_start(
                    ysc.h[:, :], yf_dram.h[tc * 128:(tc + 1) * 128, :]),
                       reads=(yf_dram.key,), writes=(ysc.key,), stream="yrd")
                for dc in range(NDC):
                    pb = P[dc % 7]
                    em.op("tensor", lambda e, dc=dc, pb=pb: e.transpose(
                        pb.h[0:128, 0:128], ysc.h[:, dc * 128:(dc + 1) * 128],
                        ident_f.h[:, :]),
                          reads=(ysc.key, ident_f.key), writes=(pb.key,))
                    em.op("vector", lambda e, dc=dc, tc=tc, pb=pb, X=X: e.tensor_add(
                        X[dc].h[:, tc * 128:(tc + 1) * 128],
                        X[dc].h[:, tc * 128:(tc + 1) * 128], pb.h[0:128, 0:128]),
                          reads=(X[dc].key, pb.key), writes=(X[dc].key,))
        xp.close()

    def attn_b_partA(X, R, x_scope):
        for sc in range(NSC):
            t0 = sc * 512
            layernorm(X, sc, 1e-5, dst_list=None)
            for c in range(NDC):
                mm(P[5], SL[0:128, 0:512], ones_row_f.h, SL[0:1, 0:128],
                   arow.h, SL[0:1, :], (ones_row_f.key, arow.key), True, True)
                mm(P[6], SL[0:128, 0:512], ones_row_f.h, SL[0:1, 0:128],
                   rows2.h, SL[0:1, 0:512], (ones_row_f.key, rows2.key), True, True)
                em.op("vector", lambda e, c=c, t0=t0, X=X: e.tensor_mul(
                    xtmp.h[:, :], X[c].h[:, t0:t0 + 512], P[5].h[0:128, 0:512]),
                      reads=(X[c].key, P[5].key), writes=(xtmp.key,))
                em.op("vector", lambda e, c=c, t0=t0, R=R: e.tensor_add(
                    R[c].h[:, t0:t0 + 512], xtmp.h[:, :], P[6].h[0:128, 0:512]),
                      reads=(xtmp.key, P[6].key), writes=(R[c].key,))
        for dc in range(NDC):
            em.dma("qs", lambda e, dc=dc, X=X: e.dma_start(
                xsp_dram.h[dc, :, :], X[dc].h[:, :]),
                   reads=(X[dc].key,), writes=(xsp_dram.key,), stream="spill")
        x_scope.close()

    def attn_b_partB(R):
        b = Scope("right")
        abW = [b.sb(f"abW{j}", [128, D], BF16) for j in range(NDC)]
        abW2 = [b.sb(f"abW2{j}", [128, D], BF16) for j in range(NDC)]
        phiqh = [b.sb(f"phiqh{h}", [64, 512], BF16) for h in range(H)]
        attn_f = [b.sb(f"attn_f{c}", [128, 512], BF16) for c in range(NDC)]
        kv_ext = b.sb("kv_ext", [64, H * 65], BF16)
        phik = b.sb("phik", [128, D], BF16)
        vtile = b.sb("vtile", [128, D], BF16)
        recr2 = [b.sb(f"recr{i}", [128, 1], F32) for i in range(8)]
        atm = [b.sb(f"atm{i}", [128, 64], BF16) for i in range(8)]
        bqrow = b.sb("bqrow", [1, D], BF16)
        bkrow = b.sb("bkrow", [1, D], BF16)
        bvrow = b.sb("bvrow", [1, D], BF16)
        borow = b.sb("borow", [1, D], BF16)

        def ln1(sc):
            layernorm(R, sc, 1e-6, dst_list=XN, scaled=True, stats_banks=(P[0], P[1]))

        def load_lnfm():
            for c in range(NDC):
                mm(P[2], SL[0:128, c:c + 1], lnps.h, SL[0:1, c * 128:(c + 1) * 128],
                   ones_row_f.h, SL[0:1, 0:1], (lnps.key, ones_row_f.key),
                   c == 0, c == NDC - 1)
                mm(P[3], SL[0:128, c:c + 1], lnpb.h, SL[0:1, c * 128:(c + 1) * 128],
                   ones_row_f.h, SL[0:1, 0:1], (lnpb.key, ones_row_f.key),
                   c == 0, c == NDC - 1)
            cp("vector", lnsc.h, SL[:, :], P[2].h, SL[0:128, 0:NDC], (P[2].key,),
               (lnsc.key,))
            cp("vector", lnsb.h, SL[:, :], P[3].h, SL[0:128, 0:NDC], (P[3].key,),
               (lnsb.key,))

        load_row_f32(lnps, D, ab_ln1s)
        load_row_f32(lnpb, D, ab_ln1b)
        load_lnfm()
        load_row_bf16(bqrow, D, ab_bq)
        load_row_bf16(bkrow, D, ab_bk)
        load_row_bf16(bvrow, D, ab_bv)
        load_row_bf16(borow, D, ab_bo)
        for j in range(NDC):
            em.dma("qa", lambda e, j=j: e.dma_start(abW[j].h[:, :],
                                                    abw_dram.h[1, j * 128:(j + 1) * 128, :]),
                   reads=(abw_dram.key,), writes=(abW[j].key,), stream=f"abw{j % 2}")
            em.dma("qa", lambda e, j=j: e.dma_start(abW2[j].h[:, :],
                                                    abw_dram.h[2, j * 128:(j + 1) * 128, :]),
                   reads=(abw_dram.key,), writes=(abW2[j].key,), stream=f"abv{j % 2}")
        for bank in (P[2], P[3], P[4]):
            mm(bank, SL[0:128, 0:512], ones_row_b.h, SL[0:1, 0:128],
               zrow_b.h, SL[0:1, 0:512], (ones_row_b.key, zrow_b.key), True, True)
        for sc in range(NSC):
            ln1(sc)
            for tt in range(4):
                tc = sc * 4 + tt
                for wi, (wsb, bias) in enumerate(((abW, bkrow), (abW2, bvrow))):
                    for nh in range(2):
                        pb = P[[0, 1, 5, 6][(wi * 2 + nh) % 4]]
                        for j in range(NDC):
                            mm(pb, SL[0:128, 0:512], XN[j].h,
                               SL[:, tt * 128:(tt + 1) * 128],
                               wsb[j].h, SL[:, nh * 512:(nh + 1) * 512],
                               (XN[j].key, wsb[j].key), j == 0, False, fuse=True)
                        mm(pb, SL[0:128, 0:512], ones_row_b.h, SL[0:1, 0:128],
                           bias.h, SL[0:1, nh * 512:(nh + 1) * 512],
                           (ones_row_b.key, bias.key), False, True, fuse=True)
                        if wi == 0:
                            em.op("vector", lambda e, nh=nh, pb=pb: e.tensor_scalar(
                                phik.h[:, nh * 512:(nh + 1) * 512], pb.h[0:128, 0:512],
                                0.0, 1e-3, ALU.max, ALU.add),
                                  reads=(pb.key,), writes=(phik.key,))
                        else:
                            cp("vector", vtile.h, SL[:, nh * 512:(nh + 1) * 512], pb.h,
                               SL[0:128, 0:512], (pb.key,), (vtile.key,))
                last = (tc == NTC - 1)
                for h in range(H):
                    bank = P[2] if h < 8 else P[3]
                    off = (h % 8) * 64
                    mm(bank, SL[0:64, off:off + 64], phik.h, SL[:, h * 64:(h + 1) * 64],
                       vtile.h, SL[:, h * 64:(h + 1) * 64], (phik.key, vtile.key),
                       False, last)
                for dc in range(NDC):
                    mm(P[4], SL[0:128, dc:dc + 1], phik.h, SL[:, dc * 128:(dc + 1) * 128],
                       ones_col_b.h, SL[0:128, 0:1], (phik.key, ones_col_b.key),
                       False, last)
        for h in range(H):
            bank = P[2] if h < 8 else P[3]
            off = (h % 8) * 64
            cp("vector", kv_ext.h, SL[:, h * 65:h * 65 + 64], bank.h,
               SL[0:64, off:off + 64], (bank.key,), (kv_ext.key,))
        for dc in range(NDC):
            cp("vector", kv_ext.h, SL[:, (2 * dc) * 65 + 64:(2 * dc) * 65 + 65], P[4].h,
               SL[0:64, dc:dc + 1], (P[4].key,), (kv_ext.key,))
            cp("vector", kv_ext.h, SL[:, (2 * dc + 1) * 65 + 64:(2 * dc + 1) * 65 + 65],
               P[4].h, SL[64:128, dc:dc + 1], (P[4].key,), (kv_ext.key,))
        for j in range(NDC):
            em.dma("qa", lambda e, j=j: e.dma_start(abW[j].h[:, :],
                                                    abw_dram.h[0, j * 128:(j + 1) * 128, :]),
                   reads=(abw_dram.key,), writes=(abW[j].key,), stream=f"abw{j % 2}")
            em.dma("qa", lambda e, j=j: e.dma_start(abW2[j].h[:, :],
                                                    abw_dram.h[3, j * 128:(j + 1) * 128, :]),
                   reads=(abw_dram.key,), writes=(abW2[j].key,), stream=f"abv{j % 2}")
        for sc in range(NSC):
            t0 = sc * 512
            ln1(sc)
            for i in range(NDC):
                pb = P[[0, 1, 5, 6][i % 4]]
                for j in range(NDC):
                    mm(pb, SL[0:128, 0:512], abW[j].h, SL[:, i * 128:(i + 1) * 128],
                       XN[j].h, SL[:, 0:512], (abW[j].key, XN[j].key), j == 0, False,
                       fuse=True)
                mm(pb, SL[0:128, 0:512], bqrow.h, SL[0:1, i * 128:(i + 1) * 128],
                   ones_row_b.h, SL[0:1, 0:512], (bqrow.key, ones_row_b.key), False, True,
                   fuse=True)
                for sub in range(2):
                    hh = 2 * i + sub
                    em.op("vector", lambda e, hh=hh, sub=sub, pb=pb: e.tensor_scalar(
                        phiqh[hh].h[:, :], pb.h[sub * 64:(sub + 1) * 64, 0:512],
                        0.0, 1e-3, ALU.max, ALU.add),
                          reads=(pb.key,), writes=(phiqh[hh].key,))
            for tp in range(2):
                for hg in range(2):
                    for hh in range(8):
                        for t2 in range(2):
                            tt = tp * 2 + t2
                            h = hg * 8 + hh
                            s8 = t2 * 4 + h % 4
                            nb = P[[2, 3, 5, 6][h % 4]] if t2 == 0 else \
                                P[[5, 6, 2, 3][h % 4]]
                            mm(nb, SL[0:128, 0:65], phiqh[h].h,
                               SL[0:64, tt * 128:(tt + 1) * 128],
                               kv_ext.h, SL[:, h * 65:(h + 1) * 65],
                               (phiqh[h].key, kv_ext.key), True, True)
                            em.op("vector", lambda e, nb=nb, s8=s8: e.reciprocal(
                                recr2[s8].h[:, :], nb.h[0:128, 64:65]),
                                  reads=(nb.key,), writes=(recr2[s8].key,))
                            em.op("vector", lambda e, nb=nb, s8=s8: e.tensor_scalar(
                                atm[s8].h[:, :], nb.h[0:128, 0:64],
                                recr2[s8].h[:, 0:1], None, ALU.mult),
                                  reads=(nb.key, recr2[s8].key), writes=(atm[s8].key,))
                            em.op("tensor", lambda e, s8=s8, hh=hh, t2=t2: e.transpose(
                                PTB.h[t2 * 64:t2 * 64 + 64, hh * 128:(hh + 1) * 128],
                                atm[s8].h[:, :], ident_b.h[:, :]),
                                  reads=(atm[s8].key, ident_b.key), writes=(PTB.key,))
                    for hh in range(8):
                        for t2 in range(2):
                            tt = tp * 2 + t2
                            h = hg * 8 + hh
                            hc, ho = h // 2, (h % 2) * 64
                            cp("vector", attn_f[hc].h,
                               SL[ho:ho + 64, tt * 128:(tt + 1) * 128],
                               PTB.h, SL[t2 * 64:t2 * 64 + 64, hh * 128:(hh + 1) * 128],
                               (PTB.key,), (attn_f[hc].key,))
            for dc in range(NDC):
                pb = P[[0, 1, 5, 6][dc % 4]]
                for j in range(NDC):
                    mm(pb, SL[0:128, 0:512], abW2[j].h, SL[:, dc * 128:(dc + 1) * 128],
                       attn_f[j].h, SL[:, :], (abW2[j].key, attn_f[j].key), j == 0, False,
                       fuse=True)
                mm(pb, SL[0:128, 0:512], borow.h, SL[0:1, dc * 128:(dc + 1) * 128],
                   ones_row_b.h, SL[0:1, 0:512], (borow.key, ones_row_b.key), False, True,
                   fuse=True)
                em.op("vector", lambda e, dc=dc, t0=t0, pb=pb, R=R: e.tensor_add(
                    R[dc].h[:, t0:t0 + 512], R[dc].h[:, t0:t0 + 512], pb.h[0:128, 0:512]),
                      reads=(R[dc].key, pb.key), writes=(R[dc].key,))
        b.close()

        m = Scope("right")
        W1q2 = [[m.sb(f"W1q{p}_{j}", [128, D], BF16) for j in range(NDC)]
                for p in range(2)]
        w2bl8 = [m.sb(f"w2bl8_{i}", [128, D], BF16) for i in range(NDC)]
        hsb8 = [m.sb(f"hsb8_{i}", [128, 512], BF16) for i in range(NDC)]
        xn2 = [m.sb(f"xn2_{c}", [128, 512], BF16) for c in range(NDC)]
        b1fmb = m.sb("b1fmbm", [128, 32], BF16)
        b1fm = m.sb("b1fmm", [128, 32], F32)
        b2row_f = m.sb("b2rowm", [1, D], F32)
        em.dma("qs", lambda e: e.dma_start(
            b1fmb.h[:, :], abb1b_dram.h[0].rearrange("(a b) -> a b", a=32),
            transpose=True),
               reads=(abb1b_dram.key,), writes=(b1fmb.key,), stream="b1t")
        cp("vector", b1fm.h, SL[:, :], b1fmb.h, SL[:, :], (b1fmb.key,), (b1fm.key,))
        load_row_f32(b2row_f, D, ab_b2)
        load_row_f32(lnps, D, ab_ln2s)
        load_row_f32(lnpb, D, ab_ln2b)
        load_lnfm()

        def load_w1q_m(fq):
            buf = W1q2[fq % 2]
            for j in range(NDC):
                em.dma("qa" if j % 2 == 0 else "qs",
                       lambda e, fq=fq, j=j, buf=buf: e.dma_start(
                           buf[j].h[:, :],
                           abw1b.h[j * 128:(j + 1) * 128, fq * 1024:(fq + 1) * 1024]),
                       reads=(abw1b.key,), writes=(buf[j].key,),
                       stream=f"abw1_{j % 4}")

        def load_w2_m(fq):
            for fj in range(NDC):
                fi = fq * 8 + fj
                em.dma("qa" if fj % 2 == 0 else "qs",
                       lambda e, fi=fi, fj=fj: e.dma_start(
                           w2bl8[fj].h[:, :], abw2b.h[fi * 128:(fi + 1) * 128, :]),
                       reads=(abw2b.key,), writes=(w2bl8[fj].key,),
                       stream=f"abw2_{fj % 4}")

        load_w1q_m(0)
        for sc in range(NSC):
            t0 = sc * 512
            layernorm(R, sc, 1e-6, dst_list=xn2, scaled=True)
            for fq in range(4):
                if fq < 3:
                    load_w1q_m(fq + 1)
                elif sc < NSC - 1:
                    load_w1q_m(0)
                W1q = W1q2[fq % 2]
                load_w2_m(fq)
                for fj in range(NDC):
                    fi = fq * 8 + fj
                    pb = P[fj % 2]
                    for j in range(NDC):
                        mm(pb, SL[0:128, 0:512], W1q[j].h, SL[:, fj * 128:(fj + 1) * 128],
                           xn2[j].h, SL[:, 0:512], (W1q[j].key, xn2[j].key),
                           j == 0, j == NDC - 1, fuse=True)
                    em.op("scalar", lambda e, fj=fj, fi=fi, pb=pb: e.activation(
                        hsb8[fj].h[:, :], pb.h[0:128, 0:512], AF.Gelu_apprx_tanh,
                        bias=b1fm.h[:, fi:fi + 1]),
                          reads=(pb.key, b1fm.key), writes=(hsb8[fj].key,))
                for fj in range(NDC):
                    fi = fq * 8 + fj
                    wb = w2bl8[fj]
                    for dc in range(7):
                        mm(P[dc], SL[0:128, 0:512], wb.h, SL[:, dc * 128:(dc + 1) * 128],
                           hsb8[fj].h, SL[:, 0:512], (wb.key, hsb8[fj].key),
                           fj == 0, fj == NDC - 1)
                for dc in range(7):
                    em.op("vector", lambda e, dc=dc, t0=t0, R=R: e.tensor_add(
                        R[dc].h[:, t0:t0 + 512], R[dc].h[:, t0:t0 + 512],
                        P[dc].h[0:128, 0:512]),
                          reads=(R[dc].key, P[dc].key), writes=(R[dc].key,))
                for fj in range(NDC):
                    fi = fq * 8 + fj
                    wb = w2bl8[fj]
                    mm(P[0], SL[0:128, 0:512], wb.h, SL[:, 7 * 128:8 * 128],
                       hsb8[fj].h, SL[:, 0:512], (wb.key, hsb8[fj].key),
                       fj == 0, fj == NDC - 1)
                em.op("vector", lambda e, t0=t0, R=R: e.tensor_add(
                    R[7].h[:, t0:t0 + 512], R[7].h[:, t0:t0 + 512],
                    P[0].h[0:128, 0:512]),
                      reads=(R[7].key, P[0].key), writes=(R[7].key,))
            xld = sqt[0]
            for dc in range(NDC):
                em.dma("qs", lambda e, dc=dc, t0=t0, xld=xld: e.dma_start(
                    xld.h[:, :], xsp_dram.h[dc, :, t0:t0 + 512]),
                       reads=(xsp_dram.key,), writes=(xld.key,), stream="spill")
                mm(P[5], SL[0:128, 0:512], b2row_f.h, SL[0:1, dc * 128:(dc + 1) * 128],
                   ones_row_f.h, SL[0:1, 0:512], (b2row_f.key, ones_row_f.key), True, True)
                em.op("vector", lambda e, xld=xld: e.tensor_add(
                    xtmp.h[:, :], xld.h[:, :], P[5].h[0:128, 0:512]),
                      reads=(xld.key, P[5].key), writes=(xtmp.key,))
                em.op("vector", lambda e, dc=dc, t0=t0, R=R: e.tensor_add(
                    R[dc].h[:, t0:t0 + 512], R[dc].h[:, t0:t0 + 512], xtmp.h[:, :]),
                      reads=(R[dc].key, xtmp.key), writes=(R[dc].key,))
        m.close()

    for seq in range(NSEQ):
        xsc = Scope("right")
        X = [xsc.sb(f"X{c}", [128, S], F32) for c in range(NDC)]
        iosc = Scope("right")
        io_in = [iosc.sb(f"ioin{i}", [128, D], F32) for i in range(2)]
        for tc in range(NTC):
            par = tc % 2
            em.dma("qs", lambda e, seq=seq, tc=tc, par=par: e.dma_start(
                io_in[par].h[:, :], x_in.h[seq, tc * 128:(tc + 1) * 128, :]),
                   reads=(x_in.key,), writes=(io_in[par].key,), stream=f"ioi{par}")
            for dc in range(NDC):
                pb = P[dc % 7]
                em.op("tensor", lambda e, dc=dc, pb=pb, par=par: e.transpose(
                    pb.h[0:128, 0:128], io_in[par].h[:, dc * 128:(dc + 1) * 128],
                    ident_f.h[:, :]),
                      reads=(io_in[par].key, ident_f.key), writes=(pb.key,))
                cp("vector", X[dc].h, SL[:, tc * 128:(tc + 1) * 128], pb.h,
                   SL[0:128, 0:128], (pb.key,), (X[dc].key,))
        iosc.close()
        if seq == 0:
            issue_w1_conversions()
            cgen = gen_w2()

            def pump(n=1):
                for _ in range(n):
                    if next(cgen, "done") == "done":
                        break
        else:
            pump = None
        if NSUB >= 1:
            attn_a(X, pump=pump)
        if seq == 0:
            for _ in range(100):
                if next(cgen, "done") == "done":
                    break
            fgen = gen_ab()

            def fpump(n=2):
                for _ in range(n):
                    if next(fgen, "done") == "done":
                        break
        else:
            fpump = None
        if NSUB >= 2:
            ffn(X, pump=fpump)
        if seq == 0 and NSUB >= 2:
            for _ in range(100):
                if next(fgen, "done") == "done":
                    break
        elif seq == 0:
            fgen2 = gen_ab()
            for _ in range(100):
                if next(fgen2, "done") == "done":
                    break
        if NSUB >= 3:
            rsc = Scope("left")
            R = [rsc.sb(f"R{c}", [128, S], F32) for c in range(NDC)]
            attn_b_partA(X, R, xsc)
            attn_b_partB(R)
            stream = R
        else:
            stream = X
        if NSUB >= 4:
            ffn(stream, final=True)
        osc = Scope("right")
        iot2 = [osc.sb(f"iot2_{i}", [128, D], F32) for i in range(2)]
        yr2 = [osc.sb(f"yr2_{i}", [128, D], F32) for i in range(2)]
        for tc in range(NTC):
            par = tc % 2
            if NSUB >= 4:
                em.dma("qa", lambda e, tc=tc, par=par: e.dma_start(
                    yr2[par].h[:, :], yf_dram.h[tc * 128:(tc + 1) * 128, :]),
                       reads=(yf_dram.key,), writes=(yr2[par].key,), stream=f"yrd{par}")
            for dc in range(NDC):
                pb = P[dc % 7]
                em.op("tensor", lambda e, dc=dc, tc=tc, pb=pb, stream=stream: e.transpose(
                    pb.h[0:128, 0:128], stream[dc].h[:, tc * 128:(tc + 1) * 128],
                    ident_f.h[:, :]),
                      reads=(stream[dc].key, ident_f.key), writes=(pb.key,))
                if NSUB >= 4:
                    em.op("vector", lambda e, dc=dc, pb=pb, par=par: e.tensor_add(
                        iot2[par].h[:, dc * 128:(dc + 1) * 128],
                        yr2[par].h[:, dc * 128:(dc + 1) * 128], pb.h[0:128, 0:128]),
                          reads=(yr2[par].key, pb.key), writes=(iot2[par].key,))
                else:
                    cp("vector", iot2[par].h, SL[:, dc * 128:(dc + 1) * 128], pb.h,
                       SL[0:128, 0:128], (pb.key,), (iot2[par].key,))
            em.dma("qs", lambda e, seq=seq, tc=tc, par=par: e.dma_start(
                y_out.h[seq, tc * 128:(tc + 1) * 128, :], iot2[par].h[:, :]),
                   reads=(iot2[par].key,), writes=(y_out.key,), stream=f"ioo{par}")
        osc.close()
        if NSUB >= 3:
            rsc.close()
        else:
            xsc.close()
    em.barrier()

    with nc.Block() as block:
        @block.tensor
        def _(e):
            for fn in em.thunks["tensor"]:
                fn(e)

        @block.vector
        def _(e):
            for fn in em.thunks["vector"]:
                fn(e)

        @block.scalar
        def _(e):
            for fn in em.thunks["scalar"]:
                fn(e)

        @block.gpsimd
        def _(e):
            for fn in em.thunks["gpsimd"]:
                fn(e)

        @block.sync
        def _(e):
            for fn in em.thunks["sync"]:
                fn(e)

    root.close()
    return nc


def kernel(**inputs):
    cfg = make_cfg()
    nc = build_program(cfg)
    x = np.ascontiguousarray(np.asarray(inputs["L0_attn_a_x"], np.float32))
    common = {
        "aa_wq": inputs["L0_attn_a_wq"], "aa_wk": inputs["L0_attn_a_wk"],
        "aa_wv": inputs["L0_attn_a_wv"], "aa_wo": inputs["L0_attn_a_wo"],
        "aa_gam": inputs["L0_attn_a_gammas"],
        "ab_ln1s": inputs["L0_attn_b_ln1_scale"], "ab_ln1b": inputs["L0_attn_b_ln1_bias"],
        "ab_wq": np.asarray(inputs["L0_attn_b_Wq"]).reshape(1024, 1024),
        "ab_wk": np.asarray(inputs["L0_attn_b_Wk"]).reshape(1024, 1024),
        "ab_wv": np.asarray(inputs["L0_attn_b_Wv"]).reshape(1024, 1024),
        "ab_bq": np.asarray(inputs["L0_attn_b_bq"]).reshape(1024),
        "ab_bk": np.asarray(inputs["L0_attn_b_bk"]).reshape(1024),
        "ab_bv": np.asarray(inputs["L0_attn_b_bv"]).reshape(1024),
        "ab_wo": np.asarray(inputs["L0_attn_b_Wo"]).reshape(1024, 1024),
        "ab_bo": inputs["L0_attn_b_bo"],
        "ab_ln2s": inputs["L0_attn_b_ln2_scale"], "ab_ln2b": inputs["L0_attn_b_ln2_bias"],
        "ab_w1": inputs["L0_attn_b_W1"], "ab_b1": inputs["L0_attn_b_b1"],
        "ab_w2": inputs["L0_attn_b_W2"], "ab_b2": inputs["L0_attn_b_b2"],
        "f_wr": inputs["L0_ffn_wr"], "f_br": inputs["L0_ffn_br"],
        "f_w1": inputs["L0_ffn_w1"], "f_b1": inputs["L0_ffn_b1"],
        "f_w2": inputs["L0_ffn_w2"], "f_b2": inputs["L0_ffn_b2"],
    }
    common = {k: np.ascontiguousarray(np.asarray(v, np.float32)) for k, v in common.items()}
    in_maps = []
    for c in range(8):
        m = dict(common)
        m["x"] = np.ascontiguousarray(x[2 * c:2 * c + 2])
        in_maps.append(m)
    res = run_bass_kernel_spmd(nc, in_maps, core_ids=list(range(8)))
    return np.concatenate([np.asarray(r["y"]) for r in res.results], axis=0)


# version 28 (final)
# speedup vs baseline: 1.0588x; 1.0086x over previous
import numpy as np
from contextlib import ExitStack
import concourse.bass as bass
import concourse.mybir as mybir
from concourse.bass_utils import run_bass_kernel_spmd

F32 = mybir.dt.float32
BF16 = mybir.dt.bfloat16
I32 = mybir.dt.int32
U32 = mybir.dt.uint32
AF = mybir.ActivationFunctionType
ALU = mybir.AluOpType
SL = np.s_

ENGINES = ("tensor", "vector", "scalar", "gpsimd", "sync")


def make_cfg(S=2048, nseq=2, cap=384, n_sub=4):
    return dict(S=S, nseq=nseq, D=1024, H=16, HD=64, E=8, F=4096,
                NDC=8, NTC=S // 128, NSC=S // 512, CAP=cap, n_sub=n_sub)


class Em:
    def __init__(self, nc, es):
        self.nc = nc
        self.thunks = {e: [] for e in ENGINES}
        self.count = {e: 0 for e in ENGINES}
        self.selfsem = {}
        self.sems = {}
        for e in ENGINES:
            name = "sq_" + e
            self.sems[name] = es.enter_context(nc.semaphore(name))
            self.selfsem[e] = name
        self.dmaq = {"qg": "gpsimd", "qs": "sync", "qa": "scalar"}
        self._es = es
        self.dmastream = {}
        self.dmacount = {}
        self.dmasems = set()
        self.waited = {e: {} for e in ENGINES}
        self.maxwait = {}
        self.last_writer = {}
        self.readers = {}
        self._pending = {}
        self.clock = {e: {} for e in ENGINES}
        self.snap = {}

    def _emit_wait(self, eng, sem, val):
        if self.maxwait.get(sem, 0) < val:
            self.maxwait[sem] = val
        ck = self.clock[eng]
        if ck.get(sem, 0) >= val:
            return
        w = self.waited[eng]
        if w.get(sem, 0) < val:
            w[sem] = val
            self.thunks[eng].append(lambda e, s=self.sems[sem], v=val: e.wait_ge(s, v))
        sn = self.snap.get((sem, val))
        if sn is not None:
            for s2, v2 in sn.items():
                if ck.get(s2, 0) < v2:
                    ck[s2] = v2
        if ck.get(sem, 0) < val:
            ck[sem] = val

    def _dep_waits(self, eng, reads, writes):
        best = {}

        def add(t):
            if t is None:
                return
            sem, val, peng = t
            if peng == "tensor" and eng == "tensor":
                return
            if sem in self.dmasems:
                val = 16 * self.dmacount[sem]
            if best.get(sem, 0) < val:
                best[sem] = val

        for k in reads:
            add(self.last_writer.get(k))
        for k in writes:
            add(self.last_writer.get(k))
            for t in self.readers.get(k, ()):
                add(t)
        for sem, val in best.items():
            self._emit_wait(eng, sem, val)

    def _record(self, tok, reads, writes):
        for k in writes:
            self.last_writer[k] = tok
            self.readers[k] = set()
        for k in reads:
            self.readers.setdefault(k, set()).add(tok)

    def op_multi(self, eng, fns, reads=(), writes=()):
        self._dep_waits(eng, reads, writes)
        self.count[eng] += 1
        sem = self.selfsem[eng]
        h = self.sems[sem]

        def thunk(e, fns=fns, h=h):
            for fn in fns[:-1]:
                fn(e)
            fns[-1](e).then_inc(h, 1)

        self.thunks[eng].append(thunk)
        tok = (sem, self.count[eng], eng)
        sn = dict(self.clock[eng])
        sn[sem] = self.count[eng]
        self.snap[(sem, self.count[eng])] = sn
        self._record(tok, reads, writes)
        return tok

    def op(self, eng, fn, reads=(), writes=(), light=False):
        self._dep_waits(eng, reads, writes)
        sem = self.selfsem[eng]
        if light:
            raise NotImplementedError
        self.count[eng] += 1
        h = self.sems[sem]

        def thunk(e, fn=fn, h=h):
            fn(e).then_inc(h, 1)

        self.thunks[eng].append(thunk)
        tok = (sem, self.count[eng], eng)
        sn = dict(self.clock[eng])
        sn[sem] = self.count[eng]
        self.snap[(sem, self.count[eng])] = sn
        self._record(tok, reads, writes)
        return tok

    def dma(self, q, fn, reads=(), writes=(), stream=None):
        eng = self.dmaq[q]
        sname = stream if stream is not None else q
        sem = self.dmastream.get(sname)
        if sem is None:
            sem = "dma_" + sname
            self.sems[sem] = self._es.enter_context(self.nc.semaphore(sem))
            self.dmastream[sname] = sem
            self.dmacount[sem] = 0
            self.dmasems.add(sem)
        mw = self.maxwait.get(sem, 0)
        if mw > 0:
            self._emit_wait(eng, sem, mw)
        self._dep_waits(eng, reads, writes)
        self.dmacount[sem] += 1
        h = self.sems[sem]

        def thunk(e, fn=fn, h=h):
            fn(e).then_inc(h, 16)

        self.thunks[eng].append(thunk)
        tok = (sem, 16 * self.dmacount[sem], eng)
        self.snap[(sem, 16 * self.dmacount[sem])] = dict(self.clock[eng])
        self._record(tok, reads, writes)
        return tok

    def barrier(self):
        targets = [(self.selfsem[e], self.count[e]) for e in ENGINES]
        targets += [(sem, 16 * cnt) for sem, cnt in self.dmacount.items()]
        for e in ENGINES:
            for sem, val in targets:
                if val > 0:
                    self._emit_wait(e, sem, val)


class T:
    __slots__ = ("h", "key")

    def __init__(self, h, key):
        self.h = h
        self.key = key


_uid = [0]


def build_program(cfg):
    S = cfg["S"]; NDC = cfg["NDC"]; NTC = cfg["NTC"]; NSC = cfg["NSC"]
    H = cfg["H"]; E = cfg["E"]; F = cfg["F"]; D = cfg["D"]
    CAP = cfg["CAP"]; NSEQ = cfg["nseq"]; NSUB = cfg["n_sub"]
    NFI = F // 128
    CAPT = CAP // 128

    nc = bass.Bass("TRN2", target_bir_lowering=False)
    root = ExitStack()
    em = Em(nc, root)

    def din(name, shape):
        return T(nc.dram_tensor(name, shape, F32, kind="ExternalInput"), "d_" + name)

    x_in = din("x", [NSEQ, S, D])
    aa_wq = din("aa_wq", [D, D]); aa_wk = din("aa_wk", [D, D])
    aa_wv = din("aa_wv", [D, D]); aa_wo = din("aa_wo", [D, D])
    aa_gam = din("aa_gam", [H])
    ab_ln1s = din("ab_ln1s", [D]); ab_ln1b = din("ab_ln1b", [D])
    ab_wq = din("ab_wq", [D, D]); ab_wk = din("ab_wk", [D, D]); ab_wv = din("ab_wv", [D, D])
    ab_bq = din("ab_bq", [D]); ab_bk = din("ab_bk", [D]); ab_bv = din("ab_bv", [D])
    ab_wo = din("ab_wo", [D, D]); ab_bo = din("ab_bo", [D])
    ab_ln2s = din("ab_ln2s", [D]); ab_ln2b = din("ab_ln2b", [D])
    ab_w1 = din("ab_w1", [D, F]); ab_b1 = din("ab_b1", [F])
    ab_w2 = din("ab_w2", [F, D]); ab_b2 = din("ab_b2", [D])
    f_wr = din("f_wr", [D, E]); f_br = din("f_br", [E])
    f_w1 = din("f_w1", [E, D, F]); f_b1 = din("f_b1", [E, F])
    f_w2 = din("f_w2", [E, F, D]); f_b2 = din("f_b2", [E, D])

    y_out = T(nc.dram_tensor("y", [NSEQ, S, D], F32, kind="ExternalOutput"), "d_y")

    xn_dram = T(nc.dram_tensor("xn_scr", [S + 128, D], BF16, kind="Internal"), "d_xnscr")
    yf_dram = T(nc.dram_tensor("yf_scr", [S + 128, D], F32, kind="Internal"), "d_yfscr")
    eid_dram = T(nc.dram_tensor("eid_scr", [NTC, 128], F32, kind="Internal"), "d_eidscr")
    idx_dram = T(nc.dram_tensor("idx_scr", [16384, 1], I32, kind="Internal"), "d_idxscr")
    wt_dram = T(nc.dram_tensor("wt_scr", [4, D, D], BF16, kind="Internal"), "d_wtscr")
    w1b_dram = T(nc.dram_tensor("w1b_scr", [E, D, F], BF16, kind="Internal"), "d_w1bscr")
    w2b_dram = T(nc.dram_tensor("w2b_scr", [E, F, D], BF16, kind="Internal"), "d_w2bscr")
    abw_dram = T(nc.dram_tensor("abw_scr", [4, D, D], BF16, kind="Internal"), "d_abwscr")
    abw1b = T(nc.dram_tensor("abw1b_scr", [D, F], BF16, kind="Internal"), "d_abw1b")
    b1b_dram = T(nc.dram_tensor("b1b_scr", [E, F], BF16, kind="Internal"), "d_b1b")
    abb1b_dram = T(nc.dram_tensor("abb1b_scr", [1, F], BF16, kind="Internal"), "d_abb1b")
    abw2b = T(nc.dram_tensor("abw2b_scr", [F, D], BF16, kind="Internal"), "d_abw2b")
    xsp_dram = T(nc.dram_tensor("xsp_scr", [NDC, 128, S], F32, kind="Internal"), "d_xspscr")

    arena = {"lo": None, "hi": None}

    def mk_sb(stack, name, shape, dt):
        _uid[0] += 1
        nm = f"{name}_{_uid[0]}"
        return T(stack.enter_context(nc.sbuf_tensor(nm, shape, dt)), nm)

    def _psize(shape, dt):
        n = 1
        for d in shape[1:]:
            n *= d
        sz = n * mybir.dt.size(dt)
        return (sz + 31) & ~31

    class Scope:
        def __init__(self, side="right"):
            self.side = side
            self.saved = None

        def _ensure(self):
            if self.saved is None:
                self.saved = arena[{"right": "hi", "left": "lo"}[self.side]]

        def sb(self, name, shape, dt):
            self._ensure()
            _uid[0] += 1
            nm = f"{name}_{_uid[0]}"
            sz = _psize(shape, dt)
            if self.side == "right":
                arena["hi"] -= sz
                off = arena["hi"]
            else:
                off = arena["lo"]
                arena["lo"] += sz
            assert arena["lo"] <= arena["hi"], \
                f"arena overflow at {nm}: lo={arena['lo']} hi={arena['hi']}"
            h = nc.alloc_sbuf_tensor_at(nm, shape, dt, offset=off)
            return T(h, nm)

        def close(self):
            em.barrier()
            self._ensure()
            if self.side == "right":
                arena["hi"] = self.saved
            else:
                arena["lo"] = self.saved

    def sbL(name, shape, dt):
        return mk_sb(root, name, shape, dt)

    P = [T(root.enter_context(nc.psum_tensor(f"P{i}", [128, 512], F32)), f"P{i}")
         for i in range(7)]
    PTB = T(root.enter_context(nc.psum_tensor("PTB", [128, 1024], BF16)), "PTB")

    ident_b = sbL("ident_b", [128, 128], BF16)
    ident_f = sbL("ident_f", [128, 128], F32)
    ones_row_f = sbL("ones_row_f", [1, 512], F32)
    ones_row_b = sbL("ones_row_b", [1, 512], BF16)
    ones_col_f = sbL("ones_col_f", [128, 1], F32)
    ones_col_b = sbL("ones_col_b", [128, 1], BF16)
    zrow_b = sbL("zrow_b", [1, 512], BF16)
    iota_tok = sbL("iota_tok", [128, NTC], I32)
    ecol = sbL("ecol", [E, 1], F32)
    brcol = sbL("brcol", [E, 1], F32)
    wrb = [sbL(f"wrb{j}", [128, E], F32) for j in range(NDC)]

    dmask = [sbL(f"dmask{h}", [128, 128], BF16) for h in range(H)]
    gqrow = [sbL(f"gqrow{h}", [128, 128], BF16) for h in range(H)]
    gkcol = sbL("gkcol", [128, H], F32)
    gcbc = sbL("gcbc", [64, H], F32)
    lngam = sbL("lngam", [1, H], F32)
    gamrow = sbL("gamrow", [1, H], F32)
    dtile = sbL("dtile", [128, 128], F32)
    riota_row = sbL("riota_row", [1, 128], F32)
    riota_col = sbL("riota_col", [128, 1], F32)
    lgcol = sbL("lgcol", [128, 1], F32)
    ln8col = sbL("ln8col", [128, 1], F32)
    eps5col = sbL("eps5col", [1, 1], F32)
    eps6col = sbL("eps6col", [1, 1], F32)
    gamsb = sbL("gamsb", [1, 16], F32)

    XN = [sbL(f"XN{c}", [128, 512], BF16) for c in range(NDC)]
    Sst = [sbL(f"Sst{h}", [64, 64], F32) for h in range(H)]
    Sbf = [sbL(f"Sbf{h}", [64, 64], BF16) for h in range(H)]
    sqt = [sbL(f"sqt{i}", [128, 512], F32) for i in range(2)]
    arow = sbL("arow", [1, 512], F32)
    brow = sbL("brow", [1, 512], F32)
    r512a = sbL("r512a", [1, 512], F32)
    r512b = sbL("r512b", [1, 512], F32)
    m2row = sbL("m2row", [1, 512], F32)
    rows2 = sbL("rows2", [2, 512], F32)
    lnps = sbL("lnps", [1, D], F32)
    lnpb = sbL("lnpb", [1, D], F32)
    lnsc = sbL("lnsc", [128, NDC], F32)
    lnsb = sbL("lnsb", [128, NDC], F32)
    xtmp = sbL("xtmp", [128, 512], F32)
    xtmp2 = sbL("xtmp2", [128, 512], F32)
    io_tm = sbL("io_tm", [128, D], F32)

    _persist_end = 0
    for _t in ([ident_b, ident_f, ones_row_f, ones_row_b, ones_col_f, ones_col_b,
                iota_tok, ecol, brcol, gkcol, gcbc, lngam, gamrow, dtile,
                riota_row, riota_col, lgcol, ln8col, eps5col, eps6col, gamsb,
                arow, brow, r512a, r512b, m2row, rows2, lnps, lnpb, lnsc, lnsb, xtmp, xtmp2, io_tm]
               + wrb + dmask + gqrow + XN + sqt):
        ml = nc.lookup_mloc(_t.h)
        _persist_end = max(_persist_end, ml.addr + _psize(list(_t.h.shape), _t.h.dtype))
    arena["lo"] = _persist_end
    arena["hi"] = 229344

    grpbuf = {"fns": [], "reads": set()}
    FUSE_GROUPS = False

    def mm(pt, prange, lhsT, lr, rhs, rr, reads, start, stop, fuse=False):
        fuse = fuse and FUSE_GROUPS
        fn = (lambda e, pt=pt, prange=prange, lhsT=lhsT, lr=lr, rhs=rhs, rr=rr,
              start=start, stop=stop: e.matmul(pt.h[prange], lhsT[lr], rhs[rr],
                                               start=start, stop=stop,
                                               skip_group_check=True))
        if fuse and not stop:
            grpbuf["fns"].append(fn)
            grpbuf["reads"].update(reads)
            return None
        if fuse:
            fns = grpbuf["fns"] + [fn]
            rds = tuple(grpbuf["reads"] | set(reads))
            grpbuf["fns"] = []
            grpbuf["reads"] = set()
            return em.op_multi("tensor", fns, reads=rds, writes=(pt.key,))
        return em.op("tensor", fn, reads=reads, writes=(pt.key,))

    def cp(eng, dst, dsl, src, ssl, reads, writes):
        if eng == "scalar":
            return em.op(eng, lambda e: e.copy(dst[dsl], src[ssl]), reads=reads, writes=writes)
        return em.op(eng, lambda e: e.tensor_copy(dst[dsl], src[ssl]), reads=reads, writes=writes)

    rot = [0]

    def transpose_pe(dstT, dsl, srcT, ssl, bf16=True, n=128):
        if bf16:
            pt = PTB
        else:
            pt = P[rot[0] % 2]
            rot[0] += 1
        idn = ident_b if bf16 else ident_f
        em.op("tensor", lambda e: e.transpose(pt.h[0:128, 0:n], srcT.h[ssl], idn.h[0:n, 0:n]),
              reads=(srcT.key, idn.key), writes=(pt.key,))
        cp("vector", dstT.h, dsl, pt.h, SL[0:128, 0:n], (pt.key,), (dstT.key,))

    def load_row_f32(dst, n, src, row=0):
        em.dma("qs", lambda e: e.dma_start(dst.h[row:row + 1, 0:n],
                                           src.h[:].rearrange("(a b) -> a b", a=1)),
               reads=(src.key,), writes=(dst.key,), stream="rows")

    def load_row_bf16(dst, n, src):
        em.dma("qg", lambda e: e.dma_start(dst.h[0:1, 0:n],
                                           src.h[:].rearrange("(a b) -> a b", a=1)),
               reads=(src.key,), writes=(dst.key,), stream="rows_b")

    em.op("gpsimd", lambda e: e.memset(ident_b.h[:, :], 1.0), writes=(ident_b.key,))
    em.op("gpsimd", lambda e: e.affine_select(ident_b.h[:, :], ident_b.h[:, :], [[-1, 128]],
                                              ALU.is_equal, 0.0, base=0, channel_multiplier=1),
          reads=(ident_b.key,), writes=(ident_b.key,))
    em.op("gpsimd", lambda e: e.memset(ident_f.h[:, :], 1.0), writes=(ident_f.key,))
    em.op("gpsimd", lambda e: e.affine_select(ident_f.h[:, :], ident_f.h[:, :], [[-1, 128]],
                                              ALU.is_equal, 0.0, base=0, channel_multiplier=1),
          reads=(ident_f.key,), writes=(ident_f.key,))
    em.op("gpsimd", lambda e: e.memset(ones_row_f.h[:, :], 1.0), writes=(ones_row_f.key,))
    em.op("gpsimd", lambda e: e.memset(ones_row_b.h[:, :], 1.0), writes=(ones_row_b.key,))
    em.op("gpsimd", lambda e: e.memset(ones_col_f.h[:, :], 1.0), writes=(ones_col_f.key,))
    em.op("gpsimd", lambda e: e.memset(ones_col_b.h[:, :], 1.0), writes=(ones_col_b.key,))
    em.op("gpsimd", lambda e: e.memset(zrow_b.h[:, :], 0.0), writes=(zrow_b.key,))
    em.op("gpsimd", lambda e: e.memset(rows2.h[0:2, :], 1.0), writes=(rows2.key,))
    em.op("gpsimd", lambda e: e.memset(ln8col.h[:, :], float(np.log(0.125))),
          writes=(ln8col.key,))
    em.op("gpsimd", lambda e: e.memset(eps5col.h[:, :], 1e-5), writes=(eps5col.key,))
    em.op("gpsimd", lambda e: e.memset(eps6col.h[:, :], 1e-6), writes=(eps6col.key,))
    em.op("gpsimd", lambda e: e.iota(iota_tok.h[:, :], [[128, NTC]], base=0,
                                     channel_multiplier=1), writes=(iota_tok.key,))
    em.op("gpsimd", lambda e: e.iota(ecol.h[:, :], [[0, 1]], base=0, channel_multiplier=1,
                                     allow_small_or_imprecise_dtypes=True), writes=(ecol.key,))
    em.op("gpsimd", lambda e: e.iota(dtile.h[:, :], [[1, 128]], base=0, channel_multiplier=-1,
                                     allow_small_or_imprecise_dtypes=True), writes=(dtile.key,))
    em.op("gpsimd", lambda e: e.iota(riota_row.h[:, :], [[1, 128]], base=1, channel_multiplier=0,
                                     allow_small_or_imprecise_dtypes=True),
          writes=(riota_row.key,))
    em.op("gpsimd", lambda e: e.iota(riota_col.h[:, :], [[0, 1]], base=127, channel_multiplier=-1,
                                     allow_small_or_imprecise_dtypes=True),
          writes=(riota_col.key,))

    em.dma("qs", lambda e: e.dma_start(gamsb.h[0:1, 0:16],
                                       aa_gam.h[:].rearrange("(a b) -> a b", a=1)),
           reads=(aa_gam.key,), writes=(gamsb.key,))
    em.dma("qs", lambda e: e.dma_start(brcol.h[0:E, 0:1],
                                       f_br.h[:].rearrange("(a b) -> a b", b=1)),
           reads=(f_br.key,), writes=(brcol.key,))
    for j in range(NDC):
        em.dma("qs", lambda e, j=j: e.dma_start(wrb[j].h[:, :], f_wr.h[j * 128:(j + 1) * 128, :]),
               reads=(f_wr.key,), writes=(wrb[j].key,), stream="rows")

    em.op("scalar", lambda e: e.activation(lngam.h[:, :], gamsb.h[0:1, 0:16], AF.Ln),
          reads=(gamsb.key,), writes=(lngam.key,))
    em.op("scalar", lambda e: e.activation(gamrow.h[:, :], lngam.h[:, :], AF.Exp, scale=128.0),
          reads=(lngam.key,), writes=(gamrow.key,))
    mm(P[5], SL[0:64, 0:16], ones_row_f.h, SL[0:1, 0:64], gamrow.h, SL[0:1, 0:16],
       (ones_row_f.key, gamrow.key), True, True)
    cp("vector", gcbc.h, SL[:, :], P[5].h, SL[0:64, 0:16], (P[5].key,), (gcbc.key,))
    LN8 = float(np.log(0.125))
    for h in range(H):
        mm(P[5], SL[0:128, 0:1], ones_row_f.h, SL[0:1, 0:128], lngam.h, SL[0:1, h:h + 1],
           (ones_row_f.key, lngam.key), True, True)
        cp("vector", lgcol.h, SL[:, :], P[5].h, SL[0:128, 0:1], (P[5].key,), (lgcol.key,))
        em.op("scalar", lambda e, h=h: e.activation(dmask[h].h[:, :], dtile.h[:, :], AF.Exp,
                                                    bias=ln8col.h[:, 0:1],
                                                    scale=lgcol.h[:, 0:1]),
              reads=(dtile.key, lgcol.key, ln8col.key), writes=(dmask[h].key,))
        em.op("gpsimd", lambda e, h=h: e.affine_select(dmask[h].h[:, :], dmask[h].h[:, :],
                                                       [[1, 128]], ALU.is_ge, 0.0, base=0,
                                                       channel_multiplier=-1),
              reads=(dmask[h].key,), writes=(dmask[h].key,))
        em.op("scalar", lambda e, h=h: e.activation(r512a.h[0:1, 0:128], riota_row.h[:, :],
                                                    AF.Exp, bias=ln8col.h[0:1, 0:1],
                                                    scale=lngam.h[0:1, h:h + 1]),
              reads=(riota_row.key, lngam.key, ln8col.key), writes=(r512a.key,))
        mm(P[5], SL[0:128, 0:128], ones_row_f.h, SL[0:1, 0:128], r512a.h, SL[0:1, 0:128],
           (ones_row_f.key, r512a.key), True, True)
        cp("vector", gqrow[h].h, SL[:, :], P[5].h, SL[0:128, 0:128], (P[5].key,),
           (gqrow[h].key,))
        em.op("scalar", lambda e, h=h: e.activation(gkcol.h[:, h:h + 1], riota_col.h[:, :],
                                                    AF.Exp, scale=lgcol.h[:, 0:1]),
              reads=(riota_col.key, lgcol.key), writes=(gkcol.key,))

    tsc = Scope("right")
    wtmp = [tsc.sb(f"wtmp{j}", [128, D], BF16) for j in range(NDC)]
    wTt = [tsc.sb(f"wTt{j}", [128, D], BF16) for j in range(NDC)]
    for wi, wsrc in enumerate((aa_wq, aa_wk, aa_wv, aa_wo)):
        for j in range(NDC):
            em.dma("qg", lambda e, j=j, wsrc=wsrc: e.dma_start(
                wtmp[j].h[:, :], wsrc.h[j * 128:(j + 1) * 128, :]),
                   reads=(wsrc.key,), writes=(wtmp[j].key,), stream=f"wt{j % 2}")
        for i in range(NDC):
            for j in range(NDC):
                transpose_pe(wTt[j], SL[:, i * 128:(i + 1) * 128],
                             wtmp[i], SL[:, j * 128:(j + 1) * 128])
        for j in range(NDC):
            em.dma("qs", lambda e, wi=wi, j=j: e.dma_start(
                wt_dram.h[wi, j * 128:(j + 1) * 128, :], wTt[j].h[:, :]),
                   reads=(wTt[j].key,), writes=(wt_dram.key,), stream=f"wtd{j % 2}")
    tsc.close()

    def issue_w1_conversions():
        for e_i in range(E):
            for r0 in range(0, D, 256):
                em.dma("qg", lambda e, e_i=e_i, r0=r0: e.dma_start(
                    w1b_dram.h[e_i, r0:r0 + 256, :], f_w1.h[e_i, r0:r0 + 256, :]),
                       reads=(f_w1.key,), writes=(w1b_dram.key,), stream="cvd")
        em.dma("qg", lambda e: e.dma_start(b1b_dram.h[:, :], f_b1.h[:, :]),
               reads=(f_b1.key,), writes=(b1b_dram.key,), stream="cvd")
        em.dma("qg", lambda e: e.dma_start(
            abb1b_dram.h[:, :], ab_b1.h[0:F].rearrange("(a b) -> a b", a=1)),
               reads=(ab_b1.key,), writes=(abb1b_dram.key,), stream="cvd")

    def gen_w2():
        for e_i in range(E):
            for r0 in range(0, F, 1024):
                em.dma("qg", lambda e, e_i=e_i, r0=r0: e.dma_start(
                    w2b_dram.h[e_i, r0:r0 + 1024, :], f_w2.h[e_i, r0:r0 + 1024, :]),
                       reads=(f_w2.key,), writes=(w2b_dram.key,), stream="cvd")
                yield

    def gen_ab():
        for wi, wsrc in enumerate((ab_wq, ab_wk, ab_wv, ab_wo)):
            for r0 in range(0, D, 512):
                em.dma("qg", lambda e, wi=wi, r0=r0, wsrc=wsrc: e.dma_start(
                    abw_dram.h[wi, r0:r0 + 512, :], wsrc.h[r0:r0 + 512, :]),
                       reads=(wsrc.key,), writes=(abw_dram.key,), stream="cvd")
                yield
        for r0 in range(0, D, 256):
            em.dma("qg", lambda e, r0=r0: e.dma_start(
                abw1b.h[r0:r0 + 256, :], ab_w1.h[r0:r0 + 256, :]),
                   reads=(ab_w1.key,), writes=(abw1b.key,), stream="cvd")
            yield
        for r0 in range(0, F, 1024):
            em.dma("qg", lambda e, r0=r0: e.dma_start(
                abw2b.h[r0:r0 + 1024, :], ab_w2.h[r0:r0 + 1024, :]),
                   reads=(ab_w2.key,), writes=(abw2b.key,), stream="cvd")
            yield

    def layernorm(src_list, sc, eps, dst_list=None, scaled=False, stats_banks=None):
        pa, pb = stats_banks if stats_banks is not None else (P[3], P[4])
        t0 = sc * 512
        csl = SL[:, t0:t0 + 512]
        for c in range(NDC):
            sq = sqt[c % 2]
            em.op("scalar", lambda e, c=c, sq=sq, csl=csl: e.activation(
                sq.h[:, :], src_list[c].h[csl], AF.Square),
                  reads=(src_list[c].key,), writes=(sq.key,))
            mm(pa, SL[0:1, 0:512], ones_col_f.h, SL[0:128, 0:1], src_list[c].h, csl,
               (ones_col_f.key, src_list[c].key), c == 0, c == NDC - 1)
            mm(pb, SL[0:1, 0:512], ones_col_f.h, SL[0:128, 0:1], sq.h, SL[:, :],
               (ones_col_f.key, sq.key), c == 0, c == NDC - 1)
        em.op("vector", lambda e, pa=pa: e.tensor_scalar_mul(
            r512a.h[:, :], pa.h[0:1, 0:512], 1.0 / D),
              reads=(pa.key,), writes=(r512a.key,))
        em.op("vector", lambda e, pb=pb: e.tensor_scalar_mul(
            r512b.h[:, :], pb.h[0:1, 0:512], 1.0 / D),
              reads=(pb.key,), writes=(r512b.key,))
        em.op("scalar", lambda e: e.activation(m2row.h[:, :], r512a.h[:, :], AF.Square),
              reads=(r512a.key,), writes=(m2row.key,))
        em.op("vector", lambda e: e.tensor_sub(r512b.h[:, :], r512b.h[:, :], m2row.h[:, :]),
              reads=(r512b.key, m2row.key), writes=(r512b.key,))
        ecl = eps5col if eps == 1e-5 else eps6col
        em.op("scalar", lambda e, ecl=ecl: e.activation(r512b.h[:, :], r512b.h[:, :], AF.Sqrt,
                                                        bias=ecl.h[0:1, 0:1]),
              reads=(r512b.key, ecl.key), writes=(r512b.key,))
        em.op("vector", lambda e: e.reciprocal(arow.h[:, :], r512b.h[:, :]),
              reads=(r512b.key,), writes=(arow.key,))
        em.op("vector", lambda e: e.tensor_scalar_mul(r512a.h[:, :], r512a.h[:, :], -1.0),
              reads=(r512a.key,), writes=(r512a.key,))
        em.op("vector", lambda e: e.tensor_mul(brow.h[:, :], r512a.h[:, :], arow.h[:, :]),
              reads=(r512a.key, arow.key), writes=(brow.key,))
        cp("vector", rows2.h, SL[0:1, :], brow.h, SL[0:1, :], (brow.key,), (rows2.key,))
        if dst_list is None:
            return
        mm(P[5], SL[0:128, 0:512], ones_row_f.h, SL[0:1, 0:128],
           arow.h, SL[0:1, :], (ones_row_f.key, arow.key), True, True)
        mm(P[6], SL[0:128, 0:512], ones_row_f.h, SL[0:1, 0:128],
           rows2.h, SL[0:1, 0:512], (ones_row_f.key, rows2.key), True, True)
        for c in range(NDC):
            xt_ = xtmp if c % 2 == 0 else xtmp2
            em.op("vector", lambda e, c=c, csl=csl, xt_=xt_: e.tensor_mul(
                xt_.h[:, :], src_list[c].h[csl], P[5].h[0:128, 0:512]),
                  reads=(src_list[c].key, P[5].key), writes=(xt_.key,))
            if scaled:
                em.op("vector", lambda e, c=c, xt_=xt_: e.tensor_add(
                    xt_.h[:, :], xt_.h[:, :], P[6].h[0:128, 0:512]),
                      reads=(xt_.key, P[6].key), writes=(xt_.key,))
                em.op("vector", lambda e, c=c, xt_=xt_: e.tensor_scalar(
                    dst_list[c].h[:, 0:512], xt_.h[:, :],
                    lnsc.h[:, c:c + 1], lnsb.h[:, c:c + 1], ALU.mult, ALU.add),
                      reads=(xt_.key, lnsc.key, lnsb.key), writes=(dst_list[c].key,))
            else:
                em.op("vector", lambda e, c=c, xt_=xt_: e.tensor_add(
                    dst_list[c].h[:, 0:512], xt_.h[:, :], P[6].h[0:128, 0:512]),
                      reads=(xt_.key, P[6].key), writes=(dst_list[c].key,))

    def attn_a(X, pump=None):
        a = Scope("right")
        wsA = [a.sb(f"wsA{j}", [128, D], BF16) for j in range(NDC)]
        wsB = [a.sb(f"wsB{j}", [128, D], BF16) for j in range(NDC)]
        qfm = [a.sb(f"qfm{c}", [128, 512], BF16) for c in range(NDC)]
        kfm = [a.sb(f"kfm{c}", [128, 512], BF16) for c in range(NDC)]
        ktm = [a.sb(f"ktm{t}", [128, D], BF16) for t in range(4)]
        vtm = [a.sb(f"vtm{t}", [128, D], BF16) for t in range(4)]
        retf = [a.sb(f"retf{c}", [128, 512], BF16) for c in range(NDC)]
        wintra = [a.sb(f"wintra{i}", [128, 128], BF16) for i in range(2)]
        qhat = [a.sb(f"qhat{i}", [64, 128], BF16) for i in range(2)]
        khat = [a.sb(f"khat{i}", [128, 64], BF16) for i in range(2)]


        def loadw(slot, wi, sname):
            for j in range(NDC):
                em.dma("qs", lambda e, j=j, wi=wi, slot=slot: e.dma_start(
                    slot[j].h[:, :], wt_dram.h[wi, j * 128:(j + 1) * 128, :]),
                       reads=(wt_dram.key,), writes=(slot[j].key,),
                       stream=sname + str(j % 2))

        def pp():
            if pump is not None:
                pump()

        for sc in range(NSC):
            t0 = sc * 512
            layernorm(X, sc, 1e-5, dst_list=XN)
            pp()
            loadw(wsA, 0, "wsa")
            loadw(wsB, 1, "wsb")
            for (ws, dstf) in ((wsA, qfm), (wsB, kfm)):
                for i in range(NDC):
                    pb = P[i % 4]
                    for j in range(NDC):
                        mm(pb, SL[0:128, 0:512], ws[j].h, SL[:, i * 128:(i + 1) * 128],
                           XN[j].h, SL[:, 0:512], (ws[j].key, XN[j].key),
                           j == 0, j == NDC - 1, fuse=True)
                    cp("vector", dstf[i].h, SL[:, :], pb.h, SL[0:128, 0:512],
                       (pb.key,), (dstf[i].key,))
                    pp()
            loadw(wsA, 2, "wsa")
            for (ws, dstt) in ((wsB, ktm), (wsA, vtm)):
                for tt in range(4):
                    for nh in range(2):
                        pb = P[(tt * 2 + nh) % 4]
                        for j in range(NDC):
                            mm(pb, SL[0:128, 0:512],
                               XN[j].h, SL[:, tt * 128:(tt + 1) * 128],
                               ws[j].h, SL[:, nh * 512:(nh + 1) * 512],
                               (XN[j].key, ws[j].key), j == 0, j == NDC - 1, fuse=True)
                        cp("vector", dstt[tt].h, SL[:, nh * 512:(nh + 1) * 512],
                           pb.h, SL[0:128, 0:512], (pb.key,), (dstt[tt].key,))
                        pp()
            loadw(wsB, 3, "wsb")
            for tt in range(4):
                c = sc * 4 + tt
                for h in range(H):
                    hc, ho = h // 2, (h % 2) * 64
                    par = h % 2
                    wi_t, qh_t, kh_t = wintra[par], qhat[par], khat[par]
                    sbk = P[[2, 5, 0][h % 3]]
                    mm(sbk, SL[0:128, 0:128], kfm[hc].h, SL[ho:ho + 64, tt * 128:(tt + 1) * 128],
                       qfm[hc].h, SL[ho:ho + 64, tt * 128:(tt + 1) * 128],
                       (kfm[hc].key, qfm[hc].key), True, True)
                    em.op("vector", lambda e, sbk=sbk, h=h, wi_t=wi_t: e.tensor_mul(
                        wi_t.h[:, :], sbk.h[0:128, 0:128], dmask[h].h[:, :]),
                          reads=(sbk.key, dmask[h].key), writes=(wi_t.key,))
                    rbk = P[[3, 6, 1][h % 3]]
                    mm(rbk, SL[0:64, 0:128], vtm[tt].h, SL[:, h * 64:(h + 1) * 64],
                       wi_t.h, SL[:, :], (vtm[tt].key, wi_t.key), True, c == 0, fuse=True)
                    if c > 0:
                        em.op("vector", lambda e, h=h, hc=hc, ho=ho, tt=tt, qh_t=qh_t: e.tensor_mul(
                            qh_t.h[:, :], qfm[hc].h[ho:ho + 64, tt * 128:(tt + 1) * 128],
                            gqrow[h].h[ho:ho + 64, :]),
                              reads=(qfm[hc].key, gqrow[h].key), writes=(qh_t.key,))
                        mm(rbk, SL[0:64, 0:128], Sbf[h].h, SL[:, :], qh_t.h, SL[:, :],
                           (Sbf[h].key, qh_t.key), False, True, fuse=True)
                    cp("scalar", retf[hc].h, SL[ho:ho + 64, tt * 128:(tt + 1) * 128],
                       rbk.h, SL[0:64, 0:128], (rbk.key,), (retf[hc].key,))
                    em.op("scalar", lambda e, h=h, tt=tt, kh_t=kh_t: e.activation(
                        kh_t.h[:, :], ktm[tt].h[:, h * 64:(h + 1) * 64],
                        AF.Copy, scale=gkcol.h[:, h:h + 1]),
                          reads=(ktm[tt].key, gkcol.key), writes=(kh_t.key,))
                    ubk = P[4]
                    mm(ubk, SL[0:64, 0:64], kh_t.h, SL[:, :], vtm[tt].h,
                       SL[:, h * 64:(h + 1) * 64], (kh_t.key, vtm[tt].key), True, True)
                    if c == 0:
                        cp("vector", Sst[h].h, SL[:, :], ubk.h, SL[0:64, 0:64],
                           (ubk.key,), (Sst[h].key,))
                    else:
                        em.op("vector", lambda e, h=h, ubk=ubk: e.scalar_tensor_tensor(
                            Sst[h].h[:, :], Sst[h].h[:, :], gcbc.h[:, h:h + 1],
                            ubk.h[0:64, 0:64], ALU.mult, ALU.add),
                              reads=(Sst[h].key, gcbc.key, ubk.key), writes=(Sst[h].key,))
                    if c < NTC - 1:
                        cp("scalar", Sbf[h].h, SL[:, :], Sst[h].h, SL[:, :],
                           (Sst[h].key,), (Sbf[h].key,))
            for dc in range(NDC):
                pb = P[dc % 4]
                for j in range(NDC):
                    mm(pb, SL[0:128, 0:512], wsB[j].h, SL[:, dc * 128:(dc + 1) * 128],
                       retf[j].h, SL[:, :], (wsB[j].key, retf[j].key), j == 0, j == NDC - 1,
                       fuse=True)
                em.op("vector", lambda e, dc=dc, pb=pb, t0=t0, X=X: e.tensor_add(
                    X[dc].h[:, t0:t0 + 512], X[dc].h[:, t0:t0 + 512], pb.h[0:128, 0:512]),
                      reads=(X[dc].key, pb.key), writes=(X[dc].key,))
                pp()
            for _ in range(8):
                pp()
        a.close()

    def ffn(X, pump=None, final=False):
        r = Scope("right")
        sc_s1 = r.sb("sc_s1", [E, S], F32)
        mask8 = r.sb("mask8", [E, S], F32)
        rank8 = r.sb("rank8", [E, S], F32)
        zero8 = r.sb("zero8", [E, S], F32)
        tsc8 = r.sb("tsc8", [128, 8], F32)
        tmax = r.sb("tmax", [128, 8], F32)
        tidx = r.sb("tidx", [128, 8], U32)
        eidc = r.sb("eidc", [128, NTC], F32)
        eidrow = r.sb("eidrow", [1, S], F32)
        wsum_c = r.sb("wsum_c", [E, 1], F32)
        rsel = r.sb("rsel", [128, 1], F32)
        slotf = r.sb("slotf", [128, 1], F32)
        sloti = r.sb("sloti", [128, 1], I32)
        selc = r.sb("selc", [128, 1], F32)
        senti = r.sb("senti", [128, 32], I32)
        eid16 = r.sb("eid16", [16, 128], F32)
        stage = [r.sb(f"stage{i}", [128, D], BF16) for i in range(2)]

        em.op("gpsimd", lambda e: e.memset(zero8.h[:, :], 0.0), writes=(zero8.key,))
        em.op("gpsimd", lambda e: e.memset(stage[0].h[:, :], 0.0), writes=(stage[0].key,))
        em.dma("qs", lambda e: e.dma_start(xn_dram.h[S:S + 128, :], stage[0].h[:, :]),
               reads=(stage[0].key,), writes=(xn_dram.key,), stream="xnw0")
        em.op("gpsimd", lambda e: e.memset(senti.h[:, :], S), writes=(senti.key,))
        for sc in range(NSC):
            t0 = sc * 512
            layernorm(X, sc, 1e-5, dst_list=XN)
            for tt in range(4):
                tc = sc * 4 + tt
                stg = stage[tt % 2]
                for dc in range(NDC):
                    em.op("tensor", lambda e, dc=dc, tt=tt: e.transpose(
                        PTB.h[0:128, dc * 128:(dc + 1) * 128],
                        XN[dc].h[:, tt * 128:(tt + 1) * 128], ident_b.h[:, :]),
                          reads=(XN[dc].key, ident_b.key), writes=(PTB.key,))
                cp("vector", stg.h, SL[:, :], PTB.h, SL[0:128, 0:1024],
                   (PTB.key,), (stg.key,))
                em.dma("qs", lambda e, tc=tc, stg=stg: e.dma_start(
                    xn_dram.h[tc * 128:(tc + 1) * 128, :], stg.h[:, :]),
                       reads=(stg.key,), writes=(xn_dram.key,),
                       stream=f"xnw{tt % 2}")
            for j in range(NDC):
                mm(P[2], SL[0:E, 0:512], wrb[j].h, SL[:, :], X[j].h, SL[:, t0:t0 + 512],
                   (wrb[j].key, X[j].key), j == 0, j == NDC - 1, fuse=True)
            cp("vector", sc_s1.h, SL[0:E, t0:t0 + 512], P[2].h, SL[0:E, 0:512],
               (P[2].key,), (sc_s1.key,))
            if sc == 0:
                for j in range(NDC):
                    mm(P[2], SL[0:E, 0:1], wrb[j].h, SL[:, :], ones_col_f.h, SL[:, 0:1],
                       (wrb[j].key, ones_col_f.key), j == 0, j == NDC - 1)
                cp("vector", wsum_c.h, SL[:, :], P[2].h, SL[0:E, 0:1],
                   (P[2].key,), (wsum_c.key,))
            mm(P[5], SL[0:E, 0:512], ones_row_f.h, SL[0:1, 0:E], arow.h, SL[0:1, :],
               (ones_row_f.key, arow.key), True, True)
            mm(P[6], SL[0:E, 0:512], ones_row_f.h, SL[0:1, 0:E], brow.h, SL[0:1, :],
               (ones_row_f.key, brow.key), True, True)
            em.op("vector", lambda e, t0=t0: e.tensor_mul(
                sc_s1.h[0:E, t0:t0 + 512], sc_s1.h[0:E, t0:t0 + 512], P[5].h[0:E, 0:512]),
                  reads=(sc_s1.key, P[5].key), writes=(sc_s1.key,))
            em.op("vector", lambda e: e.tensor_scalar(
                xtmp.h[0:E, 0:512], P[6].h[0:E, 0:512], wsum_c.h[:, 0:1],
                brcol.h[:, 0:1], ALU.mult, ALU.add),
                  reads=(P[6].key, wsum_c.key, brcol.key), writes=(xtmp.key,))
            em.op("vector", lambda e, t0=t0: e.tensor_add(
                sc_s1.h[0:E, t0:t0 + 512], sc_s1.h[0:E, t0:t0 + 512], xtmp.h[0:E, 0:512]),
                  reads=(sc_s1.key, xtmp.key), writes=(sc_s1.key,))
        for tc in range(NTC):
            transpose_pe(tsc8, SL[0:128, 0:8], sc_s1, SL[0:E, tc * 128:(tc + 1) * 128],
                         bf16=False, n=8)
            em.op("vector", lambda e: e.max(tmax.h[:, :], tsc8.h[:, 0:8]),
                  reads=(tsc8.key,), writes=(tmax.key,))
            em.op("vector", lambda e: e.max_index(tidx.h[:, :], tmax.h[:, :], tsc8.h[:, 0:8]),
                  reads=(tmax.key, tsc8.key), writes=(tidx.key,))
            cp("vector", eidc.h, SL[:, tc:tc + 1], tidx.h, SL[:, 0:1], (tidx.key,), (eidc.key,))
        em.op("tensor", lambda e: e.transpose(P[2].h[0:NTC, 0:128], eidc.h[:, :],
                                              ident_f.h[0:128, 0:128]),
              reads=(eidc.key, ident_f.key), writes=(P[2].key,))
        cp("vector", eid16.h, SL[0:NTC, :], P[2].h, SL[0:NTC, 0:128], (P[2].key,), (eid16.key,))
        em.dma("qs", lambda e: e.dma_start(eid_dram.h[:, :], eid16.h[0:NTC, :]),
               reads=(eid16.key,), writes=(eid_dram.key,))
        em.dma("qs", lambda e: e.dma_start(
            eidrow.h[0:1, 0:S],
            eid_dram.h[:, :].rearrange("a b -> (a b)").rearrange("(a b) -> a b", a=1)),
               reads=(eid_dram.key,), writes=(eidrow.key,))
        for sc in range(NSC):
            bank = P[5] if sc % 2 == 0 else P[6]
            mm(bank, SL[0:E, 0:512], ones_row_f.h, SL[0:1, 0:E],
               eidrow.h, SL[0:1, sc * 512:(sc + 1) * 512], (ones_row_f.key, eidrow.key),
               True, True)
            em.op("vector", lambda e, sc=sc, bank=bank: e.tensor_scalar(
                mask8.h[0:E, sc * 512:(sc + 1) * 512], bank.h[0:E, 0:512],
                ecol.h[:, 0:1], None, ALU.is_equal),
                  reads=(bank.key, ecol.key), writes=(mask8.key,))
        em.op("vector", lambda e: e.tensor_tensor_scan(
            rank8.h[0:E, 0:S], mask8.h[0:E, 0:S], zero8.h[0:E, 0:S], 0.0, ALU.add, ALU.add),
              reads=(mask8.key, zero8.key), writes=(rank8.key,))
        nfull = (E * CAP) // 128
        em.dma("qs", lambda e, nfull=nfull: e.dma_start(
            idx_dram.h[0:E * CAP, :].rearrange("(p a) b -> p (a b)", p=128),
            senti.h[:, 0:nfull]),
               reads=(senti.key,), writes=(idx_dram.key,))
        em.op("vector", lambda e: e.tensor_mul(rank8.h[0:E, 0:S], rank8.h[0:E, 0:S],
                                               mask8.h[0:E, 0:S]),
              reads=(rank8.key, mask8.key), writes=(rank8.key,))
        for tc in range(NTC):
            mm(P[5], SL[0:128, 0:1], rank8.h, SL[0:E, tc * 128:(tc + 1) * 128],
               ones_col_f.h, SL[0:E, 0:1], (rank8.key, ones_col_f.key), True, True)
            cp("vector", rsel.h, SL[:, :], P[5].h, SL[0:128, 0:1], (P[5].key,), (rsel.key,))
            em.op("vector", lambda e, tc=tc: e.tensor_scalar(
                slotf.h[:, :], eidc.h[:, tc:tc + 1], float(CAP), -1.0, ALU.mult, ALU.add),
                  reads=(eidc.key,), writes=(slotf.key,))
            em.op("vector", lambda e: e.tensor_add(slotf.h[:, :], slotf.h[:, :], rsel.h[:, :]),
                  reads=(slotf.key, rsel.key), writes=(slotf.key,))
            em.op("vector", lambda e: e.tensor_scalar(
                selc.h[:, :], rsel.h[:, :], float(CAP), None, ALU.is_gt),
                  reads=(rsel.key,), writes=(selc.key,))
            em.op("vector", lambda e: e.scalar_tensor_tensor(
                slotf.h[:, :], selc.h[:, :], 8192.0, slotf.h[:, :], ALU.mult, ALU.add),
                  reads=(selc.key, slotf.key), writes=(slotf.key,))
            cp("vector", sloti.h, SL[:, :], slotf.h, SL[:, :], (slotf.key,), (sloti.key,))
            em.dma("qg", lambda e, tc=tc: e.indirect_dma_start(
                idx_dram.h[:, :], bass.IndirectOffsetOnAxis(sloti.h[:, :], 0),
                iota_tok.h[:, tc:tc + 1], None),
                   reads=(sloti.key, iota_tok.key), writes=(idx_dram.key,))
        r.close()

        xp = Scope("right")
        xg_tm = [xp.sb(f"xg_tm{i}", [128, D], BF16) for i in range(3)]
        idxc2 = [xp.sb(f"idxc2_{i}", [128, 1], I32) for i in range(3)]
        xg_fm = [xp.sb(f"xg_fm{c}", [128, CAP], BF16) for c in range(NDC)]
        hsb = [xp.sb(f"hsb{i}", [128, CAP], BF16) for i in range(NFI)]
        W1Q = [[xp.sb(f"W1Q{p}_{j}", [128, 1024], BF16) for j in range(NDC)]
               for p in range(2)]
        w2pr = [xp.sb(f"w2pr{i}", [128, 2048], BF16) for i in range(2)]
        b1fmb = xp.sb("b1fmb", [128, 32], BF16)
        b1fm = xp.sb("b1fm", [128, 32], F32)
        b2row = xp.sb("b2row", [1, D], BF16)
        ysc = xp.sb("ysc", [128, D], F32)
        ysc2 = [ysc, ysc]
        idxcol = xp.sb("idxcol", [128, 1], I32)
        for c in range(NDC):
            em.op("gpsimd", lambda e, c=c: e.memset(xg_fm[c].h[:, :], 0.0),
                  writes=(xg_fm[c].key,))
        em.op("gpsimd", lambda e: e.memset(ysc.h[:, :], 0.0), writes=(ysc.key,))
        for tc in range(NTC):
            em.dma("qs", lambda e, tc=tc: e.dma_start(
                yf_dram.h[tc * 128:(tc + 1) * 128, :], ysc.h[:, :]),
                   reads=(ysc.key,), writes=(yf_dram.key,), stream="ypre")

        def load_w1q(e_i, q4):
            buf = W1Q[q4 % 2]
            for j in range(NDC):
                em.dma("qa" if j % 2 == 0 else "qs",
                       lambda e, e_i=e_i, q4=q4, j=j, buf=buf: e.dma_start(
                           buf[j].h[:, :],
                           w1b_dram.h[e_i, j * 128:(j + 1) * 128,
                                      q4 * 1024:(q4 + 1) * 1024]),
                       reads=(w1b_dram.key,), writes=(buf[j].key,),
                       stream=f"w1_{j % 4}")

        def load_w2pair(e_i, fp):
            wb = w2pr[fp % 2]
            em.dma("qa" if fp % 2 == 0 else "qs",
                   lambda e, e_i=e_i, fp=fp, wb=wb: e.dma_start(
                       wb.h[:, :].rearrange("p (a d) -> p a d", a=2),
                       w2b_dram.h[e_i, 2 * fp * 128:(2 * fp + 2) * 128, :].rearrange(
                           "(a p) d -> p a d", p=128)),
                   reads=(w2b_dram.key,), writes=(wb.key,), stream=f"w2_{fp % 2}")
            return wb

        load_w1q(0, 0)
        def gather_tile(e_i, k):
            par = (e_i * CAPT + k) % 3
            gt, ic = xg_tm[par], idxc2[par]
            em.dma("qs", lambda e, e_i=e_i, k=k, ic=ic: e.dma_start(
                ic.h[:, :],
                idx_dram.h[e_i * CAP + k * 128:e_i * CAP + (k + 1) * 128, :]),
                   reads=(idx_dram.key,), writes=(ic.key,), stream=f"idx{par}")
            em.dma("qg", lambda e, gt=gt, ic=ic: e.indirect_dma_start(
                gt.h[:, :], None, xn_dram.h[:, :],
                bass.IndirectOffsetOnAxis(ic.h[:, :], 0)),
                   reads=(xn_dram.key, ic.key), writes=(gt.key,),
                   stream=f"xg{par}")

        def place_tile(e_i, k):
            par = (e_i * CAPT + k) % 3
            gt = xg_tm[par]
            for dc in range(NDC):
                em.op("tensor", lambda e, dc=dc, gt=gt: e.transpose(
                    PTB.h[0:128, dc * 128:(dc + 1) * 128],
                    gt.h[:, dc * 128:(dc + 1) * 128], ident_b.h[:, :]),
                      reads=(gt.key, ident_b.key), writes=(PTB.key,))
            for dc in range(NDC):
                cp("vector",
                   xg_fm[dc].h, SL[:, k * 128:(k + 1) * 128],
                   PTB.h, SL[0:128, dc * 128:(dc + 1) * 128],
                   (PTB.key,), (xg_fm[dc].key,))

        gather_tile(0, 0)
        if E * CAPT > 1:
            gather_tile(1 // CAPT, 1 % CAPT)
        for e_i in range(E):
            for k in range(CAPT):
                g2 = e_i * CAPT + k + 2
                if g2 < E * CAPT:
                    gather_tile(g2 // CAPT, g2 % CAPT)
                place_tile(e_i, k)
            em.dma("qg", lambda e, e_i=e_i: e.dma_start(b2row.h[0:1, :],
                                                        f_b2.h[e_i:e_i + 1, :]),
                   reads=(f_b2.key,), writes=(b2row.key,), stream="brow")
            em.dma("qs", lambda e, e_i=e_i: e.dma_start(
                b1fmb.h[:, :],
                b1b_dram.h[e_i].rearrange("(a b) -> a b", a=32), transpose=True),
                   reads=(b1b_dram.key,), writes=(b1fmb.key,), stream="b1t")
            cp("vector", b1fm.h, SL[:, :], b1fmb.h, SL[:, :], (b1fmb.key,), (b1fm.key,))
            for q4 in range(4):
                if q4 < 3:
                    load_w1q(e_i, q4 + 1)
                elif e_i < E - 1:
                    load_w1q(e_i + 1, 0)
                buf = W1Q[q4 % 2]
                for fj in range(NDC):
                    fi = q4 * 8 + fj
                    pb = P[fj % 7]
                    for j in range(NDC):
                        mm(pb, SL[0:128, 0:CAP], buf[j].h, SL[:, fj * 128:(fj + 1) * 128],
                           xg_fm[j].h, SL[:, 0:CAP], (buf[j].key, xg_fm[j].key),
                           j == 0, j == NDC - 1, fuse=True)
                    em.op("scalar", lambda e, fi=fi, pb=pb: e.activation(
                        hsb[fi].h[:, 0:CAP], pb.h[0:128, 0:CAP], AF.Gelu,
                        bias=b1fm.h[:, fi:fi + 1]),
                          reads=(pb.key, b1fm.key), writes=(hsb[fi].key,))
            if pump is not None:
                pump(2)
            load_w2pair(e_i, 0)
            for fi in range(NFI):
                fp = fi // 2
                if fi % 2 == 0:
                    if fp + 1 < NFI // 2:
                        load_w2pair(e_i, fp + 1)
                wb = w2pr[fp % 2]
                co = (fi % 2) * 1024
                for k in range(CAPT):
                    pb, pb2 = P[1 + 2 * k], P[2 + 2 * k]
                    mm(pb, SL[0:128, 0:512], hsb[fi].h, SL[:, k * 128:(k + 1) * 128],
                       wb.h, SL[:, co:co + 512], (hsb[fi].key, wb.key),
                       fi == 0, False)
                    mm(pb2, SL[0:128, 0:512], hsb[fi].h, SL[:, k * 128:(k + 1) * 128],
                       wb.h, SL[:, co + 512:co + 1024], (hsb[fi].key, wb.key),
                       fi == 0, False)
            for k in range(CAPT):
                ybuf = ysc2[k % 2]
                pb, pb2 = P[1 + 2 * k], P[2 + 2 * k]
                mm(pb, SL[0:128, 0:512], ones_row_b.h, SL[0:1, 0:128],
                   b2row.h, SL[0:1, 0:512], (ones_row_b.key, b2row.key), False, True)
                mm(pb2, SL[0:128, 0:512], ones_row_b.h, SL[0:1, 0:128],
                   b2row.h, SL[0:1, 512:1024], (ones_row_b.key, b2row.key), False, True)
                cp("vector", ybuf.h, SL[:, 0:512], pb.h, SL[0:128, 0:512],
                   (pb.key,), (ybuf.key,))
                cp("vector", ybuf.h, SL[:, 512:1024], pb2.h, SL[0:128, 0:512],
                   (pb2.key,), (ybuf.key,))
                em.dma("qs", lambda e, e_i=e_i, k=k: e.dma_start(
                    idxcol.h[:, :],
                    idx_dram.h[e_i * CAP + k * 128:e_i * CAP + (k + 1) * 128, :]),
                       reads=(idx_dram.key,), writes=(idxcol.key,), stream="idx")
                em.dma("qg", lambda e, k=k, ybuf=ybuf: e.indirect_dma_start(
                    yf_dram.h[:, :], bass.IndirectOffsetOnAxis(idxcol.h[:, :], 0),
                    ybuf.h[:, :], None),
                       reads=(ybuf.key, idxcol.key), writes=(yf_dram.key,),
                       stream="ysct")
        if not final:
            for tc in range(NTC):
                em.dma("qs", lambda e, tc=tc: e.dma_start(
                    ysc.h[:, :], yf_dram.h[tc * 128:(tc + 1) * 128, :]),
                       reads=(yf_dram.key,), writes=(ysc.key,), stream="yrd")
                for dc in range(NDC):
                    pb = P[dc % 7]
                    em.op("tensor", lambda e, dc=dc, pb=pb: e.transpose(
                        pb.h[0:128, 0:128], ysc.h[:, dc * 128:(dc + 1) * 128],
                        ident_f.h[:, :]),
                          reads=(ysc.key, ident_f.key), writes=(pb.key,))
                    em.op("vector", lambda e, dc=dc, tc=tc, pb=pb, X=X: e.tensor_add(
                        X[dc].h[:, tc * 128:(tc + 1) * 128],
                        X[dc].h[:, tc * 128:(tc + 1) * 128], pb.h[0:128, 0:128]),
                          reads=(X[dc].key, pb.key), writes=(X[dc].key,))
        xp.close()

    def attn_b_partA(X, R, x_scope):
        for sc in range(NSC):
            t0 = sc * 512
            layernorm(X, sc, 1e-5, dst_list=None)
            for c in range(NDC):
                mm(P[5], SL[0:128, 0:512], ones_row_f.h, SL[0:1, 0:128],
                   arow.h, SL[0:1, :], (ones_row_f.key, arow.key), True, True)
                mm(P[6], SL[0:128, 0:512], ones_row_f.h, SL[0:1, 0:128],
                   rows2.h, SL[0:1, 0:512], (ones_row_f.key, rows2.key), True, True)
                em.op("vector", lambda e, c=c, t0=t0, X=X: e.tensor_mul(
                    xtmp.h[:, :], X[c].h[:, t0:t0 + 512], P[5].h[0:128, 0:512]),
                      reads=(X[c].key, P[5].key), writes=(xtmp.key,))
                em.op("vector", lambda e, c=c, t0=t0, R=R: e.tensor_add(
                    R[c].h[:, t0:t0 + 512], xtmp.h[:, :], P[6].h[0:128, 0:512]),
                      reads=(xtmp.key, P[6].key), writes=(R[c].key,))
        for dc in range(NDC):
            em.dma("qs", lambda e, dc=dc, X=X: e.dma_start(
                xsp_dram.h[dc, :, :], X[dc].h[:, :]),
                   reads=(X[dc].key,), writes=(xsp_dram.key,), stream="spill")
        x_scope.close()

    def attn_b_partB(R):
        b = Scope("right")
        abW = [b.sb(f"abW{j}", [128, D], BF16) for j in range(NDC)]
        abW2 = [b.sb(f"abW2{j}", [128, D], BF16) for j in range(NDC)]
        phiqh = [b.sb(f"phiqh{h}", [64, 512], BF16) for h in range(H)]
        attn_f = [b.sb(f"attn_f{c}", [128, 512], BF16) for c in range(NDC)]
        kv_ext = b.sb("kv_ext", [64, H * 65], BF16)
        phik = b.sb("phik", [128, D], BF16)
        vtile = b.sb("vtile", [128, D], BF16)
        recr2 = [b.sb(f"recr{i}", [128, 1], F32) for i in range(4)]
        atm = [b.sb(f"atm{i}", [128, 64], BF16) for i in range(4)]
        bqrow = b.sb("bqrow", [1, D], BF16)
        bkrow = b.sb("bkrow", [1, D], BF16)
        bvrow = b.sb("bvrow", [1, D], BF16)
        borow = b.sb("borow", [1, D], BF16)

        def ln1(sc):
            layernorm(R, sc, 1e-6, dst_list=XN, scaled=True, stats_banks=(P[0], P[1]))

        def load_lnfm():
            for c in range(NDC):
                mm(P[2], SL[0:128, c:c + 1], lnps.h, SL[0:1, c * 128:(c + 1) * 128],
                   ones_row_f.h, SL[0:1, 0:1], (lnps.key, ones_row_f.key),
                   c == 0, c == NDC - 1)
                mm(P[3], SL[0:128, c:c + 1], lnpb.h, SL[0:1, c * 128:(c + 1) * 128],
                   ones_row_f.h, SL[0:1, 0:1], (lnpb.key, ones_row_f.key),
                   c == 0, c == NDC - 1)
            cp("vector", lnsc.h, SL[:, :], P[2].h, SL[0:128, 0:NDC], (P[2].key,),
               (lnsc.key,))
            cp("vector", lnsb.h, SL[:, :], P[3].h, SL[0:128, 0:NDC], (P[3].key,),
               (lnsb.key,))

        load_row_f32(lnps, D, ab_ln1s)
        load_row_f32(lnpb, D, ab_ln1b)
        load_lnfm()
        load_row_bf16(bqrow, D, ab_bq)
        load_row_bf16(bkrow, D, ab_bk)
        load_row_bf16(bvrow, D, ab_bv)
        load_row_bf16(borow, D, ab_bo)
        for j in range(NDC):
            em.dma("qa", lambda e, j=j: e.dma_start(abW[j].h[:, :],
                                                    abw_dram.h[1, j * 128:(j + 1) * 128, :]),
                   reads=(abw_dram.key,), writes=(abW[j].key,), stream=f"abw{j % 2}")
            em.dma("qa", lambda e, j=j: e.dma_start(abW2[j].h[:, :],
                                                    abw_dram.h[2, j * 128:(j + 1) * 128, :]),
                   reads=(abw_dram.key,), writes=(abW2[j].key,), stream=f"abv{j % 2}")
        for bank in (P[2], P[3], P[4]):
            mm(bank, SL[0:128, 0:512], ones_row_b.h, SL[0:1, 0:128],
               zrow_b.h, SL[0:1, 0:512], (ones_row_b.key, zrow_b.key), True, True)
        for sc in range(NSC):
            ln1(sc)
            for tt in range(4):
                tc = sc * 4 + tt
                for wi, (wsb, bias) in enumerate(((abW, bkrow), (abW2, bvrow))):
                    for nh in range(2):
                        pb = P[[0, 1, 5, 6][(wi * 2 + nh) % 4]]
                        for j in range(NDC):
                            mm(pb, SL[0:128, 0:512], XN[j].h,
                               SL[:, tt * 128:(tt + 1) * 128],
                               wsb[j].h, SL[:, nh * 512:(nh + 1) * 512],
                               (XN[j].key, wsb[j].key), j == 0, False, fuse=True)
                        mm(pb, SL[0:128, 0:512], ones_row_b.h, SL[0:1, 0:128],
                           bias.h, SL[0:1, nh * 512:(nh + 1) * 512],
                           (ones_row_b.key, bias.key), False, True, fuse=True)
                        if wi == 0:
                            em.op("vector", lambda e, nh=nh, pb=pb: e.tensor_scalar(
                                phik.h[:, nh * 512:(nh + 1) * 512], pb.h[0:128, 0:512],
                                0.0, 1e-3, ALU.max, ALU.add),
                                  reads=(pb.key,), writes=(phik.key,))
                        else:
                            cp("vector", vtile.h, SL[:, nh * 512:(nh + 1) * 512], pb.h,
                               SL[0:128, 0:512], (pb.key,), (vtile.key,))
                last = (tc == NTC - 1)
                for h in range(H):
                    bank = P[2] if h < 8 else P[3]
                    off = (h % 8) * 64
                    mm(bank, SL[0:64, off:off + 64], phik.h, SL[:, h * 64:(h + 1) * 64],
                       vtile.h, SL[:, h * 64:(h + 1) * 64], (phik.key, vtile.key),
                       False, last)
                for dc in range(NDC):
                    mm(P[4], SL[0:128, dc:dc + 1], phik.h, SL[:, dc * 128:(dc + 1) * 128],
                       ones_col_b.h, SL[0:128, 0:1], (phik.key, ones_col_b.key),
                       False, last)
        for h in range(H):
            bank = P[2] if h < 8 else P[3]
            off = (h % 8) * 64
            cp("vector", kv_ext.h, SL[:, h * 65:h * 65 + 64], bank.h,
               SL[0:64, off:off + 64], (bank.key,), (kv_ext.key,))
        for dc in range(NDC):
            cp("vector", kv_ext.h, SL[:, (2 * dc) * 65 + 64:(2 * dc) * 65 + 65], P[4].h,
               SL[0:64, dc:dc + 1], (P[4].key,), (kv_ext.key,))
            cp("vector", kv_ext.h, SL[:, (2 * dc + 1) * 65 + 64:(2 * dc + 1) * 65 + 65],
               P[4].h, SL[64:128, dc:dc + 1], (P[4].key,), (kv_ext.key,))
        for j in range(NDC):
            em.dma("qa", lambda e, j=j: e.dma_start(abW[j].h[:, :],
                                                    abw_dram.h[0, j * 128:(j + 1) * 128, :]),
                   reads=(abw_dram.key,), writes=(abW[j].key,), stream=f"abw{j % 2}")
            em.dma("qa", lambda e, j=j: e.dma_start(abW2[j].h[:, :],
                                                    abw_dram.h[3, j * 128:(j + 1) * 128, :]),
                   reads=(abw_dram.key,), writes=(abW2[j].key,), stream=f"abv{j % 2}")
        for sc in range(NSC):
            t0 = sc * 512
            ln1(sc)
            for i in range(NDC):
                pb = P[[0, 1, 5, 6][i % 4]]
                for j in range(NDC):
                    mm(pb, SL[0:128, 0:512], abW[j].h, SL[:, i * 128:(i + 1) * 128],
                       XN[j].h, SL[:, 0:512], (abW[j].key, XN[j].key), j == 0, False,
                       fuse=True)
                mm(pb, SL[0:128, 0:512], bqrow.h, SL[0:1, i * 128:(i + 1) * 128],
                   ones_row_b.h, SL[0:1, 0:512], (bqrow.key, ones_row_b.key), False, True,
                   fuse=True)
                for sub in range(2):
                    hh = 2 * i + sub
                    em.op("vector", lambda e, hh=hh, sub=sub, pb=pb: e.tensor_scalar(
                        phiqh[hh].h[:, :], pb.h[sub * 64:(sub + 1) * 64, 0:512],
                        0.0, 1e-3, ALU.max, ALU.add),
                          reads=(pb.key,), writes=(phiqh[hh].key,))
            for tt in range(4):
                for hg in range(2):
                    for hh in range(8):
                        h = hg * 8 + hh
                        nb = P[[2, 3, 5, 6][h % 4]]
                        mm(nb, SL[0:128, 0:65], phiqh[h].h,
                           SL[0:64, tt * 128:(tt + 1) * 128],
                           kv_ext.h, SL[:, h * 65:(h + 1) * 65],
                           (phiqh[h].key, kv_ext.key), True, True)
                        em.op("vector", lambda e, nb=nb, h=h: e.reciprocal(
                            recr2[h % 4].h[:, :], nb.h[0:128, 64:65]),
                              reads=(nb.key,), writes=(recr2[h % 4].key,))
                        em.op("vector", lambda e, nb=nb, h=h: e.tensor_scalar(
                            atm[h % 4].h[:, :], nb.h[0:128, 0:64],
                            recr2[h % 4].h[:, 0:1], None, ALU.mult),
                              reads=(nb.key, recr2[h % 4].key), writes=(atm[h % 4].key,))
                        em.op("tensor", lambda e, h=h, hh=hh: e.transpose(
                            PTB.h[0:64, hh * 128:(hh + 1) * 128], atm[h % 4].h[:, :],
                            ident_b.h[:, :]),
                              reads=(atm[h % 4].key, ident_b.key), writes=(PTB.key,))
                    for hh in range(8):
                        h = hg * 8 + hh
                        hc, ho = h // 2, (h % 2) * 64
                        cp("vector", attn_f[hc].h,
                           SL[ho:ho + 64, tt * 128:(tt + 1) * 128],
                           PTB.h, SL[0:64, hh * 128:(hh + 1) * 128],
                           (PTB.key,), (attn_f[hc].key,))
            for dc in range(NDC):
                pb = P[[0, 1, 5, 6][dc % 4]]
                for j in range(NDC):
                    mm(pb, SL[0:128, 0:512], abW2[j].h, SL[:, dc * 128:(dc + 1) * 128],
                       attn_f[j].h, SL[:, :], (abW2[j].key, attn_f[j].key), j == 0, False,
                       fuse=True)
                mm(pb, SL[0:128, 0:512], borow.h, SL[0:1, dc * 128:(dc + 1) * 128],
                   ones_row_b.h, SL[0:1, 0:512], (borow.key, ones_row_b.key), False, True,
                   fuse=True)
                em.op("vector", lambda e, dc=dc, t0=t0, pb=pb, R=R: e.tensor_add(
                    R[dc].h[:, t0:t0 + 512], R[dc].h[:, t0:t0 + 512], pb.h[0:128, 0:512]),
                      reads=(R[dc].key, pb.key), writes=(R[dc].key,))
        b.close()

        m = Scope("right")
        W1q2 = [[m.sb(f"W1q{p}_{j}", [128, D], BF16) for j in range(NDC)]
                for p in range(2)]
        w2bl8 = [m.sb(f"w2bl8_{i}", [128, D], BF16) for i in range(NDC)]
        hsb8 = [m.sb(f"hsb8_{i}", [128, 512], BF16) for i in range(NDC)]
        xn2 = [m.sb(f"xn2_{c}", [128, 512], BF16) for c in range(NDC)]
        b1fmb = m.sb("b1fmbm", [128, 32], BF16)
        b1fm = m.sb("b1fmm", [128, 32], F32)
        b2row_f = m.sb("b2rowm", [1, D], F32)
        em.dma("qs", lambda e: e.dma_start(
            b1fmb.h[:, :], abb1b_dram.h[0].rearrange("(a b) -> a b", a=32),
            transpose=True),
               reads=(abb1b_dram.key,), writes=(b1fmb.key,), stream="b1t")
        cp("vector", b1fm.h, SL[:, :], b1fmb.h, SL[:, :], (b1fmb.key,), (b1fm.key,))
        load_row_f32(b2row_f, D, ab_b2)
        load_row_f32(lnps, D, ab_ln2s)
        load_row_f32(lnpb, D, ab_ln2b)
        load_lnfm()

        def load_w1q_m(fq):
            buf = W1q2[fq % 2]
            for j in range(NDC):
                em.dma("qa" if j % 2 == 0 else "qs",
                       lambda e, fq=fq, j=j, buf=buf: e.dma_start(
                           buf[j].h[:, :],
                           abw1b.h[j * 128:(j + 1) * 128, fq * 1024:(fq + 1) * 1024]),
                       reads=(abw1b.key,), writes=(buf[j].key,),
                       stream=f"abw1_{j % 4}")

        def load_w2_m(fq):
            for fj in range(NDC):
                fi = fq * 8 + fj
                em.dma("qa" if fj % 2 == 0 else "qs",
                       lambda e, fi=fi, fj=fj: e.dma_start(
                           w2bl8[fj].h[:, :], abw2b.h[fi * 128:(fi + 1) * 128, :]),
                       reads=(abw2b.key,), writes=(w2bl8[fj].key,),
                       stream=f"abw2_{fj % 4}")

        load_w1q_m(0)
        for sc in range(NSC):
            t0 = sc * 512
            layernorm(R, sc, 1e-6, dst_list=xn2, scaled=True)
            for fq in range(4):
                if fq < 3:
                    load_w1q_m(fq + 1)
                elif sc < NSC - 1:
                    load_w1q_m(0)
                W1q = W1q2[fq % 2]
                load_w2_m(fq)
                for fj in range(NDC):
                    fi = fq * 8 + fj
                    pb = P[fj % 2]
                    for j in range(NDC):
                        mm(pb, SL[0:128, 0:512], W1q[j].h, SL[:, fj * 128:(fj + 1) * 128],
                           xn2[j].h, SL[:, 0:512], (W1q[j].key, xn2[j].key),
                           j == 0, j == NDC - 1, fuse=True)
                    em.op("scalar", lambda e, fj=fj, fi=fi, pb=pb: e.activation(
                        hsb8[fj].h[:, :], pb.h[0:128, 0:512], AF.Gelu_apprx_tanh,
                        bias=b1fm.h[:, fi:fi + 1]),
                          reads=(pb.key, b1fm.key), writes=(hsb8[fj].key,))
                for fj in range(NDC):
                    fi = fq * 8 + fj
                    wb = w2bl8[fj]
                    for dc in range(7):
                        mm(P[dc], SL[0:128, 0:512], wb.h, SL[:, dc * 128:(dc + 1) * 128],
                           hsb8[fj].h, SL[:, 0:512], (wb.key, hsb8[fj].key),
                           fj == 0, fj == NDC - 1)
                for dc in range(7):
                    em.op("vector", lambda e, dc=dc, t0=t0, R=R: e.tensor_add(
                        R[dc].h[:, t0:t0 + 512], R[dc].h[:, t0:t0 + 512],
                        P[dc].h[0:128, 0:512]),
                          reads=(R[dc].key, P[dc].key), writes=(R[dc].key,))
                for fj in range(NDC):
                    fi = fq * 8 + fj
                    wb = w2bl8[fj]
                    mm(P[0], SL[0:128, 0:512], wb.h, SL[:, 7 * 128:8 * 128],
                       hsb8[fj].h, SL[:, 0:512], (wb.key, hsb8[fj].key),
                       fj == 0, fj == NDC - 1)
                em.op("vector", lambda e, t0=t0, R=R: e.tensor_add(
                    R[7].h[:, t0:t0 + 512], R[7].h[:, t0:t0 + 512],
                    P[0].h[0:128, 0:512]),
                      reads=(R[7].key, P[0].key), writes=(R[7].key,))
            xld = sqt[0]
            for dc in range(NDC):
                em.dma("qs", lambda e, dc=dc, t0=t0, xld=xld: e.dma_start(
                    xld.h[:, :], xsp_dram.h[dc, :, t0:t0 + 512]),
                       reads=(xsp_dram.key,), writes=(xld.key,), stream="spill")
                mm(P[5], SL[0:128, 0:512], b2row_f.h, SL[0:1, dc * 128:(dc + 1) * 128],
                   ones_row_f.h, SL[0:1, 0:512], (b2row_f.key, ones_row_f.key), True, True)
                em.op("vector", lambda e, xld=xld: e.tensor_add(
                    xtmp.h[:, :], xld.h[:, :], P[5].h[0:128, 0:512]),
                      reads=(xld.key, P[5].key), writes=(xtmp.key,))
                em.op("vector", lambda e, dc=dc, t0=t0, R=R: e.tensor_add(
                    R[dc].h[:, t0:t0 + 512], R[dc].h[:, t0:t0 + 512], xtmp.h[:, :]),
                      reads=(R[dc].key, xtmp.key), writes=(R[dc].key,))
        m.close()

    for seq in range(NSEQ):
        xsc = Scope("right")
        X = [xsc.sb(f"X{c}", [128, S], F32) for c in range(NDC)]
        iosc = Scope("right")
        io_in = [iosc.sb(f"ioin{i}", [128, D], F32) for i in range(2)]
        for tc in range(NTC):
            par = tc % 2
            em.dma("qs", lambda e, seq=seq, tc=tc, par=par: e.dma_start(
                io_in[par].h[:, :], x_in.h[seq, tc * 128:(tc + 1) * 128, :]),
                   reads=(x_in.key,), writes=(io_in[par].key,), stream=f"ioi{par}")
            for dc in range(NDC):
                pb = P[dc % 7]
                em.op("tensor", lambda e, dc=dc, pb=pb, par=par: e.transpose(
                    pb.h[0:128, 0:128], io_in[par].h[:, dc * 128:(dc + 1) * 128],
                    ident_f.h[:, :]),
                      reads=(io_in[par].key, ident_f.key), writes=(pb.key,))
                cp("vector", X[dc].h, SL[:, tc * 128:(tc + 1) * 128], pb.h,
                   SL[0:128, 0:128], (pb.key,), (X[dc].key,))
        iosc.close()
        if seq == 0:
            issue_w1_conversions()
            cgen = gen_w2()

            def pump(n=1):
                for _ in range(n):
                    if next(cgen, "done") == "done":
                        break
        else:
            pump = None
        if NSUB >= 1:
            attn_a(X, pump=pump)
        if seq == 0:
            for _ in range(100):
                if next(cgen, "done") == "done":
                    break
            fgen = gen_ab()

            def fpump(n=2):
                for _ in range(n):
                    if next(fgen, "done") == "done":
                        break
        else:
            fpump = None
        if NSUB >= 2:
            ffn(X, pump=fpump)
        if seq == 0 and NSUB >= 2:
            for _ in range(100):
                if next(fgen, "done") == "done":
                    break
        elif seq == 0:
            fgen2 = gen_ab()
            for _ in range(100):
                if next(fgen2, "done") == "done":
                    break
        if NSUB >= 3:
            rsc = Scope("left")
            R = [rsc.sb(f"R{c}", [128, S], F32) for c in range(NDC)]
            attn_b_partA(X, R, xsc)
            attn_b_partB(R)
            stream = R
        else:
            stream = X
        if NSUB >= 4:
            ffn(stream, final=True)
        osc = Scope("right")
        iot2 = [osc.sb(f"iot2_{i}", [128, D], F32) for i in range(2)]
        yr2 = [osc.sb(f"yr2_{i}", [128, D], F32) for i in range(2)]
        for tc in range(NTC):
            par = tc % 2
            if NSUB >= 4:
                em.dma("qa", lambda e, tc=tc, par=par: e.dma_start(
                    yr2[par].h[:, :], yf_dram.h[tc * 128:(tc + 1) * 128, :]),
                       reads=(yf_dram.key,), writes=(yr2[par].key,), stream=f"yrd{par}")
            for dc in range(NDC):
                pb = P[dc % 7]
                em.op("tensor", lambda e, dc=dc, tc=tc, pb=pb, stream=stream: e.transpose(
                    pb.h[0:128, 0:128], stream[dc].h[:, tc * 128:(tc + 1) * 128],
                    ident_f.h[:, :]),
                      reads=(stream[dc].key, ident_f.key), writes=(pb.key,))
                if NSUB >= 4:
                    em.op("vector", lambda e, dc=dc, pb=pb, par=par: e.tensor_add(
                        iot2[par].h[:, dc * 128:(dc + 1) * 128],
                        yr2[par].h[:, dc * 128:(dc + 1) * 128], pb.h[0:128, 0:128]),
                          reads=(yr2[par].key, pb.key), writes=(iot2[par].key,))
                else:
                    cp("vector", iot2[par].h, SL[:, dc * 128:(dc + 1) * 128], pb.h,
                       SL[0:128, 0:128], (pb.key,), (iot2[par].key,))
            em.dma("qs", lambda e, seq=seq, tc=tc, par=par: e.dma_start(
                y_out.h[seq, tc * 128:(tc + 1) * 128, :], iot2[par].h[:, :]),
                   reads=(iot2[par].key,), writes=(y_out.key,), stream=f"ioo{par}")
        osc.close()
        if NSUB >= 3:
            rsc.close()
        else:
            xsc.close()
    em.barrier()

    with nc.Block() as block:
        @block.tensor
        def _(e):
            for fn in em.thunks["tensor"]:
                fn(e)

        @block.vector
        def _(e):
            for fn in em.thunks["vector"]:
                fn(e)

        @block.scalar
        def _(e):
            for fn in em.thunks["scalar"]:
                fn(e)

        @block.gpsimd
        def _(e):
            for fn in em.thunks["gpsimd"]:
                fn(e)

        @block.sync
        def _(e):
            for fn in em.thunks["sync"]:
                fn(e)

    root.close()
    return nc


def kernel(**inputs):
    cfg = make_cfg()
    nc = build_program(cfg)
    x = np.ascontiguousarray(np.asarray(inputs["L0_attn_a_x"], np.float32))
    common = {
        "aa_wq": inputs["L0_attn_a_wq"], "aa_wk": inputs["L0_attn_a_wk"],
        "aa_wv": inputs["L0_attn_a_wv"], "aa_wo": inputs["L0_attn_a_wo"],
        "aa_gam": inputs["L0_attn_a_gammas"],
        "ab_ln1s": inputs["L0_attn_b_ln1_scale"], "ab_ln1b": inputs["L0_attn_b_ln1_bias"],
        "ab_wq": np.asarray(inputs["L0_attn_b_Wq"]).reshape(1024, 1024),
        "ab_wk": np.asarray(inputs["L0_attn_b_Wk"]).reshape(1024, 1024),
        "ab_wv": np.asarray(inputs["L0_attn_b_Wv"]).reshape(1024, 1024),
        "ab_bq": np.asarray(inputs["L0_attn_b_bq"]).reshape(1024),
        "ab_bk": np.asarray(inputs["L0_attn_b_bk"]).reshape(1024),
        "ab_bv": np.asarray(inputs["L0_attn_b_bv"]).reshape(1024),
        "ab_wo": np.asarray(inputs["L0_attn_b_Wo"]).reshape(1024, 1024),
        "ab_bo": inputs["L0_attn_b_bo"],
        "ab_ln2s": inputs["L0_attn_b_ln2_scale"], "ab_ln2b": inputs["L0_attn_b_ln2_bias"],
        "ab_w1": inputs["L0_attn_b_W1"], "ab_b1": inputs["L0_attn_b_b1"],
        "ab_w2": inputs["L0_attn_b_W2"], "ab_b2": inputs["L0_attn_b_b2"],
        "f_wr": inputs["L0_ffn_wr"], "f_br": inputs["L0_ffn_br"],
        "f_w1": inputs["L0_ffn_w1"], "f_b1": inputs["L0_ffn_b1"],
        "f_w2": inputs["L0_ffn_w2"], "f_b2": inputs["L0_ffn_b2"],
    }
    common = {k: np.ascontiguousarray(np.asarray(v, np.float32)) for k, v in common.items()}
    in_maps = []
    for c in range(8):
        m = dict(common)
        m["x"] = np.ascontiguousarray(x[2 * c:2 * c + 2])
        in_maps.append(m)
    res = run_bass_kernel_spmd(nc, in_maps, core_ids=list(range(8)))
    return np.concatenate([np.asarray(r["y"]) for r in res.results], axis=0)
